# version 01
# baseline (speedup 1.0000x reference)
import numpy as np
from contextlib import ExitStack

import concourse.bass as bass
import concourse.mybir as mybir
from concourse.bass_utils import run_bass_kernel_spmd

F32 = mybir.dt.float32
BF16 = mybir.dt.bfloat16
AF = mybir.ActivationFunctionType
ALU = mybir.AluOpType
AX = mybir.AxisListType

ENGINES = ("sync", "scalar", "vector", "tensor", "gpsimd")

D = 1024
NCH = 8
B = 2
S = 16384
QT = 4096
NEG = -1e30


SERIALIZE = [False]
SER_EXCLUDE_TAGS = set()


class Plan:
    def __init__(self, nc):
        self.nc = nc
        self.steps = []
        self.count = {e: 0 for e in ENGINES}
        self._prev = None
        self.default_tag = None

    def emit(self, engine, fn, after=(), dma=False, n_dma=1, strict_after=(),
             tag=None):
        waits = {}
        ser = SERIALIZE[0]
        if isinstance(ser, tuple):
            ser = ser[0] <= len(self.steps) < ser[1]
        tg = tag if tag is not None else self.default_tag
        if ser and tg is not None and tg in SER_EXCLUDE_TAGS:
            ser = False
        extra = [self._prev] if (ser and self._prev) else []
        for t in list(after) + list(strict_after) + extra:
            if t is None:
                continue
            e, c = t
            if waits.get(e, 0) < c:
                waits[e] = c
        inc = 16 * n_dma if dma else 1
        self.count[engine] += inc
        tok = (engine, self.count[engine])
        self.steps.append((engine, waits, fn, dma))
        self._prev = tok
        return tok

    def barrier(self):
        return [(e, self.count[e]) for e in ENGINES if self.count[e] > 0]

    def build(self, block, sems):
        per_engine = {e: [] for e in ENGINES}
        for (e, waits, fn, dma) in self.steps:
            per_engine[e].append((waits, fn, dma))

        def make(e):
            def body(eng):
                waited = {}
                for waits, fn, dma in per_engine[e]:
                    for we, wc in waits.items():
                        if waited.get(we, 0) < wc:
                            eng.wait_ge(sems[we], wc)
                            waited[we] = wc
                    insts = fn(eng)
                    if insts is None:
                        raise RuntimeError("step emitted nothing")
                    if not isinstance(insts, (list, tuple)):
                        insts = [insts]
                    if dma:
                        for i in insts:
                            i.then_inc(sems[e], 16)
                    else:
                        insts[-1].then_inc(sems[e], 1)
            body.__name__ = f"_{e}_body"
            return body

        block.sync(make("sync"))
        block.scalar(make("scalar"))
        block.vector(make("vector"))
        block.tensor(make("tensor"))
        block.gpsimd(make("gpsimd"))


class Ctx:
    def __init__(self, name):
        self.nc = bass.Bass("TRN2", target_bir_lowering=False)
        self.pl = Plan(self.nc)
        self.es = ExitStack()
        self.inputs = {}
        self.outputs = {}
        self._uid = 0
        self.bank_last = {}

    def bank_wait(self, ps):
        t = self.bank_last.get(id(ps))
        return [t] if t is not None else []

    def bank_set(self, ps, tok):
        self.bank_last[id(ps)] = tok

    def uid(self, base):
        self._uid += 1
        return f"{base}_{self._uid}"

    def inp(self, name, shape, dtype=F32):
        t = self.nc.dram_tensor(name, list(shape), dtype, kind="ExternalInput")
        self.inputs[name] = t
        return t.ap()

    def outp(self, name, shape, dtype=F32):
        t = self.nc.dram_tensor(name, list(shape), dtype, kind="ExternalOutput")
        self.outputs[name] = t
        return t.ap()

    def sb(self, base, shape, dtype=F32, stack=None, side="left"):
        return (stack or self.es).enter_context(
            self.nc.sbuf_tensor(self.uid(base), list(shape), dtype, side=side))

    def ps(self, base, shape, dtype=F32, stack=None):
        return (stack or self.es).enter_context(
            self.nc.psum_tensor(self.uid(base), list(shape), dtype))

    def finish(self):
        nc = self.nc
        with ExitStack() as st:
            sems = {e: st.enter_context(nc.semaphore(f"sem_{e}")) for e in ENGINES}
            with nc.Block() as block:
                self.pl.build(block, sems)
        self.es.close()
        return nc


def last(toks):
    best = {}
    for t in toks:
        if t is None:
            continue
        e, c = t
        if best.get(e, 0) < c:
            best[e] = c
    return [(e, c) for e, c in best.items()]



def load_weight(cx, dram_ap, rows, cols, name, dep=(), to_bf16=True):
    nk = rows // 128
    stage = cx.sb(name + "_st", [128, nk * cols], F32)
    wsb = cx.sb(name, [128, nk * cols], BF16) if to_bf16 else stage
    toks = []
    for k in range(nk):
        toks.append(cx.pl.emit(
            "sync",
            lambda eng, k=k: eng.dma_start(
                out=stage[:, k * cols:(k + 1) * cols],
                in_=dram_ap[128 * k:128 * (k + 1), :]),
            after=dep, dma=True))
    if not to_bf16:
        return stage, last(toks)[0] if len(last(toks)) == 1 else ("sync", cx.pl.count["sync"])
    tok = cx.pl.emit(
        "vector",
        lambda eng: eng.tensor_copy(out=wsb[:, :], in_=stage[:, :]),
        after=toks)
    return wsb, tok


def emit_linear(cx, wsb, nk, nout, x_bf, T, out_write, after=(), psA=None,
                Tn=512, extra_rhs_tok=None):
    pl = cx.pl
    _saved_tag = pl.default_tag
    pl.default_tag = "lin"
    n_of = (nout + 127) // 128
    evac_toks = []
    bi = 0
    for of in range(n_of):
        o0 = of * 128
        osz = min(128, nout - o0)
        for nt in range(0, T, Tn):
            tsz = min(Tn, T - nt)
            ps = psA[bi % len(psA)]
            waits = list(after) + cx.bank_wait(ps)

            def mmgroup(eng, o0=o0, osz=osz, nt=nt, tsz=tsz, ps=ps):
                ret = []
                for k in range(nk):
                    ret.append(eng.matmul(
                        ps[0:osz, 0:tsz],
                        wsb[:, k * nout + o0: k * nout + o0 + osz],
                        x_bf[k][:, nt:nt + tsz],
                        start=(k == 0), stop=(k == nk - 1)))
                return ret[-1]
            pe_tok = pl.emit("tensor", mmgroup, after=waits)
            ev = out_write(of, o0, osz, nt, tsz, ps, pe_tok)
            evac_toks.append(ev)
            cx.bank_set(ps, ev)
            bi += 1
    pl.default_tag = _saved_tag
    return evac_toks


def simple_linear(cx, wsb, nk, nout, x_bf, T, dst_fm, after=(), psA=None,
                  bias_col=None, act=None, dst2_fm=None, Tn=512,
                  evac_engine="scalar", scale=1.0):
    def write(of, o0, osz, nt, tsz, ps, pe_tok):
        if act is None and bias_col is not None:
            def ev(eng, of=of, osz=osz, nt=nt, tsz=tsz, ps=ps):
                return eng.tensor_scalar_add(
                    out=dst_fm[of][0:osz, nt:nt + tsz], in0=ps[0:osz, 0:tsz],
                    scalar1=bias_col[0:osz, of:of + 1])
            t1 = cx.pl.emit("vector", ev, after=[pe_tok])
            if dst2_fm is not None:
                t2 = cx.pl.emit(
                    "vector",
                    lambda eng, of=of, osz=osz, nt=nt, tsz=tsz: eng.tensor_copy(
                        out=dst2_fm[of][0:osz, nt:nt + tsz],
                        in_=dst_fm[of][0:osz, nt:nt + tsz]),
                    after=[t1])
                return t2
            return t1

        def ev(eng, of=of, osz=osz, nt=nt, tsz=tsz, ps=ps):
            kw = {}
            if bias_col is not None:
                kw["bias"] = bias_col[0:osz, of:of + 1]
            r = eng.activation(out=dst_fm[of][0:osz, nt:nt + tsz],
                               in_=ps[0:osz, 0:tsz],
                               func=(act or AF.Copy), scale=scale, **kw)
            return r
        t1 = cx.pl.emit(evac_engine, ev, after=[pe_tok])
        if dst2_fm is not None:
            t2 = cx.pl.emit(
                "vector",
                lambda eng, of=of, osz=osz, nt=nt, tsz=tsz: eng.tensor_copy(
                    out=dst2_fm[of][0:osz, nt:nt + tsz],
                    in_=dst_fm[of][0:osz, nt:nt + tsz]),
                after=[t1])
            return t2
        return t1
    return emit_linear(cx, wsb, nk, nout, x_bf, T, write, after=after, psA=psA, Tn=Tn)


def emit_ln(cx, x_fm, T, dst_bf=None, dst_f32=None, after=(), psum2=None,
            ones_col=None, ones_row=None, scratch=None, eps=1e-5,
            gain_col=None, bias_col=None, nch=NCH, Tn=512):
    pl = cx.pl
    st = scratch
    toks_out = []
    _saved_tag = pl.default_tag
    pl.default_tag = "ln"
    for nt in range(0, T, Tn):
        tsz = min(Tn, T - nt)
        psStat, psMu, psRs = psum2
        t_sum = pl.emit(
            "tensor",
            lambda eng, nt=nt, tsz=tsz: [
                eng.matmul(psStat[0:1, 0:tsz], ones_col[:, 0:1],
                           x_fm[c][:, nt:nt + tsz],
                           start=(c == 0), stop=(c == nch - 1))
                for c in range(nch)][-1],
            after=list(after) + cx.bank_wait(psStat))
        t_mmq = []
        t_sqc = []
        for c in range(nch):
            dep = list(after)
            if len(t_mmq) >= 2:
                dep.append(t_mmq[-2])
            t_sq = pl.emit(
                "scalar",
                lambda eng, c=c, nt=nt, tsz=tsz: eng.activation(
                    out=st["sq"][c % 2][:, 0:tsz], in_=x_fm[c][:, nt:nt + tsz],
                    func=AF.Square),
                after=dep)
            t_sqc.append(t_sq)
            t_mmq.append(pl.emit(
                "tensor",
                lambda eng, c=c, tsz=tsz: eng.matmul(
                    psStat[32:33, 0:tsz], ones_col[:, 0:1],
                    st["sq"][c % 2][:, 0:tsz],
                    start=(c == 0), stop=(c == nch - 1)),
                after=[t_sq]))
        mu, var, r0, r1 = st["mu"], st["var"], st["r0"], st["r1"]
        t_mu = pl.emit(
            "vector",
            lambda eng, tsz=tsz: eng.tensor_scalar_mul(
                out=mu[0:1, 0:tsz], in0=psStat[0:1, 0:tsz],
                scalar1=1.0 / (128 * nch)),
            after=[t_sum])
        t_mu2 = pl.emit(
            "vector",
            lambda eng, tsz=tsz: eng.tensor_mul(
                out=r0[0:1, 0:tsz], in0=mu[0:1, 0:tsz], in1=mu[0:1, 0:tsz]),
            after=[t_mu])
        t_var = pl.emit(
            "vector",
            lambda eng, tsz=tsz: eng.scalar_tensor_tensor(
                out=var[0:1, 0:tsz], in0=psStat[32:33, 0:tsz],
                scalar=1.0 / (128 * nch), in1=r0[0:1, 0:tsz],
                op0=ALU.mult, op1=ALU.subtract),
            after=[t_mmq[-1], t_mu2])
        cx.bank_set(psStat, t_var)
        t_vae = pl.emit(
            "vector",
            lambda eng, tsz=tsz: eng.tensor_scalar_add(
                out=var[0:1, 0:tsz], in0=var[0:1, 0:tsz], scalar1=eps),
            after=[t_var])
        t_ln = pl.emit(
            "scalar",
            lambda eng, tsz=tsz: eng.activation(
                out=r1[0:1, 0:tsz], in_=var[0:1, 0:tsz], func=AF.Ln),
            after=[t_vae])
        t_rs = pl.emit(
            "scalar",
            lambda eng, tsz=tsz: eng.activation(
                out=r0[0:1, 0:tsz], in_=r1[0:1, 0:tsz], func=AF.Exp, scale=-0.5),
            after=[t_ln])
        t_bmu = pl.emit(
            "tensor",
            lambda eng, tsz=tsz: eng.matmul(
                psMu[:, 0:tsz], ones_row[0:1, :], mu[0:1, 0:tsz],
                start=True, stop=True),
            after=[t_mu] + cx.bank_wait(psMu))
        t_brs = pl.emit(
            "tensor",
            lambda eng, tsz=tsz: eng.matmul(
                psRs[:, 0:tsz], ones_row[0:1, :], r0[0:1, 0:tsz],
                start=True, stop=True),
            after=[t_rs] + cx.bank_wait(psRs))
        for c in range(nch):
            t_a = pl.emit(
                "vector",
                lambda eng, c=c, nt=nt, tsz=tsz: eng.tensor_sub(
                    out=st["tmp"][:, 0:tsz], in0=x_fm[c][:, nt:nt + tsz],
                    in1=psMu[:, 0:tsz]),
                after=[t_bmu] + list(after))
            dsts = ([dst_f32] if dst_f32 is not None else []) + \
                   ([dst_bf] if dst_bf is not None else [])
            prev = t_a
            if gain_col is None:
                for dd in dsts:
                    prev = pl.emit(
                        "vector",
                        lambda eng, c=c, nt=nt, tsz=tsz, dd=dd: eng.tensor_mul(
                            out=dd[c][:, nt:nt + tsz], in0=st["tmp"][:, 0:tsz],
                            in1=psRs[:, 0:tsz]),
                        after=[t_brs, t_a])
            else:
                prev = pl.emit(
                    "vector",
                    lambda eng, tsz=tsz: eng.tensor_mul(
                        out=st["tmp"][:, 0:tsz], in0=st["tmp"][:, 0:tsz],
                        in1=psRs[:, 0:tsz]),
                    after=[t_brs, t_a])
                for dd in dsts:
                    prev = pl.emit(
                        "vector",
                        lambda eng, c=c, nt=nt, tsz=tsz, dd=dd: eng.tensor_scalar(
                            out=dd[c][:, nt:nt + tsz], in0=st["tmp"][:, 0:tsz],
                            scalar1=gain_col[:, c:c + 1],
                            scalar2=bias_col[:, c:c + 1],
                            op0=ALU.mult, op1=ALU.add),
                        after=[prev])
            toks_out.append(prev)
        cx.bank_set(psMu, toks_out[-1])
        cx.bank_set(psRs, toks_out[-1])
    pl.default_tag = _saved_tag
    return toks_out


def ln_scratch(cx, Tn=512, stack=None):
    return {
        "sq": [cx.sb("lnsq", [128, Tn], F32, stack) for _ in range(2)],
        "tmp": cx.sb("lntmp", [128, Tn], F32, stack),
        "mu": cx.sb("lnmu", [1, Tn], F32, stack),
        "var": cx.sb("lnvar", [1, Tn], F32, stack),
        "r0": cx.sb("lnr0", [1, Tn], F32, stack),
        "r1": cx.sb("lnr1", [1, Tn], F32, stack),
    }


def build_launch_A(debug=False):
    cx = Ctx("A")
    pl = cx.pl

    xq = cx.inp("xq", [D, QT])
    xinit = cx.inp("xinit", [D, 1024])
    wq_d = cx.inp("Wq", [D, D]); wk_d = cx.inp("Wk", [D, D])
    wv_d = cx.inp("Wv", [D, D]); wo_d = cx.inp("Wo", [D, D])
    w1_d = cx.inp("W1", [D, 512]); w2_d = cx.inp("W2", [512, D])
    b1_d = cx.inp("b1c", [128, 4]); b2_d = cx.inp("b2c", [128, 8])
    onesc_d = cx.inp("ones_col", [128, 1])
    onesr_d = cx.inp("ones_row", [1, 128])
    mask_d = cx.inp("causal_mask", [1024, 1024])
    flag_d = cx.inp("flag", [128, 2])
    ident_d = cx.inp("ident_bf", [128, 128], BF16)

    x1_out = cx.outp("x1", [D, QT])
    gpart_out = cx.outp("gpart", [128, 8])
    if debug:
        dbg_ctx = cx.outp("d_ctx", [D, 256], BF16)
        dbg_kctx = cx.outp("d_kctx", [D, 256], BF16)
        dbg_vtc = cx.outp("d_vtc", [256, D], BF16)
        dbg_wq = cx.outp("d_wq", [128, 8 * D], BF16)
        dbg_wo = cx.outp("d_wo", [128, 8 * D], BF16)
        dbg_q = cx.outp("d_q", [D, 512], BF16)
        dbg_ipo = cx.outp("d_ipo", [D, 1024], BF16)

    Tn = 512
    ones_col = cx.sb("onesc", [128, 1]); ones_row = cx.sb("onesr", [1, 128])
    flag_sb = cx.sb("flag", [128, 2])
    ident_bf = cx.sb("identbf", [128, 128], BF16)
    b1_sb = cx.sb("b1", [128, 4]); b2_sb = cx.sb("b2", [128, 8])
    lns = ln_scratch(cx, Tn=512)
    ctx_bf = [cx.sb("ctxb", [128, 256], BF16) for _ in range(NCH)]
    k_ctx = [cx.sb("kctx", [128, 256], BF16) for _ in range(NCH)]
    vT_ctx = [cx.sb("vtc", [128, D], BF16) for _ in range(2)]
    initial_preO = [cx.sb("ipo", [128, 1024], BF16) for _ in range(NCH)]
    sm_s = cx.sb("sms", [128, 1024], F32)
    sm_m = cx.sb("smm", [128, 1], F32)
    sm_r = cx.sb("smr", [128, 1], F32)
    sm_p = cx.sb("smp", [128, 1024], BF16)
    sm_pt = cx.sb("smpt", [128, 1024], BF16)
    gacc = cx.sb("gacc", [128, 8], F32)

    tok_const = [
        pl.emit("sync", lambda eng: eng.dma_start(out=ones_col[:], in_=onesc_d), dma=True),
        pl.emit("sync", lambda eng: eng.dma_start(out=ones_row[:], in_=onesr_d), dma=True),
        pl.emit("sync", lambda eng: eng.dma_start(out=flag_sb[:], in_=flag_d), dma=True),
        pl.emit("sync", lambda eng: eng.dma_start(out=ident_bf[:], in_=ident_d), dma=True),
        pl.emit("sync", lambda eng: eng.dma_start(out=b1_sb[:], in_=b1_d), dma=True),
        pl.emit("sync", lambda eng: eng.dma_start(out=b2_sb[:], in_=b2_d), dma=True),
    ]

    psA = [cx.ps("psA", [128, 512], F32) for _ in range(2)]
    ps_ln = [cx.ps("psStat", [64, 512], F32),
             cx.ps("psMu", [128, 512], F32), cx.ps("psRs", [128, 512], F32)]
    psS = cx.ps("psS", [128, 1024], F32)
    psT = cx.ps("psT", [128, 1024], BF16)
    psV = psA[0]

    mid = ExitStack()
    ln_init_bf = [cx.sb("lni", [128, 1024], BF16, mid, side="right") for _ in range(NCH)]
    k_init = [cx.sb("kini", [128, 1024], BF16, mid, side="right") for _ in range(NCH)]
    vT_init = [cx.sb("vti", [128, D], BF16, mid, side="right") for _ in range(8)]

    ph0 = ExitStack()
    xinit_f = [cx.sb("xi", [128, 1024], F32, ph0, side="right") for _ in range(NCH)]
    t_xi = [pl.emit("sync",
                    lambda eng, c=c: eng.dma_start(
                        out=xinit_f[c][:, :], in_=xinit[128 * c:128 * (c + 1), :]),
                    dma=True)
            for c in range(NCH)]
    ln0_toks = emit_ln(cx, xinit_f, 1024, dst_bf=ln_init_bf, after=t_xi + tok_const,
                       psum2=ps_ln, ones_col=ones_col, ones_row=ones_row,
                       scratch=lns)
    ph0.close()

    ph1 = ExitStack()
    bar1 = pl.barrier()
    w1_sb, tw1 = load_weight_ph(cx, w1_d, D, 512, "w1", ph1, dep=bar1)
    w2_sb, tw2 = load_weight_ph(cx, w2_d, 512, D, "w2", ph1, dep=bar1)
    h1_bf = [cx.sb("h1", [128, 1024], BF16, ph1, side="right") for _ in range(4)]
    comp_f = [cx.sb("comp", [128, 1024], F32, ph1, side="right") for _ in range(NCH)]
    t_h1 = simple_linear(cx, w1_sb, NCH, 512, ln_init_bf, 1024, h1_bf,
                         after=ln0_toks + [tw1], psA=psA, bias_col=b1_sb,
                         act=AF.Relu)
    t_c2 = simple_linear(cx, w2_sb, 4, D, h1_bf, 1024, comp_f,
                         after=t_h1 + [tw2], psA=psA, bias_col=b2_sb)
    t_pool = []
    for c in range(NCH):
        tp = pl.emit(
            "vector",
            lambda eng, c=c: eng.tensor_reduce(
                out=lns["sq"][0][:, 0:256],
                in_=comp_f[c][:, :].rearrange("p (g four) -> p g four", four=4),
                axis=AX.X, op=ALU.add),
            after=t_c2)
        tp = pl.emit(
            "vector",
            lambda eng, c=c: eng.tensor_scalar_mul(
                out=ctx_bf[c][:, :], in0=lns["sq"][0][:, 0:256], scalar1=0.25),
            after=[tp])
        t_pool.append(tp)
    ph1.close()

    ph2 = ExitStack()
    bar2 = pl.barrier()
    wk_sb, twk = load_weight_ph(cx, wk_d, D, D, "wk", ph2, dep=bar2)
    wv_sb, twv = load_weight_ph(cx, wv_d, D, D, "wv", ph2, dep=bar2)
    t_kctx = simple_linear(cx, wk_sb, NCH, D, ctx_bf, 256, k_ctx,
                           after=t_pool + [twk], psA=psA)
    t_kini = simple_linear(cx, wk_sb, NCH, D, ln_init_bf, 1024, k_init,
                           after=ln0_toks + [twk], psA=psA)

    def vT_make(act_bf, ntok, dst, dep):
        toks = []
        for tt in range(ntok // 128):
            for half in range(2):
                ps = psA[half]

                def mm(eng, tt=tt, half=half, ps=ps):
                    ret = []
                    for k in range(NCH):
                        ret.append(eng.matmul(
                            ps[:, 0:512],
                            act_bf[k][:, 128 * tt:128 * (tt + 1)],
                            wv_sb[:, k * D + 512 * half: k * D + 512 * half + 512],
                            start=(k == 0), stop=(k == NCH - 1)))
                    return ret[-1]
                pe = pl.emit("tensor", mm, after=list(dep) + cx.bank_wait(ps))
                ev = pl.emit(
                    "vector",
                    lambda eng, tt=tt, half=half, ps=ps: eng.tensor_copy(
                        out=dst[tt][:, 512 * half:512 * half + 512],
                        in_=ps[:, 0:512]),
                    after=[pe])
                cx.bank_set(ps, ev)
                toks.append(ev)
        return toks
    t_vtc = vT_make(ctx_bf, 256, vT_ctx, t_pool + [twv])
    t_vti = vT_make(ln_init_bf, 1024, vT_init, ln0_toks + [twv])
    ph2.close()

    ph3 = ExitStack()
    cx.ph3_step_idx = len(pl.steps)
    bar3 = pl.barrier()
    wq_sb, twq = load_weight_ph(cx, wq_d, D, D, "wq", ph3, wsb_stack=None,
                                dep=bar3)
    wo_sb, two = load_weight_ph(cx, wo_d, D, D, "wo", ph3, wsb_stack=None,
                                dep=bar3)
    q_init = [cx.sb("qini", [128, 1024], BF16, ph3, side="right") for _ in range(NCH)]
    mask_sb = cx.sb("maskt", [128, 1024], F32, ph3, side="right")
    t_qini = simple_linear(cx, wq_sb, NCH, D, ln_init_bf, 1024, q_init,
                           after=bar3 + [twq], psA=psA)

    def attn_unit(qsrc, ksrc, vTsrc, nkeys, qt, head, use_mask, dep, dst):
        hc, hb = head // 2, (head % 2) * 64

        def mm_s(eng):
            ret = []
            for half in range((nkeys + 511) // 512):
                w = min(512, nkeys - 512 * half)
                ret.append(eng.matmul(
                    psS[:, 512 * half:512 * half + w],
                    qsrc[hc][hb:hb + 64, 128 * qt:128 * (qt + 1)],
                    ksrc[hc][hb:hb + 64, 512 * half:512 * half + w],
                    start=True, stop=True))
            return ret[-1]
        t_s = pl.emit("tensor", mm_s, after=dep, tag="attn")
        if use_mask:
            t_ms = pl.emit(
                "vector",
                lambda eng: eng.scalar_tensor_tensor(
                    out=sm_s[:, 0:nkeys], in0=psS[:, 0:nkeys], scalar=0.125,
                    in1=mask_sb[:, 0:nkeys],
                    op0=ALU.mult, op1=ALU.add),
                after=[t_s], tag="attn")
        else:
            t_ms = pl.emit(
                "vector",
                lambda eng: eng.tensor_scalar_mul(
                    out=sm_s[:, 0:nkeys], in0=psS[:, 0:nkeys], scalar1=0.125),
                after=[t_s], tag="attn")
        t_m = pl.emit(
            "vector",
            lambda eng: eng.reduce_max(out=sm_m[:, 0:1], in_=sm_s[:, 0:nkeys],
                                       axis=AX.X, negate=True),
            after=[t_ms], tag="attn")
        t_p = pl.emit(
            "scalar",
            lambda eng: eng.activation(out=sm_p[:, 0:nkeys], in_=sm_s[:, 0:nkeys],
                                       func=AF.Exp, bias=sm_m[:, 0:1],
                                       accum_out=sm_r[:, 0:1]),
            after=[t_m], tag="attn")
        t_rc = pl.emit(
            "vector",
            lambda eng: eng.reciprocal(out=sm_r[:, 0:1], in_=sm_r[:, 0:1]),
            after=[t_p], tag="attn")
        t_pn = pl.emit(
            "vector",
            lambda eng: eng.tensor_scalar_mul(
                out=sm_p[:, 0:nkeys], in0=sm_p[:, 0:nkeys], scalar1=sm_r[:, 0:1]),
            after=[t_rc], strict_after=[t_rc], tag="attn")

        def mm_t(eng):
            ret = []
            for kc in range(nkeys // 128):
                ret.append(eng.transpose(
                    psT[:, 128 * kc:128 * (kc + 1)],
                    sm_p[:, 128 * kc:128 * (kc + 1)], ident_bf[:, :]))
            return ret[-1]
        t_tr = pl.emit("tensor", mm_t, after=[t_pn], tag="attn")
        t_te = pl.emit(
            "vector",
            lambda eng: eng.tensor_copy(out=sm_pt[:, 0:nkeys], in_=psT[:, 0:nkeys]),
            after=[t_tr], tag="attn")

        def mm_av(eng):
            ret = []
            for kc in range(nkeys // 128):
                ret.append(eng.matmul(
                    psV[0:64, 0:128],
                    vTsrc[kc][:, head * 64:head * 64 + 64],
                    sm_pt[:, 128 * kc:128 * (kc + 1)],
                    start=(kc == 0), stop=(kc == nkeys // 128 - 1)))
            return ret[-1]
        t_av = pl.emit("tensor", mm_av, after=[t_te] + cx.bank_wait(psV), tag="attn")
        t_ev = pl.emit(
            "scalar",
            lambda eng: eng.copy(
                out=dst[hc][hb:hb + 64, 128 * qt:128 * (qt + 1)],
                in_=psV[0:64, 0:128]),
            after=[t_av], tag="attn")
        cx.bank_set(psV, t_ev)
        return t_ev

    att_toks = []
    prev = last(t_qini + t_kini + t_vti + tok_const)
    for qt in range(8):
        t_mk = pl.emit("sync",
                       lambda eng, qt=qt: eng.dma_start(
                           out=mask_sb[:, :],
                           in_=mask_d[128 * qt:128 * (qt + 1), :]),
                       after=prev, dma=True)
        for head in range(16):
            t = attn_unit(q_init, k_init, vT_init, 1024, qt, head,
                          True, last(list(prev) + [t_mk]), initial_preO)
            prev = [t]
            att_toks.append(t)
    ph3.close()
    mid.close()

    ph4 = ExitStack()
    x_t = [cx.sb("xt", [128, Tn], F32, ph4, side="right") for _ in range(NCH)]
    ln1_bf = [cx.sb("l1b", [128, Tn], BF16, ph4, side="right") for _ in range(NCH)]
    q_bf = [cx.sb("qb", [128, Tn], BF16, ph4, side="right") for _ in range(NCH)]
    att_f = [cx.sb("acat", [128, Tn], F32, ph4, side="right") for _ in range(NCH)]
    mrg_bf = [cx.sb("mrg", [128, Tn], BF16, ph4, side="right") for _ in range(NCH)]
    o_f = [cx.sb("of", [128, Tn], F32, ph4, side="right") for _ in range(NCH)]
    x1_t = [cx.sb("x1t", [128, Tn], F32, ph4, side="right") for _ in range(NCH)]
    lnp_f = att_f
    ln2_f = o_f

    bar4 = pl.barrier()
    t_gz = pl.emit("vector", lambda eng: eng.memset(gacc[:, :], 0.0))

    prev_tile_toks = last(bar4 + [two, t_gz])
    store_toks = []
    for ti in range(QT // Tn):
        t0 = ti * Tn
        t_x = [pl.emit("sync",
                       lambda eng, c=c, t0=t0: eng.dma_start(
                           out=x_t[c][:, :], in_=xq[128 * c:128 * (c + 1), t0:t0 + Tn]),
                       after=prev_tile_toks, dma=True, tag="dma4")
               for c in range(NCH)]
        t_ln1 = emit_ln(cx, x_t, Tn, dst_bf=ln1_bf, after=t_x + tok_const,
                        psum2=ps_ln, ones_col=ones_col, ones_row=ones_row,
                        scratch=lns)
        t_q = simple_linear(cx, wq_sb, NCH, D, ln1_bf, Tn, q_bf,
                            after=t_ln1, psA=psA)
        aw = last(t_q)
        for head in range(16):
            for sq in range(Tn // 128):
                t = attn_unit(q_bf, k_ctx, vT_ctx, 256, sq, head, False,
                              aw, att_f)
                aw = [t]
        t_cross = aw
        t_mrg = []
        for c in range(NCH):
            if t0 < 1024:
                t_a = pl.emit(
                    "vector",
                    lambda eng, c=c, t0=t0: eng.tensor_scalar_mul(
                        out=lns["sq"][0][:, 0:Tn],
                        in0=initial_preO[c][:, t0:t0 + Tn],
                        scalar1=flag_sb[:, 0:1]),
                    after=t_cross, tag="mr")
                t_b = pl.emit(
                    "vector",
                    lambda eng, c=c: eng.scalar_tensor_tensor(
                        out=lns["sq"][1][:, 0:Tn], in0=att_f[c][:, 0:Tn],
                        scalar=flag_sb[:, 1:2], in1=lns["sq"][0][:, 0:Tn],
                        op0=ALU.mult, op1=ALU.add),
                    after=[t_a], tag="mr")
                t_c = pl.emit(
                    "vector",
                    lambda eng, c=c: eng.tensor_copy(
                        out=mrg_bf[c][:, :], in_=lns["sq"][1][:, 0:Tn]),
                    after=[t_b], tag="mr")
            else:
                t_c = pl.emit(
                    "vector",
                    lambda eng, c=c: eng.tensor_copy(
                        out=mrg_bf[c][:, :], in_=att_f[c][:, 0:Tn]),
                    after=t_cross, tag="mr")
            t_mrg.append(t_c)
        t_o = simple_linear(cx, wo_sb, NCH, D, mrg_bf, Tn, o_f,
                            after=t_mrg + [two], psA=psA)
        t_lnp = emit_ln(cx, o_f, Tn, dst_f32=lnp_f, after=t_o,
                        psum2=ps_ln, ones_col=ones_col, ones_row=ones_row,
                        scratch=lns)
        t_res = [pl.emit(
            "vector",
            lambda eng, c=c: eng.tensor_add(
                out=x1_t[c][:, :], in0=x_t[c][:, :], in1=lnp_f[c][:, :]),
            after=t_lnp + t_x, tag="mr") for c in range(NCH)]
        t_ln2 = emit_ln(cx, x1_t, Tn, dst_f32=ln2_f, after=t_res,
                        psum2=ps_ln, ones_col=ones_col, ones_row=ones_row,
                        scratch=lns)
        t_gs = []
        for c in range(NCH):
            tg = pl.emit(
                "vector",
                lambda eng, c=c: eng.tensor_reduce(
                    out=lns["sq"][0][:, 0:1],
                    in_=ln2_f[c][:, :], axis=AX.X, op=ALU.add),
                after=t_ln2, tag="mr")
            tg = pl.emit(
                "vector",
                lambda eng, c=c: eng.tensor_add(
                    out=gacc[:, c:c + 1], in0=gacc[:, c:c + 1],
                    in1=lns["sq"][0][:, 0:1]),
                after=[tg], strict_after=[tg], tag="mr")
            t_gs.append(tg)
        t_st = [pl.emit("gpsimd",
                        lambda eng, c=c, t0=t0: eng.dma_start(
                            out=x1_out[128 * c:128 * (c + 1), t0:t0 + Tn],
                            in_=x1_t[c][:, :]),
                        after=t_res[c:c + 1], dma=True, tag="dma4")
                for c in range(NCH)]
        store_toks += t_st
        prev_tile_toks = last(t_res + t_gs + t_ln2 + t_st)
    t_gp = pl.emit("gpsimd",
                   lambda eng: eng.dma_start(out=gpart_out, in_=gacc[:, :]),
                   after=prev_tile_toks, dma=True)
    if debug:
        for c in range(NCH):
            pl.emit("gpsimd", lambda eng, c=c: eng.dma_start(
                out=dbg_ctx[128 * c:128 * (c + 1), :], in_=ctx_bf[c][:, :]),
                after=prev_tile_toks, dma=True)
            pl.emit("gpsimd", lambda eng, c=c: eng.dma_start(
                out=dbg_kctx[128 * c:128 * (c + 1), :], in_=k_ctx[c][:, :]),
                after=prev_tile_toks, dma=True)
        for tt in range(2):
            pl.emit("gpsimd", lambda eng, tt=tt: eng.dma_start(
                out=dbg_vtc[128 * tt:128 * (tt + 1), :], in_=vT_ctx[tt][:, :]),
                after=prev_tile_toks, dma=True)
        pl.emit("gpsimd", lambda eng: eng.dma_start(out=dbg_wq, in_=wq_sb[:, :]),
                after=prev_tile_toks, dma=True)
        pl.emit("gpsimd", lambda eng: eng.dma_start(out=dbg_wo, in_=wo_sb[:, :]),
                after=prev_tile_toks, dma=True)
        for c in range(NCH):
            pl.emit("gpsimd", lambda eng, c=c: eng.dma_start(
                out=dbg_q[128 * c:128 * (c + 1), :], in_=q_bf[c][:, :]),
                after=prev_tile_toks, dma=True)
            pl.emit("gpsimd", lambda eng, c=c: eng.dma_start(
                out=dbg_ipo[128 * c:128 * (c + 1), :], in_=initial_preO[c][:, :]),
                after=prev_tile_toks, dma=True)
    pl.emit("gpsimd", lambda eng: eng.engine_nop(),
            after=last(store_toks + [t_gp]))
    ph4.close()

    return cx.finish()


def load_weight_ph(cx, dram_ap, rows, cols, name, phase_stack, wsb_stack="phase",
                   dep=(), cast_engine="gpsimd"):
    nk = (rows + 127) // 128
    stage = [cx.sb(name + "_st", [128, cols], F32, phase_stack, side="right")
             for _ in range(2)]
    if wsb_stack == "phase":
        wsb = cx.sb(name, [128, nk * cols], BF16, phase_stack, side="right")
    else:
        wsb = cx.sb(name, [128, nk * cols], BF16)
    cast_toks = [None, None]
    tok = None
    _saved_tag = cx.pl.default_tag
    cx.pl.default_tag = "wload"
    for k in range(nk):
        sg = stage[k % 2]
        rsz = min(128, rows - 128 * k)
        waits = list(dep) + ([cast_toks[k % 2]] if cast_toks[k % 2] else [])
        t_d = cx.pl.emit(
            "sync",
            lambda eng, k=k, sg=sg, rsz=rsz: eng.dma_start(
                out=sg[0:rsz, :], in_=dram_ap[128 * k:128 * k + rsz, :]),
            after=waits, dma=True)
        tok = cx.pl.emit(
            cast_engine,
            lambda eng, k=k, sg=sg, rsz=rsz: eng.tensor_copy(
                out=wsb[0:rsz, k * cols:(k + 1) * cols], in_=sg[0:rsz, :]),
            after=[t_d])
        cast_toks[k % 2] = tok
    cx.pl.default_tag = _saved_tag
    return wsb, tok


def build_launch_B():
    cx = Ctx("B")
    pl = cx.pl
    Tn = 512

    x1q = cx.inp("x1q", [D, QT])
    gp_all = cx.inp("gp_all", [128, 32])
    cnw1_d = cx.inp("cn_w1", [D, D]); cnw2_d = cx.inp("cn_w2", [D, 512])
    cnw3_d = cx.inp("cn_w3", [512, 256]); cnw4_d = cx.inp("cn_w4", [256, 1])
    cb1_d = cx.inp("cn_b1c", [128, 8]); cb2_d = cx.inp("cn_b2c", [128, 4])
    cb3_d = cx.inp("cn_b3c", [128, 2]); cb4_d = cx.inp("cn_b4c", [1, 1])
    gw1_d = cx.inp("gc_w1", [D, 512]); gw2_d = cx.inp("gc_w2", [512, 256])
    gw3_d = cx.inp("gc_w3", [256, 1])
    gb1_d = cx.inp("gc_b1c", [128, 4]); gb2_d = cx.inp("gc_b2c", [128, 2])
    gb3_d = cx.inp("gc_b3c", [1, 1])
    ar_d = cx.inp("ar", [1, 1])
    onesc_d = cx.inp("ones_col", [128, 1]); onesr_d = cx.inp("ones_row", [1, 128])
    negr_d = cx.inp("neg_row", [1, 128])

    x2_out = cx.outp("x2", [D, QT])
    ln3_out = cx.outp("ln3", [D, QT], BF16)

    ones_col = cx.sb("onesc", [128, 1]); ones_row = cx.sb("onesr", [1, 128])
    neg_row = cx.sb("negr", [1, 128])
    lns = ln_scratch(cx, Tn=512)
    rowbuf = cx.sb("rowb", [1, 5 * Tn], F32)
    sc_sb = cx.sb("scs", [1, 8], F32)
    b_cols = cx.sb("bcols", [128, 8 + 4 + 2 + 4 + 2], F32)
    cb4_sb = cx.sb("cb4", [1, 2], F32)

    tok_c = [
        pl.emit("sync", lambda eng: eng.dma_start(out=ones_col[:], in_=onesc_d), dma=True),
        pl.emit("sync", lambda eng: eng.dma_start(out=ones_row[:], in_=onesr_d), dma=True),
        pl.emit("sync", lambda eng: eng.dma_start(out=neg_row[:], in_=negr_d), dma=True),
        pl.emit("sync", lambda eng: eng.dma_start(out=b_cols[:, 0:8], in_=cb1_d), dma=True),
        pl.emit("sync", lambda eng: eng.dma_start(out=b_cols[:, 8:12], in_=cb2_d), dma=True),
        pl.emit("sync", lambda eng: eng.dma_start(out=b_cols[:, 12:14], in_=cb3_d), dma=True),
        pl.emit("sync", lambda eng: eng.dma_start(out=b_cols[:, 14:18], in_=gb1_d), dma=True),
        pl.emit("sync", lambda eng: eng.dma_start(out=b_cols[:, 18:20], in_=gb2_d), dma=True),
        pl.emit("sync", lambda eng: eng.dma_start(out=cb4_sb[:, 0:1], in_=cb4_d), dma=True),
        pl.emit("sync", lambda eng: eng.dma_start(out=cb4_sb[:, 1:2], in_=gb3_d), dma=True),
        pl.emit("sync", lambda eng: eng.dma_start(out=sc_sb[0:1, 0:1], in_=ar_d), dma=True),
    ]
    cb1 = b_cols[:, 0:8]; cb2 = b_cols[:, 8:12]; cb3 = b_cols[:, 12:14]
    gb1 = b_cols[:, 14:18]; gb2 = b_cols[:, 18:20]

    psA = [cx.ps("psA", [128, 512], F32) for _ in range(2)]
    ps_ln = [cx.ps("psStat", [64, 512], F32),
             cx.ps("psMu", [128, 512], F32), cx.ps("psRs", [128, 512], F32)]
    psB = [cx.ps("psB", [128, 512], F32) for _ in range(2)]

    wstack = ExitStack()
    cnw1_sb, t1 = load_weight_ph(cx, cnw1_d, D, D, "cnw1", wstack, wsb_stack=None)
    cnw2_sb, t2 = load_weight_ph(cx, cnw2_d, D, 512, "cnw2", wstack, wsb_stack=None)
    cnw3_sb, t3 = load_weight_ph(cx, cnw3_d, 512, 256, "cnw3", wstack, wsb_stack=None)
    cnw4_sb, t4 = load_weight_ph(cx, cnw4_d, 256, 1, "cnw4", wstack, wsb_stack=None)
    gw1_sb, t5 = load_weight_ph(cx, gw1_d, D, 512, "gw1", wstack, wsb_stack=None)
    gw2_sb, t6 = load_weight_ph(cx, gw2_d, 512, 256, "gw2", wstack, wsb_stack=None)
    gw3_sb, t7 = load_weight_ph(cx, gw3_d, 256, 1, "gw3", wstack, wsb_stack=None)
    wtoks = [t1, t2, t3, t4, t5, t6, t7]
    wstack.close()

    gp_sb = cx.sb("gp", [128, 32])
    gc_f = cx.sb("gc", [128, 8], F32)
    gc_bf = cx.sb("gcb", [128, 8], BF16)
    g1_bf = [cx.sb("g1", [128, 1], BF16) for _ in range(4)]
    g2_bf = [cx.sb("g2", [128, 1], BF16) for _ in range(2)]
    t_gp = pl.emit("sync", lambda eng: eng.dma_start(out=gp_sb[:], in_=gp_all), dma=True)
    t_gs = pl.emit(
        "vector",
        lambda eng: eng.tensor_reduce(
            out=gc_f[:, :],
            in_=gp_sb[:, :].rearrange("p (g e) -> p e g", g=4),
            axis=AX.X, op=ALU.add),
        after=[t_gp])
    t_gs = pl.emit(
        "vector",
        lambda eng: eng.tensor_scalar_mul(out=gc_f[:, :], in0=gc_f[:, :],
                                          scalar1=1.0 / 16384.0),
        after=[t_gs], strict_after=[t_gs])
    t_gc = pl.emit("vector", lambda eng: eng.tensor_copy(out=gc_bf[:, :], in_=gc_f[:, :]),
                   after=[t_gs], strict_after=[t_gs])
    t_g1 = simple_linear(cx, gw1_sb, NCH, 512, [gc_bf[:, c:c + 1] for c in range(NCH)],
                         1, g1_bf, after=[t_gc, t5] + tok_c, psA=psA,
                         bias_col=gb1, act=AF.Gelu)
    t_g2 = simple_linear(cx, gw2_sb, 4, 256, g1_bf, 1, g2_bf,
                         after=t_g1 + [t6], psA=psA, bias_col=gb2, act=AF.Gelu)
    def gf_mm(eng):
        return [eng.matmul(psA[0][0:1, 0:1], gw3_sb[:, k * 1:k * 1 + 1],
                           g2_bf[k][:, 0:1], start=(k == 0), stop=(k == 1))
                for k in range(2)][-1]
    t_g3 = pl.emit("tensor", gf_mm, after=last(t_g2) + cx.bank_wait(psA[0]))
    t_gf = pl.emit(
        "vector",
        lambda eng: eng.tensor_scalar_add(out=sc_sb[0:1, 1:2], in0=psA[0][0:1, 0:1],
                                          scalar1=cb4_sb[0:1, 1:2]),
        after=[t_g3])
    cx.bank_set(psA[0], t_gf)
    t_gf = pl.emit(
        "scalar",
        lambda eng: eng.activation(out=sc_sb[0:1, 1:2], in_=sc_sb[0:1, 1:2],
                                   func=AF.Sigmoid),
        after=[t_gf], strict_after=[t_gf])
    t_a = pl.emit("vector", lambda eng: eng.tensor_scalar_mul(
        out=sc_sb[0:1, 2:3], in0=sc_sb[0:1, 0:1], scalar1=0.6),
        after=[t_gf] + tok_c)
    t_b = pl.emit("vector", lambda eng: eng.tensor_mul(
        out=sc_sb[0:1, 4:5], in0=sc_sb[0:1, 0:1], in1=sc_sb[0:1, 1:2]),
        after=[t_gf] + tok_c)
    t_b = pl.emit("vector", lambda eng: eng.scalar_tensor_tensor(
        out=sc_sb[0:1, 3:4], in0=sc_sb[0:1, 4:5], scalar=1.4,
        in1=sc_sb[0:1, 0:1], op0=ALU.mult, op1=ALU.subtract),
        after=[t_b], strict_after=[t_b])
    t_b = pl.emit("vector", lambda eng: eng.tensor_scalar_add(
        out=sc_sb[0:1, 3:4], in0=sc_sb[0:1, 3:4], scalar1=0.1),
        after=[t_b], strict_after=[t_b])

    ph = ExitStack()
    bar = pl.barrier()
    x1_t = [cx.sb("x1t", [128, Tn], F32, ph, side="right") for _ in range(NCH)]
    outA = [cx.sb("oa", [128, Tn], F32, ph, side="right") for _ in range(NCH)]
    outB = [cx.sb("ob", [128, Tn], F32, ph, side="right") for _ in range(NCH)]
    out_bf = [cx.sb("obf", [128, Tn], BF16, ph, side="right") for _ in range(NCH)]
    h1_bf = [cx.sb("h1", [128, Tn], BF16, ph, side="right") for _ in range(NCH)]
    h2_bf = [cx.sb("h2", [128, Tn], BF16, ph, side="right") for _ in range(4)]
    h3_bf = [cx.sb("h3", [128, Tn], BF16, ph, side="right") for _ in range(2)]
    x2_t = [cx.sb("x2t", [128, Tn], F32, ph, side="right") for _ in range(NCH)]
    ln3_t = [cx.sb("l3t", [128, Tn], BF16, ph, side="right") for _ in range(NCH)]
    tw_s = cx.sb("tws", [128, Tn], F32, ph, side="right")

    TW_SRC = [4, 5, 6, 7, 0, 1, 2, 3]
    TW_SGN = [+1, +1, -1, -1, -1, -1, +1, +1]

    prev_tile = last(bar + wtoks + [t_a, t_b])
    store_toks = []
    for ti in range(QT // Tn):
        t0 = ti * Tn
        t_x = [pl.emit("sync",
                       lambda eng, c=c, t0=t0: eng.dma_start(
                           out=x1_t[c][:, :], in_=x1q[128 * c:128 * (c + 1), t0:t0 + Tn]),
                       after=prev_tile, dma=True, tag="dma4")
               for c in range(NCH)]
        t_ln = emit_ln(cx, x1_t, Tn, dst_f32=outA, dst_bf=out_bf,
                       after=t_x + tok_c, psum2=ps_ln, ones_col=ones_col,
                       ones_row=ones_row, scratch=lns)
        cur, nxt = outA, outB
        t_prev = t_ln
        for cyc in range(3):
            t_h1 = simple_linear(cx, cnw1_sb, NCH, D, out_bf, Tn, h1_bf,
                                 after=t_prev + [t1], psA=psA, bias_col=cb1,
                                 act=AF.Gelu)
            t_h2 = simple_linear(cx, cnw2_sb, NCH, 512, h1_bf, Tn, h2_bf,
                                 after=t_h1 + [t2], psA=psA, bias_col=cb2,
                                 act=AF.Gelu)
            t_h3 = simple_linear(cx, cnw3_sb, 4, 256, h2_bf, Tn, h3_bf,
                                 after=t_h2 + [t3], psA=psA, bias_col=cb3,
                                 act=AF.Gelu)
            def tf_mm(eng):
                return [eng.matmul(psA[0][0:1, 0:Tn], cnw4_sb[:, k:k + 1],
                                   h3_bf[k][:, 0:Tn], start=(k == 0), stop=(k == 1))
                        for k in range(2)][-1]
            t_tf = pl.emit("tensor", tf_mm, after=last(t_h3) + cx.bank_wait(psA[0]))
            t_tf2 = pl.emit("vector", lambda eng: eng.tensor_scalar_add(
                out=rowbuf[0:1, 0:Tn], in0=psA[0][0:1, 0:Tn],
                scalar1=cb4_sb[0:1, 0:1]), after=[t_tf])
            cx.bank_set(psA[0], t_tf2)
            t_tf3 = pl.emit("scalar", lambda eng: eng.activation(
                out=rowbuf[0:1, 0:Tn], in_=rowbuf[0:1, 0:Tn], func=AF.Sigmoid),
                after=[t_tf2], strict_after=[t_tf2])
            t_cp = pl.emit("vector", lambda eng: eng.tensor_scalar(
                out=rowbuf[0:1, Tn:2 * Tn], in0=rowbuf[0:1, 0:Tn],
                scalar1=sc_sb[0:1, 2:3], scalar2=sc_sb[0:1, 3:4],
                op0=ALU.mult, op1=ALU.add),
                after=[t_tf3, t_a, t_b], strict_after=[t_tf3])
            t_bp = pl.emit("tensor", lambda eng: eng.matmul(
                psB[0][:, 0:Tn], ones_row[0:1, :], rowbuf[0:1, Tn:2 * Tn],
                start=True, stop=True),
                after=[t_cp] + cx.bank_wait(psB[0]))
            t_bn = pl.emit("tensor", lambda eng: eng.matmul(
                psB[1][:, 0:Tn], neg_row[0:1, :], rowbuf[0:1, Tn:2 * Tn],
                start=True, stop=True),
                after=[t_cp] + cx.bank_wait(psB[1]))
            t_tw = []
            for c in range(NCH):
                psx = psB[0] if TW_SGN[c] > 0 else psB[1]
                srcb = cur[TW_SRC[c]]; curb = cur[c]; nxtb = nxt[c]
                ta = pl.emit("vector", lambda eng, srcb=srcb, psx=psx: eng.tensor_mul(
                    out=tw_s[:, 0:Tn], in0=srcb[:, 0:Tn],
                    in1=psx[:, 0:Tn]),
                    after=[t_bp, t_bn] + t_prev)
                tb = pl.emit("vector", lambda eng, curb=curb, nxtb=nxtb: eng.tensor_add(
                    out=nxtb[:, 0:Tn], in0=curb[:, 0:Tn], in1=tw_s[:, 0:Tn]),
                    after=[ta], strict_after=[ta])
                if cyc < 2:
                    tb = pl.emit("vector", lambda eng, c=c, nxtb=nxtb: eng.tensor_copy(
                        out=out_bf[c][:, :], in_=nxtb[:, 0:Tn]),
                        after=[tb], strict_after=[tb])
                t_tw.append(tb)
            cx.bank_set(psB[0], t_tw[-1]); cx.bank_set(psB[1], t_tw[-1])
            cur, nxt = nxt, cur
            t_prev = t_tw
        t_lnp = emit_ln(cx, cur, Tn, dst_f32=nxt, after=t_prev,
                        psum2=ps_ln, ones_col=ones_col, ones_row=ones_row,
                        scratch=lns)
        t_res = [pl.emit("vector", lambda eng, c=c: eng.tensor_add(
            out=x2_t[c][:, :], in0=x1_t[c][:, :], in1=nxt[c][:, 0:Tn]),
            after=t_lnp + t_x) for c in range(NCH)]
        t_ln3 = emit_ln(cx, x2_t, Tn, dst_bf=ln3_t, after=t_res,
                        psum2=ps_ln, ones_col=ones_col, ones_row=ones_row,
                        scratch=lns)
        t_st = [pl.emit("gpsimd", lambda eng, c=c, t0=t0: eng.dma_start(
            out=x2_out[128 * c:128 * (c + 1), t0:t0 + Tn], in_=x2_t[c][:, :]),
            after=t_res[c:c + 1], dma=True, tag="dma4") for c in range(NCH)]
        t_st += [pl.emit("gpsimd", lambda eng, c=c, t0=t0: eng.dma_start(
            out=ln3_out[128 * c:128 * (c + 1), t0:t0 + Tn], in_=ln3_t[c][:, :]),
            after=t_ln3, dma=True, tag="dma4") for c in range(NCH)]
        store_toks += t_st
        prev_tile = last(t_res + t_ln3 + t_st)
    pl.emit("gpsimd", lambda eng: eng.engine_nop(), after=last(store_toks))
    ph.close()
    return cx.finish()


def build_launch_C():
    cx = Ctx("C")
    pl = cx.pl
    Tn = 512
    NT = S // Tn

    ln3f = cx.inp("ln3f", [D, S], BF16)
    winx_d = cx.inp("winxT", [D, 512])
    winz_d = cx.inp("winzT", [D, 512])
    convw_d = cx.inp("convw_c", [128, 16])
    convb_d = cx.inp("convb_c", [128, 4])
    wx_d = cx.inp("wxT", [512, 96])

    xsil_out = cx.outp("xsil", [512, S])
    zsil_out = cx.outp("zsil", [512, S])
    xp_out = cx.outp("xp", [96, S])

    convw = cx.sb("convw", [128, 16]); convb = cx.sb("convb", [128, 4])
    tok_c = [
        pl.emit("sync", lambda eng: eng.dma_start(out=convw[:], in_=convw_d), dma=True),
        pl.emit("sync", lambda eng: eng.dma_start(out=convb[:], in_=convb_d), dma=True),
    ]
    wst = ExitStack()
    winx_sb, tx = load_weight_ph(cx, winx_d, D, 512, "winx", wst, wsb_stack=None)
    winz_sb, tz = load_weight_ph(cx, winz_d, D, 512, "winz", wst, wsb_stack=None)
    wx_sb, twx = load_weight_ph(cx, wx_d, 512, 96, "wx", wst, wsb_stack=None)
    wst.close()

    psA = [cx.ps("psA", [128, 512], F32) for _ in range(2)]
    psZ = [cx.ps("psZ", [128, 512], F32) for _ in range(2)]
    psP = cx.ps("psP", [96, 512], F32)

    ph = ExitStack()
    bar = pl.barrier()
    u_t = [cx.sb("ut", [128, Tn], BF16, ph, side="right") for _ in range(NCH)]
    xin_ext = [cx.sb("xe", [128, Tn + 3], F32, ph, side="right") for _ in range(4)]
    tail = cx.sb("tail", [128, 4 * 3], F32)
    xcv = [cx.sb("xc", [128, Tn], F32, ph, side="right") for _ in range(4)]
    xsb = [cx.sb("xsb", [128, Tn], BF16, ph, side="right") for _ in range(4)]
    zs = [cx.sb("zs", [128, Tn], F32, ph, side="right") for _ in range(4)]
    xpt = cx.sb("xpt", [96, Tn], F32, ph, side="right")

    t_tz = pl.emit("vector", lambda eng: eng.memset(tail[:, :], 0.0))
    prev_tile = last(bar + [tx, tz, twx, t_tz] + tok_c)
    store_toks = []
    for ti in range(NT):
        t0 = ti * Tn
        t_u = [pl.emit("sync", lambda eng, c=c, t0=t0: eng.dma_start(
            out=u_t[c][:, :], in_=ln3f[128 * c:128 * (c + 1), t0:t0 + Tn]),
            after=prev_tile, dma=True, tag="dma4") for c in range(NCH)]
        def xwrite(of, o0, osz, nt, tsz, ps, pe_tok):
            t1 = pl.emit("vector", lambda eng, of=of: eng.tensor_copy(
                out=xin_ext[of][:, 3:3 + Tn], in_=ps[:, 0:Tn]), after=[pe_tok])
            return t1
        t_xin = emit_linear(cx, winx_sb, NCH, 512, u_t, Tn, xwrite,
                            after=t_u + [tx], psA=psA)
        t_tl = [pl.emit("vector", lambda eng, c4=c4: eng.tensor_copy(
            out=xin_ext[c4][:, 0:3], in_=tail[:, 3 * c4:3 * c4 + 3]),
            after=prev_tile + t_u) for c4 in range(4)]
        t_z = simple_linear(cx, winz_sb, NCH, 512, u_t, Tn, zs,
                            after=t_u + [tz], psA=psZ, act=AF.Silu)
        t_stz = [pl.emit("gpsimd", lambda eng, c4=c4, t0=t0: eng.dma_start(
            out=zsil_out[128 * c4:128 * (c4 + 1), t0:t0 + Tn], in_=zs[c4][:, :]),
            after=t_z, dma=True, tag="dma4") for c4 in range(4)]
        t_cv = []
        for c4 in range(4):
            tt = pl.emit("vector", lambda eng, c4=c4: eng.tensor_scalar_mul(
                out=xcv[c4][:, :], in0=xin_ext[c4][:, 0:Tn],
                scalar1=convw[:, 4 * c4:4 * c4 + 1]),
                after=t_xin + t_tl)
            for k in range(1, 4):
                tt = pl.emit("vector", lambda eng, c4=c4, k=k: eng.scalar_tensor_tensor(
                    out=xcv[c4][:, :], in0=xin_ext[c4][:, k:Tn + k],
                    scalar=convw[:, 4 * c4 + k:4 * c4 + k + 1],
                    in1=xcv[c4][:, :], op0=ALU.mult, op1=ALU.add),
                    after=[tt], strict_after=[tt])
            tt = pl.emit("scalar", lambda eng, c4=c4: eng.activation(
                out=xcv[c4][:, :], in_=xcv[c4][:, :], func=AF.Silu,
                bias=convb[:, c4:c4 + 1]),
                after=[tt], strict_after=[tt])
            tt2 = pl.emit("vector", lambda eng, c4=c4: eng.tensor_copy(
                out=xsb[c4][:, :], in_=xcv[c4][:, :]),
                after=[tt], strict_after=[tt])
            t_cv.append(tt2)
        t_tl2 = [pl.emit("vector", lambda eng, c4=c4: eng.tensor_copy(
            out=tail[:, 3 * c4:3 * c4 + 3], in_=xin_ext[c4][:, Tn:Tn + 3]),
            after=t_cv + t_xin) for c4 in range(4)]
        t_sx = [pl.emit("gpsimd", lambda eng, c4=c4, t0=t0: eng.dma_start(
            out=xsil_out[128 * c4:128 * (c4 + 1), t0:t0 + Tn], in_=xcv[c4][:, :]),
            after=t_cv, dma=True, tag="dma4") for c4 in range(4)]
        def xp_mm(eng):
            return [eng.matmul(psP[:, 0:Tn], wx_sb[:, k * 96:(k + 1) * 96],
                               xsb[k][:, :], start=(k == 0), stop=(k == 3))
                    for k in range(4)][-1]
        t_xp = pl.emit("tensor", xp_mm, after=last(t_cv + [twx]) + cx.bank_wait(psP))
        t_xpe = pl.emit("vector", lambda eng: eng.tensor_copy(
            out=xpt[:, :], in_=psP[:, 0:Tn]), after=[t_xp])
        cx.bank_set(psP, t_xpe)
        t_sp = pl.emit("gpsimd", lambda eng, t0=t0: eng.dma_start(
            out=xp_out[:, t0:t0 + Tn], in_=xpt[:, :]),
            after=[t_xpe], dma=True, tag="dma4")
        store_toks += t_sx + t_stz + [t_sp]
        prev_tile = last(t_cv + t_tl2 + [t_xpe, t_sp] + t_sx + t_stz)
    pl.emit("gpsimd", lambda eng: eng.engine_nop(), after=last(store_toks))
    ph.close()
    return cx.finish()


def build_launch_D():
    cx = Ctx("D")
    pl = cx.pl
    TC = 1024
    NCK = S // TC
    NSUB = TC // 512

    xsil_d = cx.inp("xsil", [512, S])
    zsil_d = cx.inp("zsil", [512, S])
    xp_d = [cx.inp(f"xp{i}", [96, S]) for i in range(4)]
    wdt_d = cx.inp("wdtT", [64, 512])
    bdt_d = cx.inp("bdt_c", [128, 4])
    alog_d = cx.inp("alog_c", [128, 64])
    dp_d = cx.inp("dp_c", [128, 4])
    onesc_d = cx.inp("ones_col", [128, 1])
    onesr_d = cx.inp("ones_row", [1, 128])
    sel_d = cx.inp("sel16", [16, 2048])

    yg_out = cx.outp("ygated", [512, S], BF16)

    ones_col = cx.sb("onesc", [128, 1]); ones_row = cx.sb("onesr", [1, 128])
    bdt = cx.sb("bdt", [128, 4]); acol = cx.sb("acol", [128, 64])
    dpc = cx.sb("dpc", [128, 4])
    state = cx.sb("state", [128, 64], F32)
    sel16 = cx.sb("sel16", [16, 2048], F32)
    tok_c = [
        pl.emit("sync", lambda eng: eng.dma_start(out=sel16[:], in_=sel_d), dma=True),
        pl.emit("sync", lambda eng: eng.dma_start(out=ones_col[:], in_=onesc_d), dma=True),
        pl.emit("sync", lambda eng: eng.dma_start(out=ones_row[:], in_=onesr_d), dma=True),
        pl.emit("sync", lambda eng: eng.dma_start(out=bdt[:], in_=bdt_d), dma=True),
        pl.emit("sync", lambda eng: eng.dma_start(out=acol[:], in_=alog_d), dma=True),
        pl.emit("sync", lambda eng: eng.dma_start(out=dpc[:], in_=dp_d), dma=True),
    ]
    t_ac = pl.emit("scalar", lambda eng: eng.activation(
        out=acol[:, :], in_=acol[:, :], func=AF.Exp), after=tok_c)
    t_ac = pl.emit("vector", lambda eng: eng.tensor_scalar_mul(
        out=acol[:, :], in0=acol[:, :], scalar1=-1.0),
        after=[t_ac], strict_after=[t_ac])
    t_sz = pl.emit("vector", lambda eng: eng.memset(state[:, :], 0.0))

    wst = ExitStack()
    wdt_sb, twdt = load_weight_ph(cx, wdt_d, 64, 512, "wdt", wst, wsb_stack=None)
    wst.close()

    psD = [cx.ps("psD", [128, 512], F32) for _ in range(2)]
    psB = cx.ps("psBb", [128, 1024], F32)
    psC = cx.ps("psCb", [128, 1024], F32)

    ph = ExitStack()
    bar = pl.barrier()
    xs_t = [cx.sb("xs", [128, TC], F32, ph, side="right") for _ in range(4)]
    dt_t = [cx.sb("dt", [128, TC], F32, ph, side="right") for _ in range(4)]
    w_t = [cx.sb("w", [128, TC], F32, ph, side="right") for _ in range(4)]
    y_t = [cx.sb("y", [128, TC], F32, ph, side="right") for _ in range(4)]
    dA_t = cx.sb("dA", [128, TC], F32, ph, side="right")
    b_t = cx.sb("b", [128, TC], F32, ph, side="right")
    h_t = cx.sb("h", [128, TC], F32, ph, side="right")
    din = cx.sb("din", [64, TC], F32, ph, side="right")
    din_bf = cx.sb("dinb", [64, TC], BF16, ph, side="right")
    brow = cx.sb("brow", [16, TC], F32, ph, side="right")
    crow = cx.sb("crow", [16, TC], F32, ph, side="right")
    dpart = cx.sb("dpart", [64, TC], F32, ph, side="right")
    bpart = cx.sb("bpart", [16, TC], F32, ph, side="right")
    cpart = cx.sb("cpart", [16, TC], F32, ph, side="right")
    zrow = cx.sb("zrow", [128, 512], F32, ph, side="right")
    ygb = cx.sb("ygb", [128, TC], BF16, ph, side="right")

    prev = last(bar + [t_ac, t_sz, twdt])
    store_toks = []
    for ck in range(NCK):
        t0 = ck * TC
        t_x = [pl.emit("sync", lambda eng, c4=c4, t0=t0: eng.dma_start(
            out=xs_t[c4][:, :], in_=xsil_d[128 * c4:128 * (c4 + 1), t0:t0 + TC]),
            after=prev, dma=True, tag="dma4") for c4 in range(4)]
        t_acc = []
        for i in range(4):
            td = pl.emit("sync", lambda eng, i=i, t0=t0: eng.dma_start(
                out=dpart[0:64, :], in_=xp_d[i][0:64, t0:t0 + TC]),
                after=prev + t_acc, dma=True, tag="dma4")
            tb = pl.emit("sync", lambda eng, i=i, t0=t0: eng.dma_start(
                out=bpart[0:16, :], in_=xp_d[i][64:80, t0:t0 + TC]),
                after=prev + t_acc, dma=True, tag="dma4")
            tc2 = pl.emit("sync", lambda eng, i=i, t0=t0: eng.dma_start(
                out=cpart[0:16, :], in_=xp_d[i][80:96, t0:t0 + TC]),
                after=prev + t_acc, dma=True, tag="dma4")
            if i == 0:
                t_acc = [
                    pl.emit("vector", lambda eng: eng.tensor_copy(out=din[:, :], in_=dpart[:, :]), after=[td]),
                    pl.emit("vector", lambda eng: eng.tensor_copy(out=brow[:, :], in_=bpart[:, :]), after=[tb]),
                    pl.emit("vector", lambda eng: eng.tensor_copy(out=crow[:, :], in_=cpart[:, :]), after=[tc2]),
                ]
            else:
                t_acc = [
                    pl.emit("vector", lambda eng: eng.tensor_add(out=din[:, :], in0=din[:, :], in1=dpart[:, :]), after=[td], strict_after=t_acc),
                    pl.emit("vector", lambda eng: eng.tensor_add(out=brow[:, :], in0=brow[:, :], in1=bpart[:, :]), after=[tb], strict_after=t_acc),
                    pl.emit("vector", lambda eng: eng.tensor_add(out=crow[:, :], in0=crow[:, :], in1=cpart[:, :]), after=[tc2], strict_after=t_acc),
                ]
        t_dinb = pl.emit("gpsimd", lambda eng: eng.tensor_copy(
            out=din_bf[:, :], in_=din[:, :]), after=t_acc)
        t_dt = []
        for of in range(4):
            ps = psD[of % 2]
            toks_of = []
            for sb in range(NSUB):
                def mm2(eng, of=of, ps=ps, sb=sb):
                    return eng.matmul(ps[:, 0:512],
                                      wdt_sb[0:64, 128 * of:128 * of + 128],
                                      din_bf[0:64, 512 * sb:512 * (sb + 1)],
                                      start=True, stop=True)
                pe = pl.emit("tensor", mm2,
                             after=last(t_acc + [twdt, t_dinb]) + cx.bank_wait(ps))
                tv = pl.emit("vector", lambda eng, of=of, sb=sb, ps=ps: eng.tensor_scalar_add(
                    out=dt_t[of][:, 512 * sb:512 * (sb + 1)], in0=ps[:, 0:512],
                    scalar1=bdt[:, of:of + 1]), after=[pe])
                cx.bank_set(ps, tv)
                toks_of.append(tv)
            tm = pl.emit("vector", lambda eng, of=of: eng.tensor_scalar_max(
                out=w_t[of][:, :], in0=dt_t[of][:, :], scalar1=0.0),
                after=toks_of, strict_after=toks_of)
            tab = pl.emit("scalar", lambda eng, of=of: eng.activation(
                out=dA_t[:, :], in_=dt_t[of][:, :], func=AF.Abs),
                after=toks_of)
            te = pl.emit("scalar", lambda eng: eng.activation(
                out=dA_t[:, :], in_=dA_t[:, :], func=AF.Exp, scale=-1.0),
                after=[tab], strict_after=[tab])
            tl = pl.emit("scalar", lambda eng: eng.activation(
                out=dA_t[:, :], in_=dA_t[:, :], func=AF.Ln,
                bias=ones_col[:, 0:1]),
                after=[te] + tok_c, strict_after=[te])
            tdt = pl.emit("vector", lambda eng, of=of: eng.tensor_add(
                out=dt_t[of][:, :], in0=w_t[of][:, :], in1=dA_t[:, :]),
                after=[tl, tm], strict_after=[tl, tm])
            t_dt.append(tdt)
        t_w = [pl.emit("vector", lambda eng, c4=c4: eng.tensor_mul(
            out=w_t[c4][:, :], in0=dt_t[c4][:, :], in1=xs_t[c4][:, :]),
            after=[t_dt[c4]] + t_x, strict_after=[t_dt[c4]]) for c4 in range(4)]
        t_y0 = [pl.emit("vector", lambda eng, c4=c4: eng.memset(y_t[c4][:, :], 0.0),
                        after=prev) for c4 in range(4)]
        y_toks = list(t_y0)
        for n in range(16):
            t_bb = pl.emit("tensor", lambda eng, n=n: [
                eng.matmul(psB[:, 512 * hh:512 * (hh + 1)],
                           sel16[0:16, 128 * n:128 * (n + 1)],
                           brow[0:16, 512 * hh:512 * (hh + 1)],
                           start=True, stop=True) for hh in range(NSUB)][-1],
                after=last(t_acc) + cx.bank_wait(psB))
            t_cb = pl.emit("tensor", lambda eng, n=n: [
                eng.matmul(psC[:, 512 * hh:512 * (hh + 1)],
                           sel16[0:16, 128 * n:128 * (n + 1)],
                           crow[0:16, 512 * hh:512 * (hh + 1)],
                           start=True, stop=True) for hh in range(NSUB)][-1],
                after=last(t_acc) + cx.bank_wait(psC))
            b_read = None; c_read = None
            for c4 in range(4):
                sidx = 16 * c4 + n
                t_da = pl.emit("scalar", lambda eng, c4=c4, sidx=sidx: eng.activation(
                    out=dA_t[:, :], in_=dt_t[c4][:, :], func=AF.Exp,
                    scale=acol[:, sidx:sidx + 1]),
                    after=[t_dt[c4], t_ac] + y_toks[-1:])
                t_b = pl.emit("vector", lambda eng, c4=c4: eng.tensor_mul(
                    out=b_t[:, :], in0=w_t[c4][:, :], in1=psB[:, 0:TC]),
                    after=[t_w[c4], t_bb] + y_toks[-1:])
                b_read = t_b
                t_h = pl.emit("vector", lambda eng, sidx=sidx: eng.tensor_tensor_scan(
                    out=h_t[:, :], data0=dA_t[:, :], data1=b_t[:, :],
                    initial=state[:, sidx:sidx + 1],
                    op0=ALU.mult, op1=ALU.add),
                    after=[t_da, t_b, t_sz], strict_after=[t_b])
                t_su = pl.emit("vector", lambda eng, sidx=sidx: eng.tensor_copy(
                    out=state[:, sidx:sidx + 1], in_=h_t[:, TC - 1:TC]),
                    after=[t_h], strict_after=[t_h])
                t_hc = pl.emit("vector", lambda eng: eng.tensor_mul(
                    out=b_t[:, :], in0=h_t[:, :], in1=psC[:, 0:TC]),
                    after=[t_h, t_cb], strict_after=[t_h])
                c_read = t_hc
                t_ya = pl.emit("vector", lambda eng, c4=c4: eng.tensor_add(
                    out=y_t[c4][:, :], in0=y_t[c4][:, :], in1=b_t[:, :]),
                    after=[t_hc, t_su] + t_y0, strict_after=[t_hc])
                y_toks.append(t_ya)
            cx.bank_set(psB, b_read)
            cx.bank_set(psC, c_read)
        for c4 in range(4):
            t_yd = pl.emit("vector", lambda eng, c4=c4: eng.scalar_tensor_tensor(
                out=y_t[c4][:, :], in0=xs_t[c4][:, :], scalar=dpc[:, c4:c4 + 1],
                in1=y_t[c4][:, :], op0=ALU.mult, op1=ALU.add),
                after=y_toks[-4:], strict_after=y_toks[-4:])
            tg_all = []
            for sb in range(NSUB):
                t_zl = pl.emit("sync", lambda eng, c4=c4, sb=sb, t0=t0: eng.dma_start(
                    out=zrow[:, :], in_=zsil_d[128 * c4:128 * (c4 + 1),
                                              t0 + 512 * sb:t0 + 512 * (sb + 1)]),
                    after=[t_yd] + tg_all, dma=True, tag="dma4")
                t_g = pl.emit("vector", lambda eng, c4=c4, sb=sb: eng.tensor_mul(
                    out=ygb[:, 512 * sb:512 * (sb + 1)],
                    in0=y_t[c4][:, 512 * sb:512 * (sb + 1)], in1=zrow[:, :]),
                    after=[t_zl, t_yd])
                tg_all.append(t_g)
            t_sy = pl.emit("gpsimd", lambda eng, c4=c4, t0=t0: eng.dma_start(
                out=yg_out[128 * c4:128 * (c4 + 1), t0:t0 + TC], in_=ygb[:, :]),
                after=tg_all, dma=True, tag="dma4")
            store_toks.append(t_sy)
            y_toks.append(t_sy)
        prev = last(y_toks + t_w + t_dt)
    pl.emit("gpsimd", lambda eng: eng.engine_nop(), after=last(store_toks))
    ph.close()
    return cx.finish()


def build_launch_E():
    cx = Ctx("E")
    pl = cx.pl
    Tn = 512

    yg_d = cx.inp("ygq", [2048, QT], BF16)
    x2q = cx.inp("x2q", [D, QT])
    woutT_d = cx.inp("woutT", [2048, D])
    onesc_d = cx.inp("ones_col", [128, 1]); onesr_d = cx.inp("ones_row", [1, 128])

    x3_out = cx.outp("x3", [D, QT])
    xl_out = cx.outp("xl", [D, QT])

    ones_col = cx.sb("onesc", [128, 1]); ones_row = cx.sb("onesr", [1, 128])
    lns = ln_scratch(cx, Tn=512)
    tok_c = [
        pl.emit("sync", lambda eng: eng.dma_start(out=ones_col[:], in_=onesc_d), dma=True),
        pl.emit("sync", lambda eng: eng.dma_start(out=ones_row[:], in_=onesr_d), dma=True),
    ]
    wst = ExitStack()
    wout_sb, tw = load_weight_ph(cx, woutT_d, 2048, D, "wout", wst, wsb_stack=None)
    wst.close()

    psA = [cx.ps("psA", [128, 512], F32) for _ in range(2)]
    ps_ln = [cx.ps("psStat", [64, 512], F32),
             cx.ps("psMu", [128, 512], F32), cx.ps("psRs", [128, 512], F32)]

    ph = ExitStack()
    bar = pl.barrier()
    yg_t = [cx.sb("yg", [128, Tn], BF16, ph, side="right") for _ in range(16)]
    o_f = [cx.sb("of", [128, Tn], F32, ph, side="right") for _ in range(NCH)]
    lnp = [cx.sb("lnp", [128, Tn], F32, ph, side="right") for _ in range(NCH)]
    x2_t = [cx.sb("x2t", [128, Tn], F32, ph, side="right") for _ in range(NCH)]
    x3_t = [cx.sb("x3t", [128, Tn], F32, ph, side="right") for _ in range(NCH)]
    xl_t = [cx.sb("xlt", [128, Tn], F32, ph, side="right") for _ in range(NCH)]

    prev = last(bar + [tw] + tok_c)
    stores = []
    for ti in range(QT // Tn):
        t0 = ti * Tn
        t_y = [pl.emit("sync", lambda eng, k=k, t0=t0: eng.dma_start(
            out=yg_t[k][:, :], in_=yg_d[128 * k:128 * (k + 1), t0:t0 + Tn]),
            after=prev, dma=True, tag="dma4") for k in range(16)]
        t_x2 = [pl.emit("sync", lambda eng, c=c, t0=t0: eng.dma_start(
            out=x2_t[c][:, :], in_=x2q[128 * c:128 * (c + 1), t0:t0 + Tn]),
            after=prev, dma=True, tag="dma4") for c in range(NCH)]
        t_o = simple_linear(cx, wout_sb, 16, D, yg_t, Tn, o_f,
                            after=t_y + [tw], psA=psA)
        t_lnp = emit_ln(cx, o_f, Tn, dst_f32=lnp, after=t_o, psum2=ps_ln,
                        ones_col=ones_col, ones_row=ones_row, scratch=lns)
        t_res = [pl.emit("vector", lambda eng, c=c: eng.tensor_add(
            out=x3_t[c][:, :], in0=x2_t[c][:, :], in1=lnp[c][:, :]),
            after=t_lnp + t_x2) for c in range(NCH)]
        t_xl = emit_ln(cx, x3_t, Tn, dst_f32=xl_t, after=t_res, psum2=ps_ln,
                       ones_col=ones_col, ones_row=ones_row, scratch=lns)
        t_s = [pl.emit("gpsimd", lambda eng, c=c, t0=t0: eng.dma_start(
            out=x3_out[128 * c:128 * (c + 1), t0:t0 + Tn], in_=x3_t[c][:, :]),
            after=t_res[c:c + 1], dma=True, tag="dma4") for c in range(NCH)]
        t_s += [pl.emit("gpsimd", lambda eng, c=c, t0=t0: eng.dma_start(
            out=xl_out[128 * c:128 * (c + 1), t0:t0 + Tn], in_=xl_t[c][:, :]),
            after=t_xl, dma=True, tag="dma4") for c in range(NCH)]
        stores += t_s
        prev = last(t_res + t_xl + t_s)
    pl.emit("gpsimd", lambda eng: eng.engine_nop(), after=last(stores))
    ph.close()
    return cx.finish()


def build_launch_F():
    cx = Ctx("F")
    pl = cx.pl
    Tn = 512

    xq_d = cx.inp("xcur", [D, QT])
    pcol_d = cx.inp("pcol", [128, 8])
    wqkv_d = cx.inp("wqkv", [D, 1536])
    qs_d = cx.inp("qs_c", [128, 1])
    ks_d = cx.inp("ks_c", [128, 1])
    e8_d = cx.inp("e8", [128, 4 * 8])
    g8_d = cx.inp("g8", [8, 4 * 128])
    rot_d = cx.inp("rotm", [128, 128])
    csq_d = cx.inp("csq", [128, QT]); ssq_d = cx.inp("ssq", [128, QT])
    csk_d = cx.inp("csk", [128, QT]); ssk_d = cx.inp("ssk", [128, QT])

    q_out = cx.outp("q", [512, QT], BF16)
    k_out = cx.outp("k", [512, QT], BF16)
    vt_out = cx.outp("vt", [QT, 512], BF16)

    qs_c = cx.sb("qs", [128, 1]); ks_c = cx.sb("ks", [128, 1])
    e8 = cx.sb("e8", [128, 32]); g8 = cx.sb("g8", [8, 512])
    rotm = cx.sb("rotm", [128, 128], F32)
    pcol = cx.sb("pcol", [128, 8], F32)
    tok_c = [
        pl.emit("sync", lambda eng: eng.dma_start(out=qs_c[:], in_=qs_d), dma=True),
        pl.emit("sync", lambda eng: eng.dma_start(out=ks_c[:], in_=ks_d), dma=True),
        pl.emit("sync", lambda eng: eng.dma_start(out=e8[:], in_=e8_d), dma=True),
        pl.emit("sync", lambda eng: eng.dma_start(out=g8[:], in_=g8_d), dma=True),
        pl.emit("sync", lambda eng: eng.dma_start(out=rotm[:], in_=rot_d), dma=True),
        pl.emit("sync", lambda eng: eng.dma_start(
            out=pcol[:, :], in_=pcol_d), dma=True),
    ]
    t_qs8 = pl.emit("vector", lambda eng: eng.tensor_scalar_mul(
        out=qs_c[:, :], in0=qs_c[:, :], scalar1=8.0), after=tok_c)

    wst = ExitStack()
    wqkv_sb, twq = load_weight_ph(cx, wqkv_d, D, 1536, "wqkv", wst, wsb_stack=None)
    wst.close()

    psA = [cx.ps("psA", [128, 512], F32) for _ in range(2)]
    psS = cx.ps("psS", [64, 512], F32)
    psR = cx.ps("psR", [128, 512], F32)
    psRot = cx.ps("psRot", [128, 512], F32)

    ph = ExitStack()
    bar = pl.barrier()
    xw = [cx.sb("xw", [128, Tn + 1], F32, ph, side="right") for _ in range(NCH)]
    hs_bf = [cx.sb("hs", [128, Tn], BF16, ph, side="right") for _ in range(NCH)]
    q_f = [cx.sb("qf", [128, Tn], F32, ph, side="right") for _ in range(4)]
    k_f = [cx.sb("kf", [128, Tn], F32, ph, side="right") for _ in range(4)]
    sq_s = cx.sb("sqs", [128, Tn], F32, ph, side="right")
    rows = cx.sb("rows", [8, Tn], F32, ph, side="right")
    tabs = [cx.sb("tab", [128, Tn], F32, ph, side="right") for _ in range(2)]
    scr = cx.sb("scr", [128, Tn], F32, ph, side="right")
    ob = [cx.sb("obf", [128, Tn], BF16, ph, side="right") for _ in range(4)]
    vtb = cx.sb("vtb", [128, 512], BF16, ph, side="right")

    tabs_readers = [None]

    def l2n_rot(src_f, sc_col, cs_d, ss_d, dst_bf, t_in, t0, out_d):
        toks = []
        t_sq = []
        for c4 in range(4):
            ts = pl.emit("scalar", lambda eng, c4=c4: eng.activation(
                out=sq_s[:, :], in_=src_f[c4][:, :], func=AF.Square),
                after=t_in + ([t_sq[-1][1]] if t_sq else []))
            tm = pl.emit("tensor", lambda eng, c4=c4: eng.matmul(
                psS[0:8, 0:Tn], e8[:, 8 * c4:8 * (c4 + 1)], sq_s[:, :],
                start=(c4 == 0), stop=(c4 == 3)),
                after=[ts] + (cx.bank_wait(psS) if c4 == 0 else []))
            t_sq.append((ts, tm))
        t_nm = pl.emit("vector", lambda eng: eng.tensor_scalar_max(
            out=rows[0:8, :], in0=psS[0:8, 0:Tn], scalar1=1e-24),
            after=[t_sq[-1][1]])
        cx.bank_set(psS, t_nm)
        t_nm2 = pl.emit("scalar", lambda eng: eng.activation(
            out=rows[0:8, :], in_=rows[0:8, :], func=AF.Ln),
            after=[t_nm], strict_after=[t_nm])
        t_nm3 = pl.emit("scalar", lambda eng: eng.activation(
            out=rows[0:8, :], in_=rows[0:8, :], func=AF.Exp, scale=-0.5),
            after=[t_nm2], strict_after=[t_nm2])
        tdep = list(t_in) + ([tabs_readers[0]] if tabs_readers[0] else [])
        t_tab = [pl.emit("sync", lambda eng, d=d, i=i: eng.dma_start(
            out=tabs[i][:, :], in_=d[:, t0:t0 + Tn]), after=tdep, dma=True,
            tag="dma4") for i, d in ((0, cs_d), (1, ss_d))]
        for c4 in range(4):
            t_rb = pl.emit("tensor", lambda eng, c4=c4: eng.matmul(
                psR[:, 0:Tn], g8[0:8, 128 * c4:128 * (c4 + 1)], rows[0:8, :],
                start=True, stop=True),
                after=[t_nm3] + cx.bank_wait(psR))
            t_qn = pl.emit("vector", lambda eng, c4=c4: eng.scalar_tensor_tensor(
                out=src_f[c4][:, :], in0=src_f[c4][:, :], scalar=sc_col[:, 0:1],
                in1=psR[:, 0:Tn], op0=ALU.mult, op1=ALU.mult),
                after=[t_rb, t_qs8] + t_in)
            cx.bank_set(psR, t_qn)
            t_rt = pl.emit("tensor", lambda eng, c4=c4: eng.matmul(
                psRot[:, 0:Tn], rotm[:, :], src_f[c4][:, :],
                start=True, stop=True),
                after=[t_qn] + cx.bank_wait(psRot))
            t_m1 = pl.emit("vector", lambda eng, c4=c4: eng.tensor_mul(
                out=scr[:, :], in0=psRot[:, 0:Tn], in1=tabs[1][:, :]),
                after=[t_rt] + t_tab)
            cx.bank_set(psRot, t_m1)
            t_m2 = pl.emit("vector", lambda eng, c4=c4: eng.tensor_mul(
                out=src_f[c4][:, :], in0=src_f[c4][:, :], in1=tabs[0][:, :]),
                after=[t_m1] + t_tab, strict_after=[t_m1])
            t_m3 = pl.emit("vector", lambda eng, c4=c4: eng.tensor_add(
                out=dst_bf[c4][:, :], in0=src_f[c4][:, :], in1=scr[:, :]),
                after=[t_m2], strict_after=[t_m2])
            t_s = pl.emit("gpsimd", lambda eng, c4=c4: eng.dma_start(
                out=out_d[128 * c4:128 * (c4 + 1), t0:t0 + Tn],
                in_=dst_bf[c4][:, :]), after=[t_m3], dma=True, tag="dma4")
            toks.append(t_s)
            toks.append(t_m3)
            tabs_readers[0] = t_m2
        return toks

    prev = last(bar + [twq, t_qs8])
    stores = []
    for ti in range(QT // Tn):
        t0 = ti * Tn
        t_x = []
        for c in range(NCH):
            if ti == 0:
                t1 = pl.emit("sync", lambda eng, c=c: eng.dma_start(
                    out=xw[c][:, 1:Tn + 1], in_=xq_d[128 * c:128 * (c + 1), 0:Tn]),
                    after=prev, dma=True, tag="dma4")
                t2 = pl.emit("vector", lambda eng, c=c: eng.tensor_copy(
                    out=xw[c][:, 0:1], in_=pcol[:, c:c + 1]),
                    after=prev + tok_c)
                t_x += [t1, t2]
            else:
                t_x.append(pl.emit("sync", lambda eng, c=c, t0=t0: eng.dma_start(
                    out=xw[c][:, 0:Tn + 1],
                    in_=xq_d[128 * c:128 * (c + 1), t0 - 1:t0 + Tn]),
                    after=prev, dma=True, tag="dma4"))
        t_hs = []
        for c in range(NCH):
            off = 1 if c < 4 else 0
            t_hs.append(pl.emit("gpsimd", lambda eng, c=c, off=off: eng.tensor_copy(
                out=hs_bf[c][:, :], in_=xw[c][:, off:off + Tn]),
                after=t_x))
        def qk_write(dst):
            def w(of, o0, osz, nt, tsz, ps, pe_tok):
                return pl.emit("vector", lambda eng, of=of: eng.tensor_copy(
                    out=dst[of][:, :], in_=ps[:, 0:Tn]), after=[pe_tok])
            return w
        t_q = []
        for of in range(4):
            ps = psA[of % 2]
            def mmq(eng, of=of, ps=ps):
                return [eng.matmul(ps[:, 0:Tn],
                                   wqkv_sb[:, k * 1536 + 128 * of:k * 1536 + 128 * of + 128],
                                   hs_bf[k][:, :], start=(k == 0), stop=(k == NCH - 1))
                        for k in range(NCH)][-1]
            pe = pl.emit("tensor", mmq, after=last(t_hs + [twq]) + cx.bank_wait(ps))
            ev = pl.emit("vector", lambda eng, of=of, ps=ps: eng.tensor_copy(
                out=q_f[of][:, :], in_=ps[:, 0:Tn]), after=[pe])
            cx.bank_set(ps, ev)
            t_q.append(ev)
        t_k = []
        for of in range(4):
            ps = psA[of % 2]
            def mmk(eng, of=of, ps=ps):
                return [eng.matmul(ps[:, 0:Tn],
                                   wqkv_sb[:, k * 1536 + 512 + 128 * of:k * 1536 + 512 + 128 * of + 128],
                                   hs_bf[k][:, :], start=(k == 0), stop=(k == NCH - 1))
                        for k in range(NCH)][-1]
            pe = pl.emit("tensor", mmk, after=last(t_hs + [twq]) + cx.bank_wait(ps))
            ev = pl.emit("vector", lambda eng, of=of, ps=ps: eng.tensor_copy(
                out=k_f[of][:, :], in_=ps[:, 0:Tn]), after=[pe])
            cx.bank_set(ps, ev)
            t_k.append(ev)
        t_v = []
        for tt in range(Tn // 128):
            ps = psA[tt % 2]
            def mmv(eng, tt=tt, ps=ps):
                return [eng.matmul(ps[:, 0:512],
                                   hs_bf[k][:, 128 * tt:128 * (tt + 1)],
                                   wqkv_sb[:, k * 1536 + 1024:k * 1536 + 1536],
                                   start=(k == 0), stop=(k == NCH - 1))
                        for k in range(NCH)][-1]
            pe = pl.emit("tensor", mmv, after=last(t_hs + [twq]) + cx.bank_wait(ps))
            ev = pl.emit("vector", lambda eng, ps=ps: eng.tensor_copy(
                out=vtb[:, :], in_=ps[:, 0:512]),
                after=[pe] + (t_v[-1:] if t_v else []))
            cx.bank_set(ps, ev)
            t_sv = pl.emit("gpsimd", lambda eng, tt=tt, t0=t0: eng.dma_start(
                out=vt_out[t0 + 128 * tt:t0 + 128 * (tt + 1), :], in_=vtb[:, :]),
                after=[ev], dma=True, tag="dma4")
            cx.bank_set(ps, t_sv)
            t_v.append(t_sv)
        t_lq = l2n_rot(q_f, qs_c, csq_d, ssq_d, ob, t_q, t0, q_out)
        t_lk = l2n_rot(k_f, ks_c, csk_d, ssk_d, ob, t_k, t0, k_out)
        stores += t_v + t_lq + t_lk
        prev = last(t_v + t_lq + t_lk + t_hs)
    pl.emit("gpsimd", lambda eng: eng.engine_nop(), after=last(stores))
    ph.close()
    return cx.finish()


def build_launch_G():
    cx = Ctx("G")
    pl = cx.pl
    Tn = 512
    NW = QT // 512

    q_d = cx.inp("q", [512, QT], BF16)
    k_d = cx.inp("k", [512, QT], BF16)
    vt_d = cx.inp("vt", [QT, 512], BF16)
    kprev_d = cx.inp("kprev", [512, 512], BF16)
    vtprev_d = cx.inp("vtprev", [512, 512], BF16)
    mfirst_d = cx.inp("mask_first", [512, 1024])
    mrest_d = cx.inp("mask_rest", [512, 1024])
    xc_d = cx.inp("xcur", [D, QT])
    wo_d = cx.inp("wo", [512, D])
    ident_d = cx.inp("ident_bf", [128, 128], BF16)

    x4_out = cx.outp("x4", [D, QT])

    ident_bf = cx.sb("id", [128, 128], BF16)
    tok_c = [pl.emit("sync", lambda eng: eng.dma_start(out=ident_bf[:], in_=ident_d), dma=True)]
    wst = ExitStack()
    wo_sb, two = load_weight_ph(cx, wo_d, 512, D, "wo", wst, wsb_stack=None)
    wst.close()

    kq = [cx.sb("kq", [128, 512 + QT], BF16) for _ in range(4)]
    vt = [cx.sb("vt", [128, 512], BF16) for _ in range(4 + QT // 128)]
    q_sb = [cx.sb("q", [128, Tn], BF16) for _ in range(4)]
    masks = [cx.sb("mk", [128, 1024], F32) for _ in range(8)]
    sm_s = cx.sb("sms", [128, 1024], F32)
    sm_m = cx.sb("smm", [128, 1], F32)
    sm_r = cx.sb("smr", [128, 1], F32)
    sm_p = cx.sb("smp", [128, 1024], BF16)
    sm_pt = cx.sb("smpt", [128, 1024], BF16)
    o_bf = [cx.sb("ob", [128, Tn], BF16) for _ in range(4)]
    o_f = [cx.sb("of", [128, Tn], F32) for _ in range(NCH)]
    xc_t = [cx.sb("xct", [128, Tn], F32) for _ in range(NCH)]
    x4_t = [cx.sb("x4t", [128, Tn], F32) for _ in range(NCH)]

    psA = [cx.ps("psA", [128, 512], F32) for _ in range(2)]
    psS = cx.ps("psS", [128, 1024], F32)
    psT = cx.ps("psT", [128, 1024], BF16)
    psV = psA[0]

    t_kl = [pl.emit("sync", lambda eng, c4=c4: eng.dma_start(
        out=kq[c4][:, 0:512], in_=kprev_d[128 * c4:128 * (c4 + 1), :]),
        dma=True) for c4 in range(4)]
    t_kl += [pl.emit("sync", lambda eng, c4=c4: eng.dma_start(
        out=kq[c4][:, 512:512 + QT], in_=k_d[128 * c4:128 * (c4 + 1), :]),
        dma=True) for c4 in range(4)]
    t_kl += [pl.emit("sync", lambda eng, r=r: eng.dma_start(
        out=vt[r][:, :], in_=vtprev_d[128 * r:128 * (r + 1), :]),
        dma=True) for r in range(4)]
    t_kl += [pl.emit("sync", lambda eng, r=r: eng.dma_start(
        out=vt[4 + r][:, :], in_=vt_d[128 * r:128 * (r + 1), :]),
        dma=True) for r in range(QT // 128)]
    t_kl += [pl.emit("sync", lambda eng, qs=qs: eng.dma_start(
        out=masks[qs][:, :], in_=mfirst_d[128 * qs:128 * (qs + 1), :]),
        dma=True) for qs in range(4)]
    t_kl += [pl.emit("sync", lambda eng, qs=qs: eng.dma_start(
        out=masks[4 + qs][:, :], in_=mrest_d[128 * qs:128 * (qs + 1), :]),
        dma=True) for qs in range(4)]

    def attn_unit(w, head, qs, dep):
        hc, hb = head // 2, (head % 2) * 64
        def mm_s(eng):
            return [eng.matmul(
                psS[:, 512 * half:512 * (half + 1)],
                q_sb[hc][hb:hb + 64, 128 * qs:128 * (qs + 1)],
                kq[hc][hb:hb + 64, w * 512 + 512 * half:w * 512 + 512 * (half + 1)],
                start=True, stop=True) for half in range(2)][-1]
        t_s = pl.emit("tensor", mm_s, after=dep, tag="attn")
        mk = masks[qs] if w == 0 else masks[4 + qs]
        t_ms = pl.emit("vector", lambda eng: eng.tensor_add(
            out=sm_s[:, :], in0=psS[:, 0:1024], in1=mk[:, :]),
            after=[t_s], tag="attn")
        t_m = pl.emit("vector", lambda eng: eng.reduce_max(
            out=sm_m[:, 0:1], in_=sm_s[:, :], axis=AX.X, negate=True),
            after=[t_ms], strict_after=[t_ms], tag="attn")
        t_p = pl.emit("scalar", lambda eng: eng.activation(
            out=sm_p[:, :], in_=sm_s[:, :], func=AF.Exp, bias=sm_m[:, 0:1],
            accum_out=sm_r[:, 0:1]), after=[t_m], tag="attn")
        t_rc = pl.emit("vector", lambda eng: eng.reciprocal(
            out=sm_r[:, 0:1], in_=sm_r[:, 0:1]), after=[t_p], tag="attn")
        t_pn = pl.emit("vector", lambda eng: eng.tensor_scalar_mul(
            out=sm_p[:, :], in0=sm_p[:, :], scalar1=sm_r[:, 0:1]),
            after=[t_rc], strict_after=[t_rc], tag="attn")
        def mm_t(eng):
            return [eng.transpose(psT[:, 128 * kc:128 * (kc + 1)],
                                  sm_p[:, 128 * kc:128 * (kc + 1)], ident_bf[:, :])
                    for kc in range(8)][-1]
        t_tr = pl.emit("tensor", mm_t, after=[t_pn], tag="attn")
        t_te = pl.emit("vector", lambda eng: eng.tensor_copy(
            out=sm_pt[:, :], in_=psT[:, 0:1024]), after=[t_tr], tag="attn")
        def mm_av(eng):
            return [eng.matmul(psV[0:64, 0:128],
                               vt[w * 4 + kc][:, head * 64:head * 64 + 64],
                               sm_pt[:, 128 * kc:128 * (kc + 1)],
                               start=(kc == 0), stop=(kc == 7))
                    for kc in range(8)][-1]
        t_av = pl.emit("tensor", mm_av, after=[t_te] + cx.bank_wait(psV), tag="attn")
        t_ev = pl.emit("scalar", lambda eng: eng.copy(
            out=o_bf[hc][hb:hb + 64, 128 * qs:128 * (qs + 1)],
            in_=psV[0:64, 0:128]), after=[t_av], tag="attn")
        cx.bank_set(psV, t_ev)
        return t_ev

    prev = last(t_kl + [two] + tok_c)
    stores = []
    for w in range(NW):
        t0 = w * Tn
        t_q = [pl.emit("sync", lambda eng, c4=c4, t0=t0: eng.dma_start(
            out=q_sb[c4][:, :], in_=q_d[128 * c4:128 * (c4 + 1), t0:t0 + Tn]),
            after=prev, dma=True, tag="dma4") for c4 in range(4)]
        t_xc = [pl.emit("sync", lambda eng, c=c, t0=t0: eng.dma_start(
            out=xc_t[c][:, :], in_=xc_d[128 * c:128 * (c + 1), t0:t0 + Tn]),
            after=prev, dma=True, tag="dma4") for c in range(NCH)]
        aw = last(t_q + t_kl)
        for head in range(8):
            for qs in range(4):
                t = attn_unit(w, head, qs, aw)
                aw = [t]
        t_o = simple_linear(cx, wo_sb, 4, D, o_bf, Tn, o_f,
                            after=aw + [two], psA=psA)
        t_res = [pl.emit("vector", lambda eng, c=c: eng.tensor_add(
            out=x4_t[c][:, :], in0=xc_t[c][:, :], in1=o_f[c][:, :]),
            after=t_o + t_xc) for c in range(NCH)]
        t_s = [pl.emit("gpsimd", lambda eng, c=c, t0=t0: eng.dma_start(
            out=x4_out[128 * c:128 * (c + 1), t0:t0 + Tn], in_=x4_t[c][:, :]),
            after=t_res[c:c + 1], dma=True, tag="dma4") for c in range(NCH)]
        stores += t_s
        prev = last(t_res + t_s)
    pl.emit("gpsimd", lambda eng: eng.engine_nop(), after=last(stores))
    return cx.finish()


def build_launch_H():
    cx = Ctx("H")
    pl = cx.pl
    Tn = 512
    NOF = 44
    NAG = 22

    xc_d = cx.inp("xcur", [D, QT])
    pcol_d = cx.inp("pcol", [128, 8])
    w1_d = cx.inp("w1re", [D, 5632])
    b1_d = cx.inp("b1_c", [128, NOF])
    w2_d = cx.inp("w2re", [2816, D])
    b2_d = cx.inp("b2_c", [128, 8])
    lg_d = cx.inp("lg_c", [128, 8])
    lb_d = cx.inp("lb_c", [128, 8])
    onesc_d = cx.inp("ones_col", [128, 1]); onesr_d = cx.inp("ones_row", [1, 128])

    x5_out = cx.outp("x5", [D, QT])

    ones_col = cx.sb("onesc", [128, 1]); ones_row = cx.sb("onesr", [1, 128])
    b1c = cx.sb("b1c", [128, NOF]); b2c = cx.sb("b2c", [128, 8])
    lgc = cx.sb("lgc", [128, 8]); lbc = cx.sb("lbc", [128, 8])
    pcol = cx.sb("pcol", [128, 8], F32)
    lns = ln_scratch(cx, Tn=512)
    tok_c = [
        pl.emit("sync", lambda eng: eng.dma_start(out=ones_col[:], in_=onesc_d), dma=True),
        pl.emit("sync", lambda eng: eng.dma_start(out=ones_row[:], in_=onesr_d), dma=True),
        pl.emit("sync", lambda eng: eng.dma_start(out=b1c[:], in_=b1_d), dma=True),
        pl.emit("sync", lambda eng: eng.dma_start(out=b2c[:], in_=b2_d), dma=True),
        pl.emit("sync", lambda eng: eng.dma_start(out=lgc[:], in_=lg_d), dma=True),
        pl.emit("sync", lambda eng: eng.dma_start(out=lbc[:], in_=lb_d), dma=True),
        pl.emit("sync", lambda eng: eng.dma_start(
            out=pcol[:, :], in_=pcol_d), dma=True),
    ]
    w1bf = cx.nc.dram_tensor("w1bf", [D, 5632], BF16, kind="Internal").ap()
    wst = ExitStack()
    stg = [cx.sb("w1st", [128, 5632], F32, wst, side="right") for _ in range(2)]
    stgb = [cx.sb("w1sb", [128, 5632], BF16, wst, side="right") for _ in range(2)]
    t_wc = []
    prevk = [None, None]
    for k in range(NCH):
        sg = stg[k % 2]; sb = stgb[k % 2]
        dep = [prevk[k % 2]] if prevk[k % 2] else []
        t1 = pl.emit("sync", lambda eng, k=k, sg=sg: eng.dma_start(
            out=sg[:, :], in_=w1_d[128 * k:128 * (k + 1), :]), after=dep, dma=True)
        t2 = pl.emit("gpsimd", lambda eng, sg=sg, sb=sb: eng.tensor_copy(
            out=sb[:, :], in_=sg[:, :]), after=[t1])
        t3 = pl.emit("gpsimd", lambda eng, k=k, sb=sb: eng.dma_start(
            out=w1bf[128 * k:128 * (k + 1), :], in_=sb[:, :]), after=[t2], dma=True)
        prevk[k % 2] = t3
        t_wc.append(t3)
    wst.close()
    wst2 = ExitStack()
    w2_sb, tw2 = load_weight_ph(cx, w2_d, 2816, D, "w2re", wst2, wsb_stack=None,
                                dep=last(t_wc))
    wst2.close()

    psA = [cx.ps("psA", [128, 512], F32) for _ in range(2)]
    ps_ln = [cx.ps("psStat", [64, 512], F32),
             cx.ps("psMu", [128, 512], F32), cx.ps("psRs", [128, 512], F32)]

    ph = ExitStack()
    bar = pl.barrier()
    xw = [cx.sb("xw", [128, Tn + 1], F32, ph, side="right") for _ in range(NCH)]
    hs_f = [cx.sb("hsf", [128, Tn], F32, ph, side="right") for _ in range(NCH)]
    ln_bf = [cx.sb("lnb", [128, Tn], BF16, ph, side="right") for _ in range(NCH)]
    w1c = [cx.sb("w1c", [128, NCH * 128], BF16, ph, side="right") for _ in range(2)]
    a_bf = [cx.sb("abf", [128, Tn], BF16, ph, side="right") for _ in range(NAG)]
    g_s = cx.sb("gs", [128, Tn], F32, ph, side="right")
    agg = [cx.sb("agg", [128, Tn], BF16, ph, side="right") for _ in range(NAG)]
    x5_t = [cx.sb("x5t", [128, Tn], F32, ph, side="right") for _ in range(NCH)]

    prev = last(bar + [tw2] + tok_c + last(t_wc))
    stores = []
    for ti in range(QT // Tn):
        t0 = ti * Tn
        t_x = []
        for c in range(NCH):
            if ti == 0:
                t1 = pl.emit("sync", lambda eng, c=c: eng.dma_start(
                    out=xw[c][:, 1:Tn + 1], in_=xc_d[128 * c:128 * (c + 1), 0:Tn]),
                    after=prev, dma=True, tag="dma4")
                t2 = pl.emit("vector", lambda eng, c=c: eng.tensor_copy(
                    out=xw[c][:, 0:1], in_=pcol[:, c:c + 1]),
                    after=prev + tok_c)
                t_x += [t1, t2]
            else:
                t_x.append(pl.emit("sync", lambda eng, c=c, t0=t0: eng.dma_start(
                    out=xw[c][:, 0:Tn + 1],
                    in_=xc_d[128 * c:128 * (c + 1), t0 - 1:t0 + Tn]),
                    after=prev, dma=True, tag="dma4"))
        t_hs = []
        for c in range(NCH):
            off = 1 if c < 4 else 0
            t_hs.append(pl.emit("vector", lambda eng, c=c, off=off: eng.tensor_copy(
                out=hs_f[c][:, :], in_=xw[c][:, off:off + Tn]),
                after=t_x))
        t_ln = emit_ln(cx, hs_f, Tn, dst_bf=ln_bf, after=t_hs + tok_c,
                       psum2=ps_ln, ones_col=ones_col, ones_row=ones_row,
                       scratch=lns, gain_col=lgc, bias_col=lbc)
        wtoks = [None, None]
        t_ag = []
        for of in range(NOF):
            wc = w1c[of % 2]
            dep = last(t_ln) + ([wtoks[of % 2]] if wtoks[of % 2] else [])
            t_wl = [pl.emit("sync", lambda eng, k=k, of=of, wc=wc: eng.dma_start(
                out=wc[:, 128 * k:128 * (k + 1)],
                in_=w1bf[128 * k:128 * (k + 1), 128 * of:128 * (of + 1)]),
                after=dep + last(t_wc), dma=True, tag="dma4") for k in range(NCH)]
            ps = psA[of % 2]
            def mm1(eng, wc=wc, ps=ps):
                return [eng.matmul(ps[:, 0:Tn], wc[:, 128 * k:128 * (k + 1)],
                                   ln_bf[k][:, :], start=(k == 0), stop=(k == NCH - 1))
                        for k in range(NCH)][-1]
            pe = pl.emit("tensor", mm1,
                         after=last(t_wl + last(t_ln)) + cx.bank_wait(ps))
            wtoks[of % 2] = pe
            if of < NAG:
                ev = pl.emit("vector", lambda eng, of=of, ps=ps: eng.tensor_scalar_add(
                    out=a_bf[of][:, :], in0=ps[:, 0:Tn],
                    scalar1=b1c[:, of:of + 1]), after=[pe])
                cx.bank_set(ps, ev)
            else:
                og = of - NAG
                ev = pl.emit("scalar", lambda eng, of=of, ps=ps: eng.activation(
                    out=g_s[:, :], in_=ps[:, 0:Tn], func=AF.Gelu,
                    bias=b1c[:, of:of + 1]), after=[pe])
                cx.bank_set(ps, ev)
                ev = pl.emit("vector", lambda eng, og=og: eng.tensor_mul(
                    out=agg[og][:, :], in0=a_bf[og][:, :], in1=g_s[:, :]),
                    after=[ev], strict_after=[ev])
                t_ag.append(ev)
        def h2_write(of, o0, osz, nt, tsz, ps, pe_tok):
            return pl.emit("vector", lambda eng, of=of, ps=ps: eng.scalar_tensor_tensor(
                out=x5_t[of][:, :], in0=ps[:, 0:Tn], scalar=b2c[:, of:of + 1],
                in1=xw[of][:, 1:Tn + 1], op0=ALU.add, op1=ALU.add),
                after=[pe_tok] + t_x)
        t_h2 = emit_linear(cx, w2_sb, NAG, D, agg, Tn, h2_write,
                           after=t_ag + [tw2], psA=psA)
        t_s = [pl.emit("gpsimd", lambda eng, c=c, t0=t0: eng.dma_start(
            out=x5_out[128 * c:128 * (c + 1), t0:t0 + Tn], in_=x5_t[c][:, :]),
            after=t_h2, dma=True, tag="dma4") for c in range(NCH)]
        stores += t_s
        prev = last(t_h2 + t_s + t_ag)
    pl.emit("gpsimd", lambda eng: eng.engine_nop(), after=last(stores))
    ph.close()
    return cx.finish()


def build_launch_Z():
    cx = Ctx("Z")
    pl = cx.pl
    Tn = 512
    x3_d = cx.inp("x3q", [D, QT])
    cur_d = cx.inp("curq", [D, QT])
    onesc_d = cx.inp("ones_col", [128, 1]); onesr_d = cx.inp("ones_row", [1, 128])
    out_d = cx.outp("out", [D, QT])

    ones_col = cx.sb("onesc", [128, 1]); ones_row = cx.sb("onesr", [1, 128])
    lns = ln_scratch(cx, Tn=512)
    tok_c = [
        pl.emit("sync", lambda eng: eng.dma_start(out=ones_col[:], in_=onesc_d), dma=True),
        pl.emit("sync", lambda eng: eng.dma_start(out=ones_row[:], in_=onesr_d), dma=True),
    ]
    ps_ln = [cx.ps("psStat", [64, 512], F32),
             cx.ps("psMu", [128, 512], F32), cx.ps("psRs", [128, 512], F32)]
    ph = ExitStack()
    cur_t = [cx.sb("cur", [128, Tn], F32, ph, side="right") for _ in range(NCH)]
    x3_t = [cx.sb("x3", [128, Tn], F32, ph, side="right") for _ in range(NCH)]
    lnp = [cx.sb("lnp", [128, Tn], F32, ph, side="right") for _ in range(NCH)]
    o_t = [cx.sb("o", [128, Tn], F32, ph, side="right") for _ in range(NCH)]
    prev = list(tok_c)
    stores = []
    for ti in range(QT // Tn):
        t0 = ti * Tn
        t_c = [pl.emit("sync", lambda eng, c=c, t0=t0: eng.dma_start(
            out=cur_t[c][:, :], in_=cur_d[128 * c:128 * (c + 1), t0:t0 + Tn]),
            after=prev, dma=True) for c in range(NCH)]
        t_3 = [pl.emit("sync", lambda eng, c=c, t0=t0: eng.dma_start(
            out=x3_t[c][:, :], in_=x3_d[128 * c:128 * (c + 1), t0:t0 + Tn]),
            after=prev, dma=True) for c in range(NCH)]
        t_ln = emit_ln(cx, cur_t, Tn, dst_f32=lnp, after=t_c + tok_c,
                       psum2=ps_ln, ones_col=ones_col, ones_row=ones_row,
                       scratch=lns)
        t_r = [pl.emit("vector", lambda eng, c=c: eng.tensor_add(
            out=o_t[c][:, :], in0=x3_t[c][:, :], in1=lnp[c][:, :]),
            after=t_ln + t_3) for c in range(NCH)]
        t_s = [pl.emit("gpsimd", lambda eng, c=c, t0=t0: eng.dma_start(
            out=out_d[128 * c:128 * (c + 1), t0:t0 + Tn], in_=o_t[c][:, :]),
            after=t_r[c:c + 1], dma=True) for c in range(NCH)]
        stores += t_s
        prev = last(t_r + t_s)
    pl.emit("gpsimd", lambda eng: eng.engine_nop(), after=last(stores))
    ph.close()
    return cx.finish()


import ml_dtypes

_nc_cache = {}


def _get_nc(name):
    if name not in _nc_cache:
        _nc_cache[name] = globals()[f"build_launch_{name}"]()
    return _nc_cache[name]


def _run(name, in_maps):
    nc = _get_nc(name)
    return run_bass_kernel_spmd(nc, in_maps, core_ids=list(range(8))).results


def _mask_causal():
    i = np.arange(1024)[:, None]
    j = np.arange(1024)[None, :]
    return np.where(j <= i, 0.0, NEG).astype(np.float32)


def _col(v, n128):
    v = np.asarray(v, np.float32)
    return np.ascontiguousarray(v.reshape(n128, 128).T)


def _ones_consts():
    return {"ones_col": np.ones((128, 1), np.float32),
            "ones_row": np.ones((1, 128), np.float32)}


def run_A(inputs, x):
    mask = _mask_causal()
    ident_bf = np.eye(128, dtype=np.float32).astype(ml_dtypes.bfloat16)
    in_maps = []
    for c in range(8):
        b, j = c // 4, c % 4
        flag = np.zeros((128, 2), np.float32)
        flag[:, 0] = 1.0 if j == 0 else 0.0
        flag[:, 1] = 1.0 - flag[:, 0]
        in_maps.append({
            "xq": np.ascontiguousarray(x[b, 4096 * j:4096 * (j + 1), :].T),
            "xinit": np.ascontiguousarray(x[b, :1024, :].T),
            "Wq": inputs["L0_attn_Wq"], "Wk": inputs["L0_attn_Wk"],
            "Wv": inputs["L0_attn_Wv"], "Wo": inputs["L0_attn_Wo"],
            "W1": inputs["L0_attn_W1"], "W2": inputs["L0_attn_W2"],
            "b1c": _col(inputs["L0_attn_b1"], 4),
            "b2c": _col(inputs["L0_attn_b2"], 8),
            "causal_mask": mask, "flag": flag, "ident_bf": ident_bf,
            **_ones_consts(),
        })
    res = _run("A", in_maps)
    return ([res[c]["x1"] for c in range(8)],
            [res[c]["gpart"] for c in range(8)])


def run_B(inputs, x1_fm, gparts):
    in_maps = []
    for c in range(8):
        b = c // 4
        gp_all = np.concatenate([gparts[b * 4 + i] for i in range(4)], axis=1)
        in_maps.append({
            "x1q": x1_fm[c], "gp_all": np.ascontiguousarray(gp_all),
            "cn_w1": inputs["L0_ffn_cn_w1"], "cn_w2": inputs["L0_ffn_cn_w2"],
            "cn_w3": inputs["L0_ffn_cn_w3"], "cn_w4": inputs["L0_ffn_cn_w4"],
            "cn_b1c": _col(inputs["L0_ffn_cn_b1"], 8),
            "cn_b2c": _col(inputs["L0_ffn_cn_b2"], 4),
            "cn_b3c": _col(inputs["L0_ffn_cn_b3"], 2),
            "cn_b4c": np.asarray(inputs["L0_ffn_cn_b4"], np.float32).reshape(1, 1),
            "gc_w1": inputs["L0_ffn_gc_w1"], "gc_w2": inputs["L0_ffn_gc_w2"],
            "gc_w3": inputs["L0_ffn_gc_w3"],
            "gc_b1c": _col(inputs["L0_ffn_gc_b1"], 4),
            "gc_b2c": _col(inputs["L0_ffn_gc_b2"], 2),
            "gc_b3c": np.asarray(inputs["L0_ffn_gc_b3"], np.float32).reshape(1, 1),
            "ar": np.asarray(inputs["L0_ffn_adaptive_range"], np.float32).reshape(1, 1),
            "neg_row": -np.ones((1, 128), np.float32),
            **_ones_consts(),
        })
    res = _run("B", in_maps)
    return ([res[c]["x2"] for c in range(8)], [res[c]["ln3"] for c in range(8)])


def run_C(inputs, ln3_fm):
    W_in = np.asarray(inputs["L1_attn_W_in"], np.float32)
    W_x = np.asarray(inputs["L1_attn_W_x"], np.float32)
    conv_w = np.asarray(inputs["L1_attn_conv_w"], np.float32)
    conv_b = np.asarray(inputs["L1_attn_conv_b"], np.float32)
    in_maps = []
    for c in range(8):
        b, j = c // 4, c % 4
        ln3f = np.concatenate([ln3_fm[b * 4 + i] for i in range(4)], axis=1)
        cw = conv_w[512 * j:512 * (j + 1)]
        in_maps.append({
            "ln3f": np.ascontiguousarray(ln3f),
            "winxT": np.ascontiguousarray(W_in[512 * j:512 * (j + 1), :].T),
            "winzT": np.ascontiguousarray(W_in[2048 + 512 * j:2048 + 512 * (j + 1), :].T),
            "convw_c": np.ascontiguousarray(
                cw.reshape(4, 128, 4).transpose(1, 0, 2).reshape(128, 16)),
            "convb_c": _col(conv_b[512 * j:512 * (j + 1)], 4),
            "wxT": np.ascontiguousarray(W_x[:, 512 * j:512 * (j + 1)].T),
        })
    res = _run("C", in_maps)
    return ([res[c]["xsil"] for c in range(8)],
            [res[c]["zsil"] for c in range(8)],
            [res[c]["xp"] for c in range(8)])


def run_D(inputs, xsil, zsil, xp):
    W_dt = np.asarray(inputs["L1_attn_W_dt"], np.float32)
    b_dt = np.asarray(inputs["L1_attn_b_dt"], np.float32)
    A_log = np.asarray(inputs["L1_attn_A_log"], np.float32)
    Dp = np.asarray(inputs["L1_attn_Dp"], np.float32)
    sel16 = np.zeros((16, 2048), np.float32)
    for n in range(16):
        sel16[n, 128 * n:128 * (n + 1)] = 1.0
    in_maps = []
    for c in range(8):
        b, j = c // 4, c % 4
        sl = slice(512 * j, 512 * (j + 1))
        im = {
            "xsil": xsil[c], "zsil": zsil[c],
            "wdtT": np.ascontiguousarray(W_dt[sl, :].T),
            "bdt_c": _col(b_dt[sl], 4),
            "alog_c": np.ascontiguousarray(
                A_log[sl].reshape(4, 128, 16).transpose(1, 0, 2).reshape(128, 64)),
            "dp_c": _col(Dp[sl], 4),
            "sel16": sel16,
            **_ones_consts(),
        }
        for i in range(4):
            im[f"xp{i}"] = xp[b * 4 + i]
        in_maps.append(im)
    res = _run("D", in_maps)
    return [res[c]["ygated"] for c in range(8)]


def run_E(inputs, yg, x2_fm):
    W_out = np.asarray(inputs["L1_attn_W_out"], np.float32)
    woutT = np.ascontiguousarray(W_out.T)
    in_maps = []
    for c in range(8):
        b, j = c // 4, c % 4
        ygf = np.concatenate([yg[b * 4 + i] for i in range(4)], axis=0)
        in_maps.append({
            "ygq": np.ascontiguousarray(ygf[:, 4096 * j:4096 * (j + 1)]),
            "x2q": x2_fm[c], "woutT": woutT,
            **_ones_consts(),
        })
    res = _run("E", in_maps)
    return ([res[c]["x3"] for c in range(8)], [res[c]["xl"] for c in range(8)])


def _rot_tables(j):
    DIM_HEAD, WINDOW, n = 64, 512, S
    half_idx = np.arange(0, DIM_HEAD, 2, dtype=np.float64)
    inv_freq = 1.0 / 10000.0 ** (half_idx / DIM_HEAD)
    t = np.arange(4096 * j, 4096 * (j + 1), dtype=np.float64)
    freqs = t[:, None] * inv_freq[None, :]
    freqs = np.concatenate([freqs, freqs], axis=-1)
    base = (half_idx + 0.4 * DIM_HEAD) / (1.4 * DIM_HEAD)
    power = (t - n // 2) / float(WINDOW // 2)
    scale = base[None, :] ** power[:, None]
    scale = np.concatenate([scale, scale], axis=-1)
    cos, sin = np.cos(freqs), np.sin(freqs)
    csq = (cos * scale).T; ssq = (sin * scale).T
    csk = (cos / scale).T; ssk = (sin / scale).T
    out = []
    for tab in (csq, ssq, csk, ssk):
        out.append(np.ascontiguousarray(
            np.concatenate([tab, tab], axis=0).astype(np.float32)))
    return out


_e8 = None
_g8 = None
_rotm = None


def _f_consts():
    global _e8, _g8, _rotm
    if _e8 is None:
        e8 = np.zeros((128, 32), np.float32)
        g8 = np.zeros((8, 512), np.float32)
        for c4 in range(4):
            for p in range(128):
                h = 2 * c4 + (p // 64)
                e8[p, 8 * c4 + h] = 1.0
                g8[h, 128 * c4 + p] = 1.0
        rotm = np.zeros((128, 128), np.float32)
        for beta in (0, 64):
            for i in range(32):
                rotm[beta + 32 + i, beta + i] = -1.0
                rotm[beta + i, beta + 32 + i] = 1.0
        _e8, _g8, _rotm = e8, g8, rotm
    return _e8, _g8, _rotm


def run_F(inputs, l, cur_fm):
    e8, g8, rotm = _f_consts()
    qs = np.tile(np.asarray(inputs["L1_ffn_q_scale"], np.float32)[l], 2).reshape(128, 1)
    ks = np.tile(np.asarray(inputs["L1_ffn_k_scale"], np.float32)[l], 2).reshape(128, 1)
    wqkv = np.ascontiguousarray(np.asarray(inputs["L1_ffn_wqkv"], np.float32)[l])
    in_maps = []
    for c in range(8):
        b, j = c // 4, c % 4
        if j == 0:
            pcol = np.zeros((128, 8), np.float32)
        else:
            pcol = np.ascontiguousarray(cur_fm[c - 1][:, -1].reshape(8, 128).T)
        csq, ssq, csk, ssk = _rot_tables(j)
        in_maps.append({
            "xcur": cur_fm[c], "pcol": pcol, "wqkv": wqkv,
            "qs_c": qs, "ks_c": ks, "e8": e8, "g8": g8, "rotm": rotm,
            "csq": csq, "ssq": ssq, "csk": csk, "ssk": ssk,
        })
    res = _run("F", in_maps)
    return ([res[c]["q"] for c in range(8)], [res[c]["k"] for c in range(8)],
            [res[c]["vt"] for c in range(8)])


_g_masks = None


def _g_mask_consts():
    global _g_masks
    if _g_masks is None:
        qi = np.arange(512)[:, None]
        kj = np.arange(512)[None, :]
        left_rest = np.where(qi > kj, NEG, 0.0).astype(np.float32)
        right = np.where(qi < kj, NEG, 0.0).astype(np.float32)
        mrest = np.concatenate([left_rest, right], axis=1)
        mfirst = np.concatenate([np.full((512, 512), NEG, np.float32), right], axis=1)
        _g_masks = (mfirst, mrest)
    return _g_masks


def run_G(inputs, l, cur_fm, q, k, vt):
    mfirst, mrest = _g_mask_consts()
    ident_bf = np.eye(128, dtype=np.float32).astype(ml_dtypes.bfloat16)
    wo = np.ascontiguousarray(np.asarray(inputs["L1_ffn_wo"], np.float32)[l])
    zk = np.zeros((512, 512), ml_dtypes.bfloat16)
    zv = np.zeros((512, 512), ml_dtypes.bfloat16)
    in_maps = []
    for c in range(8):
        b, j = c // 4, c % 4
        kprev = zk if j == 0 else np.ascontiguousarray(k[c - 1][:, 3584:4096])
        vtprev = zv if j == 0 else np.ascontiguousarray(vt[c - 1][3584:4096, :])
        in_maps.append({
            "q": q[c], "k": k[c], "vt": vt[c],
            "kprev": kprev, "vtprev": vtprev,
            "mask_first": mfirst if j == 0 else mrest, "mask_rest": mrest,
            "xcur": cur_fm[c], "wo": wo, "ident_bf": ident_bf,
        })
    res = _run("G", in_maps)
    return [res[c]["x4"] for c in range(8)]


def run_H(inputs, l, cur_fm):
    w1 = np.asarray(inputs["L1_ffn_w1"], np.float32)[l]
    b1 = np.asarray(inputs["L1_ffn_b1"], np.float32)[l]
    w2 = np.asarray(inputs["L1_ffn_w2"], np.float32)[l]
    b2 = np.asarray(inputs["L1_ffn_b2"], np.float32)[l]
    lg = np.asarray(inputs["L1_ffn_ln_g"], np.float32)[l]
    lb = np.asarray(inputs["L1_ffn_ln_b"], np.float32)[l]
    w1re = np.zeros((1024, 5632), np.float32)
    w1re[:, 0:2730] = w1[:, 0:2730]
    w1re[:, 2816:2816 + 2730] = w1[:, 2730:5460]
    b1re = np.zeros(5632, np.float32)
    b1re[0:2730] = b1[0:2730]
    b1re[2816:2816 + 2730] = b1[2730:5460]
    w2re = np.zeros((2816, 1024), np.float32)
    w2re[0:2730] = w2
    in_maps = []
    for c in range(8):
        b, j = c // 4, c % 4
        if j == 0:
            pcol = np.zeros((128, 8), np.float32)
        else:
            pcol = np.ascontiguousarray(cur_fm[c - 1][:, -1].reshape(8, 128).T)
        in_maps.append({
            "xcur": cur_fm[c], "pcol": pcol,
            "w1re": w1re, "b1_c": _col(b1re, 44),
            "w2re": w2re, "b2_c": _col(b2, 8),
            "lg_c": _col(lg, 8), "lb_c": _col(lb, 8),
            **_ones_consts(),
        })
    res = _run("H", in_maps)
    return [res[c]["x5"] for c in range(8)]


def run_Z(x3_fm, cur_fm):
    in_maps = [{"x3q": x3_fm[c], "curq": cur_fm[c], **_ones_consts()}
               for c in range(8)]
    res = _run("Z", in_maps)
    return [res[c]["out"] for c in range(8)]


def kernel(**inputs):
    SERIALIZE[0] = True
    _nc_cache.clear()
    inputs = {k: np.asarray(v) for k, v in inputs.items()}
    x = np.asarray(inputs["L0_attn_hidden_states"], np.float32)
    x1_fm, gparts = run_A(inputs, x)
    x2_fm, ln3_fm = run_B(inputs, x1_fm, gparts)
    xsil, zsil, xp = run_C(inputs, ln3_fm)
    yg = run_D(inputs, xsil, zsil, xp)
    x3_fm, xl_fm = run_E(inputs, yg, x2_fm)
    cur = xl_fm
    for l in range(2):
        q, k, vt = run_F(inputs, l, cur)
        cur = run_G(inputs, l, cur, q, k, vt)
        cur = run_H(inputs, l, cur)
    out_fm = run_Z(x3_fm, cur)
    out = np.empty((2, S, D), np.float32)
    for c in range(8):
        b, j = c // 4, c % 4
        out[b, 4096 * j:4096 * (j + 1), :] = out_fm[c].T
    return out


# version 08 (final)
# speedup vs baseline: 1.7315x; 1.7315x over previous
import numpy as np
from contextlib import ExitStack

import concourse.bass as bass
import concourse.mybir as mybir
from concourse.bass_utils import run_bass_kernel_spmd

F32 = mybir.dt.float32
BF16 = mybir.dt.bfloat16
AF = mybir.ActivationFunctionType
ALU = mybir.AluOpType
AX = mybir.AxisListType

ENGINES = ("sync", "scalar", "vector", "tensor", "gpsimd")

D = 1024
NCH = 8
B = 2
S = 16384
QT = 4096
NEG = -1e30


SERIALIZE = [False]
SER_EXCLUDE_TAGS = set()


class Plan:
    def __init__(self, nc):
        self.nc = nc
        self.steps = []
        self.count = {e: 0 for e in ENGINES}
        self._prev = None
        self.default_tag = None

    def emit(self, engine, fn, after=(), dma=False, n_dma=1, strict_after=(),
             tag=None):
        waits = {}
        ser = SERIALIZE[0]
        if isinstance(ser, tuple):
            ser = ser[0] <= len(self.steps) < ser[1]
        tg = tag if tag is not None else self.default_tag
        if ser and tg is not None and tg in SER_EXCLUDE_TAGS:
            ser = False
        extra = [self._prev] if (ser and self._prev) else []
        for t in list(after) + list(strict_after) + extra:
            if t is None:
                continue
            e, c = t
            if waits.get(e, 0) < c:
                waits[e] = c
        inc = 16 * n_dma if dma else 1
        self.count[engine] += inc
        tok = (engine, self.count[engine])
        self.steps.append((engine, waits, fn, dma))
        self._prev = tok
        return tok

    def barrier(self):
        return [(e, self.count[e]) for e in ENGINES if self.count[e] > 0]

    def build(self, block, sems):
        per_engine = {e: [] for e in ENGINES}
        for (e, waits, fn, dma) in self.steps:
            per_engine[e].append((waits, fn, dma))

        def make(e):
            def body(eng):
                waited = {}
                for waits, fn, dma in per_engine[e]:
                    for we, wc in waits.items():
                        if waited.get(we, 0) < wc:
                            eng.wait_ge(sems[we], wc)
                            waited[we] = wc
                    insts = fn(eng)
                    if insts is None:
                        raise RuntimeError("step emitted nothing")
                    if not isinstance(insts, (list, tuple)):
                        insts = [insts]
                    if dma:
                        insts[-1].then_inc(sems[e], 16)
                    else:
                        insts[-1].then_inc(sems[e], 1)
            body.__name__ = f"_{e}_body"
            return body

        block.sync(make("sync"))
        block.scalar(make("scalar"))
        block.vector(make("vector"))
        block.tensor(make("tensor"))
        block.gpsimd(make("gpsimd"))


class Ctx:
    def __init__(self, name):
        self.nc = bass.Bass("TRN2", target_bir_lowering=False)
        self.pl = Plan(self.nc)
        self.es = ExitStack()
        self.inputs = {}
        self.outputs = {}
        self._uid = 0
        self.bank_last = {}
        self.prefix = ""
        self.io_override = {}
        self.default_stack = None
        self.fused = False

    def bank_wait(self, ps):
        t = self.bank_last.get(id(ps))
        return [t] if t is not None else []

    def bank_set(self, ps, tok):
        self.bank_last[id(ps)] = tok

    def uid(self, base):
        self._uid += 1
        return f"{base}_{self._uid}"

    def inp(self, name, shape, dtype=F32):
        if name in self.io_override:
            return self.io_override[name]
        full = self.prefix + name
        t = self.nc.dram_tensor(full, list(shape), dtype, kind="ExternalInput")
        self.inputs[full] = t
        return t.ap()

    def outp(self, name, shape, dtype=F32):
        full = self.prefix + name
        if self.fused and name != "out":
            t = self.nc.dram_tensor(full, list(shape), dtype, kind="Internal")
        else:
            t = self.nc.dram_tensor(full, list(shape), dtype, kind="ExternalOutput")
        self.outputs[full] = t
        return t.ap()

    def dram(self, name, shape, dtype=F32):
        return self.nc.dram_tensor(self.prefix + name, list(shape), dtype,
                                   kind="Internal").ap()

    def sb(self, base, shape, dtype=F32, stack=None, side="left"):
        return (stack or self.default_stack or self.es).enter_context(
            self.nc.sbuf_tensor(self.uid(base), list(shape), dtype, side=side))

    def ps(self, base, shape, dtype=F32, stack=None):
        return (stack or self.default_stack or self.es).enter_context(
            self.nc.psum_tensor(self.uid(base), list(shape), dtype))

    def finish(self):
        nc = self.nc
        with ExitStack() as st:
            sems = {e: st.enter_context(nc.semaphore(f"sem_{e}")) for e in ENGINES}
            with nc.Block() as block:
                self.pl.build(block, sems)
        self.es.close()
        return nc


def last(toks):
    best = {}
    for t in toks:
        if t is None:
            continue
        e, c = t
        if best.get(e, 0) < c:
            best[e] = c
    return [(e, c) for e, c in best.items()]



def load_weight(cx, dram_ap, rows, cols, name, dep=(), to_bf16=True):
    nk = rows // 128
    stage = cx.sb(name + "_st", [128, nk * cols], F32)
    wsb = cx.sb(name, [128, nk * cols], BF16) if to_bf16 else stage
    toks = []
    for k in range(nk):
        toks.append(cx.pl.emit(
            "sync",
            lambda eng, k=k: eng.dma_start(
                out=stage[:, k * cols:(k + 1) * cols],
                in_=dram_ap[128 * k:128 * (k + 1), :]),
            after=dep, dma=True))
    if not to_bf16:
        return stage, last(toks)[0] if len(last(toks)) == 1 else ("sync", cx.pl.count["sync"])
    tok = cx.pl.emit(
        "vector",
        lambda eng: eng.tensor_copy(out=wsb[:, :], in_=stage[:, :]),
        after=toks)
    return wsb, tok


def emit_linear(cx, wsb, nk, nout, x_bf, T, out_write, after=(), psA=None,
                Tn=512, extra_rhs_tok=None):
    pl = cx.pl
    _saved_tag = pl.default_tag
    pl.default_tag = "lin"
    n_of = (nout + 127) // 128
    evac_toks = []
    bi = 0
    for of in range(n_of):
        o0 = of * 128
        osz = min(128, nout - o0)
        for nt in range(0, T, Tn):
            tsz = min(Tn, T - nt)
            ps = psA[bi % len(psA)]
            waits = list(after) + cx.bank_wait(ps)

            def mmgroup(eng, o0=o0, osz=osz, nt=nt, tsz=tsz, ps=ps):
                ret = []
                for k in range(nk):
                    ret.append(eng.matmul(
                        ps[0:osz, 0:tsz],
                        wsb[:, k * nout + o0: k * nout + o0 + osz],
                        x_bf[k][:, nt:nt + tsz],
                        start=(k == 0), stop=(k == nk - 1)))
                return ret[-1]
            pe_tok = pl.emit("tensor", mmgroup, after=waits)
            ev = out_write(of, o0, osz, nt, tsz, ps, pe_tok)
            evac_toks.append(ev)
            cx.bank_set(ps, ev)
            bi += 1
    pl.default_tag = _saved_tag
    return evac_toks


def simple_linear(cx, wsb, nk, nout, x_bf, T, dst_fm, after=(), psA=None,
                  bias_col=None, act=None, dst2_fm=None, Tn=512,
                  evac_engine="scalar", scale=1.0):
    def write(of, o0, osz, nt, tsz, ps, pe_tok):
        if act is None and bias_col is not None:
            def ev(eng, of=of, osz=osz, nt=nt, tsz=tsz, ps=ps):
                return eng.tensor_scalar_add(
                    out=dst_fm[of][0:osz, nt:nt + tsz], in0=ps[0:osz, 0:tsz],
                    scalar1=bias_col[0:osz, of:of + 1])
            t1 = cx.pl.emit("vector", ev, after=[pe_tok])
            if dst2_fm is not None:
                t2 = cx.pl.emit(
                    "vector",
                    lambda eng, of=of, osz=osz, nt=nt, tsz=tsz: eng.tensor_copy(
                        out=dst2_fm[of][0:osz, nt:nt + tsz],
                        in_=dst_fm[of][0:osz, nt:nt + tsz]),
                    after=[t1])
                return t2
            return t1

        def ev(eng, of=of, osz=osz, nt=nt, tsz=tsz, ps=ps):
            kw = {}
            if bias_col is not None:
                kw["bias"] = bias_col[0:osz, of:of + 1]
            r = eng.activation(out=dst_fm[of][0:osz, nt:nt + tsz],
                               in_=ps[0:osz, 0:tsz],
                               func=(act or AF.Copy), scale=scale, **kw)
            return r
        t1 = cx.pl.emit(evac_engine, ev, after=[pe_tok])
        if dst2_fm is not None:
            t2 = cx.pl.emit(
                "vector",
                lambda eng, of=of, osz=osz, nt=nt, tsz=tsz: eng.tensor_copy(
                    out=dst2_fm[of][0:osz, nt:nt + tsz],
                    in_=dst_fm[of][0:osz, nt:nt + tsz]),
                after=[t1])
            return t2
        return t1
    return emit_linear(cx, wsb, nk, nout, x_bf, T, write, after=after, psA=psA, Tn=Tn)


def emit_ln(cx, x_fm, T, dst_bf=None, dst_f32=None, after=(), psum2=None,
            ones_col=None, ones_row=None, scratch=None, eps=1e-5,
            gain_col=None, bias_col=None, nch=NCH, Tn=512):
    pl = cx.pl
    st = scratch
    toks_out = []
    _saved_tag = pl.default_tag
    pl.default_tag = "ln"
    for nt in range(0, T, Tn):
        tsz = min(Tn, T - nt)
        psStat, psMu, psRs = psum2
        t_sum = pl.emit(
            "tensor",
            lambda eng, nt=nt, tsz=tsz: [
                eng.matmul(psStat[0:1, 0:tsz], ones_col[:, 0:1],
                           x_fm[c][:, nt:nt + tsz],
                           start=(c == 0), stop=(c == nch - 1))
                for c in range(nch)][-1],
            after=list(after) + cx.bank_wait(psStat))
        t_mmq = []
        t_sqc = []
        for c in range(nch):
            dep = list(after)
            if len(t_mmq) >= 2:
                dep.append(t_mmq[-2])
            t_sq = pl.emit(
                "scalar",
                lambda eng, c=c, nt=nt, tsz=tsz: eng.activation(
                    out=st["sq"][c % 2][:, 0:tsz], in_=x_fm[c][:, nt:nt + tsz],
                    func=AF.Square),
                after=dep)
            t_sqc.append(t_sq)
            t_mmq.append(pl.emit(
                "tensor",
                lambda eng, c=c, tsz=tsz: eng.matmul(
                    psStat[32:33, 0:tsz], ones_col[:, 0:1],
                    st["sq"][c % 2][:, 0:tsz],
                    start=(c == 0), stop=(c == nch - 1)),
                after=[t_sq]))
        mu, var, r0, r1 = st["mu"], st["var"], st["r0"], st["r1"]
        t_mu = pl.emit(
            "vector",
            lambda eng, tsz=tsz: eng.tensor_scalar_mul(
                out=mu[0:1, 0:tsz], in0=psStat[0:1, 0:tsz],
                scalar1=1.0 / (128 * nch)),
            after=[t_sum])
        t_mu2 = pl.emit(
            "vector",
            lambda eng, tsz=tsz: eng.tensor_mul(
                out=r0[0:1, 0:tsz], in0=mu[0:1, 0:tsz], in1=mu[0:1, 0:tsz]),
            after=[t_mu])
        t_var = pl.emit(
            "vector",
            lambda eng, tsz=tsz: eng.scalar_tensor_tensor(
                out=var[0:1, 0:tsz], in0=psStat[32:33, 0:tsz],
                scalar=1.0 / (128 * nch), in1=r0[0:1, 0:tsz],
                op0=ALU.mult, op1=ALU.subtract),
            after=[t_mmq[-1], t_mu2])
        cx.bank_set(psStat, t_var)
        t_vae = pl.emit(
            "vector",
            lambda eng, tsz=tsz: eng.tensor_scalar_add(
                out=var[0:1, 0:tsz], in0=var[0:1, 0:tsz], scalar1=eps),
            after=[t_var])
        t_ln = pl.emit(
            "scalar",
            lambda eng, tsz=tsz: eng.activation(
                out=r1[0:1, 0:tsz], in_=var[0:1, 0:tsz], func=AF.Ln),
            after=[t_vae])
        t_rs = pl.emit(
            "scalar",
            lambda eng, tsz=tsz: eng.activation(
                out=r0[0:1, 0:tsz], in_=r1[0:1, 0:tsz], func=AF.Exp, scale=-0.5),
            after=[t_ln])
        t_bmu = pl.emit(
            "tensor",
            lambda eng, tsz=tsz: eng.matmul(
                psMu[:, 0:tsz], ones_row[0:1, :], mu[0:1, 0:tsz],
                start=True, stop=True),
            after=[t_mu] + cx.bank_wait(psMu))
        t_brs = pl.emit(
            "tensor",
            lambda eng, tsz=tsz: eng.matmul(
                psRs[:, 0:tsz], ones_row[0:1, :], r0[0:1, 0:tsz],
                start=True, stop=True),
            after=[t_rs] + cx.bank_wait(psRs))
        prev_write = None
        for c in range(nch):
            t_a = pl.emit(
                "vector",
                lambda eng, c=c, nt=nt, tsz=tsz: eng.tensor_sub(
                    out=st["tmp"][:, 0:tsz], in0=x_fm[c][:, nt:nt + tsz],
                    in1=psMu[:, 0:tsz]),
                after=[t_bmu] + list(after) + ([prev_write] if prev_write else []),
                strict_after=([prev_write] if prev_write else []))
            dsts = ([dst_f32] if dst_f32 is not None else []) + \
                   ([dst_bf] if dst_bf is not None else [])
            prev = t_a
            if gain_col is None:
                for dd in dsts:
                    prev = pl.emit(
                        "vector",
                        lambda eng, c=c, nt=nt, tsz=tsz, dd=dd: eng.tensor_mul(
                            out=dd[c][:, nt:nt + tsz], in0=st["tmp"][:, 0:tsz],
                            in1=psRs[:, 0:tsz]),
                        after=[t_brs, t_a])
            else:
                prev = pl.emit(
                    "vector",
                    lambda eng, tsz=tsz: eng.tensor_mul(
                        out=st["tmp"][:, 0:tsz], in0=st["tmp"][:, 0:tsz],
                        in1=psRs[:, 0:tsz]),
                    after=[t_brs, t_a])
                for dd in dsts:
                    prev = pl.emit(
                        "vector",
                        lambda eng, c=c, nt=nt, tsz=tsz, dd=dd: eng.tensor_scalar(
                            out=dd[c][:, nt:nt + tsz], in0=st["tmp"][:, 0:tsz],
                            scalar1=gain_col[:, c:c + 1],
                            scalar2=bias_col[:, c:c + 1],
                            op0=ALU.mult, op1=ALU.add),
                        after=[prev])
            toks_out.append(prev)
            prev_write = prev
        cx.bank_set(psMu, toks_out[-1])
        cx.bank_set(psRs, toks_out[-1])
    pl.default_tag = _saved_tag
    return toks_out


def ln_scratch(cx, Tn=512, stack=None):
    return {
        "sq": [cx.sb("lnsq", [128, Tn], F32, stack) for _ in range(2)],
        "tmp": cx.sb("lntmp", [128, Tn], F32, stack),
        "mu": cx.sb("lnmu", [1, Tn], F32, stack),
        "var": cx.sb("lnvar", [1, Tn], F32, stack),
        "r0": cx.sb("lnr0", [1, Tn], F32, stack),
        "r1": cx.sb("lnr1", [1, Tn], F32, stack),
    }


def build_launch_A(cx=None, io=None, **kw):
    standalone = cx is None
    if standalone:
        cx = Ctx("A")
    _stg = ExitStack()
    _prev_stack = cx.default_stack
    cx.default_stack = _stg
    cx.io_override = io or {}
    pl = cx.pl

    xq = cx.inp("xq", [D, QT])
    xinit = cx.inp("xinit", [D, 1024])
    wq_d = cx.inp("Wq", [D, D]); wk_d = cx.inp("Wk", [D, D])
    wv_d = cx.inp("Wv", [D, D]); wo_d = cx.inp("Wo", [D, D])
    w1_d = cx.inp("W1", [D, 512]); w2_d = cx.inp("W2", [512, D])
    b1_d = cx.inp("b1c", [128, 4]); b2_d = cx.inp("b2c", [128, 8])
    onesc_d = cx.inp("ones_col", [128, 1])
    onesr_d = cx.inp("ones_row", [1, 128])
    mask_d = cx.inp("causal_mask", [1024, 1024])
    flag_d = cx.inp("flag", [128, 2])
    ident_d = cx.inp("ident_bf", [128, 128], BF16)

    x1_out = cx.outp("x1", [D, QT])
    gpart_out = cx.outp("gpart", [128, 8])

    Tn = 512
    ones_col = cx.sb("onesc", [128, 1]); ones_row = cx.sb("onesr", [1, 128])
    flag_sb = cx.sb("flag", [128, 2])
    ident_bf = cx.sb("identbf", [128, 128], BF16)
    b1_sb = cx.sb("b1", [128, 4]); b2_sb = cx.sb("b2", [128, 8])
    lns = ln_scratch(cx, Tn=512)
    ctx_bf = [cx.sb("ctxb", [128, 256], BF16) for _ in range(NCH)]
    k_ctx = [cx.sb("kctx", [128, 256], BF16) for _ in range(NCH)]
    vT_ctx = [cx.sb("vtc", [128, D], BF16) for _ in range(2)]
    initial_preO = [cx.sb("ipo", [128, 1024], BF16) for _ in range(NCH)]
    sm_s = [cx.sb("sms", [128, 1024], F32) for _ in range(2)]
    sm_m = [cx.sb("smm", [128, 1], F32) for _ in range(2)]
    sm_r = [cx.sb("smr", [128, 1], F32) for _ in range(2)]
    sm_p = [cx.sb("smp", [128, 1024], BF16) for _ in range(2)]
    sm_pt = [cx.sb("smpt", [128, 1024], BF16) for _ in range(2)]
    gacc = cx.sb("gacc", [128, 8], F32)

    tok_const = [
        pl.emit("sync", lambda eng: eng.dma_start(out=ones_col[:], in_=onesc_d), dma=True),
        pl.emit("sync", lambda eng: eng.dma_start(out=ones_row[:], in_=onesr_d), dma=True),
        pl.emit("sync", lambda eng: eng.dma_start(out=flag_sb[:], in_=flag_d), dma=True),
        pl.emit("sync", lambda eng: eng.dma_start(out=ident_bf[:], in_=ident_d), dma=True),
        pl.emit("sync", lambda eng: eng.dma_start(out=b1_sb[:], in_=b1_d), dma=True),
        pl.emit("sync", lambda eng: eng.dma_start(out=b2_sb[:], in_=b2_d), dma=True),
    ]

    psA = [cx.ps("psA", [128, 512], F32) for _ in range(2)]
    ps_ln = [cx.ps("psStat", [64, 512], F32),
             cx.ps("psMu", [128, 512], F32), cx.ps("psRs", [128, 512], F32)]
    psS = cx.ps("psS", [128, 1024], F32)
    psT = cx.ps("psT", [128, 1024], BF16)
    psV = psA[0]

    mid = ExitStack()
    ln_init_bf = [cx.sb("lni", [128, 1024], BF16, mid, side="right") for _ in range(NCH)]
    k_init = [cx.sb("kini", [128, 1024], BF16, mid, side="right") for _ in range(NCH)]
    vT_init = [cx.sb("vti", [128, D], BF16, mid, side="right") for _ in range(8)]

    ph0 = ExitStack()
    xinit_f = [cx.sb("xi", [128, 1024], F32, ph0, side="right") for _ in range(NCH)]
    t_xi = [pl.emit("sync",
                    lambda eng, c=c: eng.dma_start(
                        out=xinit_f[c][:, :], in_=xinit[128 * c:128 * (c + 1), :]),
                    dma=True)
            for c in range(NCH)]
    ln0_toks = emit_ln(cx, xinit_f, 1024, dst_bf=ln_init_bf, after=t_xi + tok_const,
                       psum2=ps_ln, ones_col=ones_col, ones_row=ones_row,
                       scratch=lns)
    ph0.close()

    ph1 = ExitStack()
    bar1 = pl.barrier()
    w1_sb, tw1 = load_weight_ph(cx, w1_d, D, 512, "w1", ph1, dep=bar1)
    w2_sb, tw2 = load_weight_ph(cx, w2_d, 512, D, "w2", ph1, dep=bar1)
    h1_bf = [cx.sb("h1", [128, 1024], BF16, ph1, side="right") for _ in range(4)]
    comp_f = [cx.sb("comp", [128, 1024], F32, ph1, side="right") for _ in range(NCH)]
    t_h1 = simple_linear(cx, w1_sb, NCH, 512, ln_init_bf, 1024, h1_bf,
                         after=ln0_toks + [tw1], psA=psA, bias_col=b1_sb,
                         act=AF.Relu)
    t_c2 = simple_linear(cx, w2_sb, 4, D, h1_bf, 1024, comp_f,
                         after=t_h1 + [tw2], psA=psA, bias_col=b2_sb)
    t_pool = []
    for c in range(NCH):
        tp = pl.emit(
            "vector",
            lambda eng, c=c: eng.tensor_reduce(
                out=lns["sq"][0][:, 0:256],
                in_=comp_f[c][:, :].rearrange("p (g four) -> p g four", four=4),
                axis=AX.X, op=ALU.add),
            after=t_c2)
        tp = pl.emit(
            "vector",
            lambda eng, c=c: eng.tensor_scalar_mul(
                out=ctx_bf[c][:, :], in0=lns["sq"][0][:, 0:256], scalar1=0.25),
            after=[tp])
        t_pool.append(tp)
    ph1.close()

    ph2 = ExitStack()
    bar2 = pl.barrier()
    wk_sb, twk = load_weight_ph(cx, wk_d, D, D, "wk", ph2, dep=bar2)
    wv_sb, twv = load_weight_ph(cx, wv_d, D, D, "wv", ph2, dep=bar2)
    t_kctx = simple_linear(cx, wk_sb, NCH, D, ctx_bf, 256, k_ctx,
                           after=t_pool + [twk], psA=psA)
    t_kini = simple_linear(cx, wk_sb, NCH, D, ln_init_bf, 1024, k_init,
                           after=ln0_toks + [twk], psA=psA)

    def vT_make(act_bf, ntok, dst, dep):
        toks = []
        for tt in range(ntok // 128):
            for half in range(2):
                ps = psA[half]

                def mm(eng, tt=tt, half=half, ps=ps):
                    ret = []
                    for k in range(NCH):
                        ret.append(eng.matmul(
                            ps[:, 0:512],
                            act_bf[k][:, 128 * tt:128 * (tt + 1)],
                            wv_sb[:, k * D + 512 * half: k * D + 512 * half + 512],
                            start=(k == 0), stop=(k == NCH - 1)))
                    return ret[-1]
                pe = pl.emit("tensor", mm, after=list(dep) + cx.bank_wait(ps))
                ev = pl.emit(
                    "vector",
                    lambda eng, tt=tt, half=half, ps=ps: eng.tensor_copy(
                        out=dst[tt][:, 512 * half:512 * half + 512],
                        in_=ps[:, 0:512]),
                    after=[pe])
                cx.bank_set(ps, ev)
                toks.append(ev)
        return toks
    t_vtc = vT_make(ctx_bf, 256, vT_ctx, t_pool + [twv])
    t_vti = vT_make(ln_init_bf, 1024, vT_init, ln0_toks + [twv])
    ph2.close()

    ph3 = ExitStack()
    cx.ph3_step_idx = len(pl.steps)
    bar3 = pl.barrier()
    wq_sb, twq = load_weight_ph(cx, wq_d, D, D, "wq", ph3, wsb_stack=None,
                                dep=bar3)
    wo_sb, two = load_weight_ph(cx, wo_d, D, D, "wo", ph3, wsb_stack=None,
                                dep=bar3)
    q_init = [cx.sb("qini", [128, 1024], BF16, ph3, side="right") for _ in range(NCH)]
    mask_sb = cx.sb("maskt", [128, 1024], F32, ph3, side="right")
    t_qini = simple_linear(cx, wq_sb, NCH, D, ln_init_bf, 1024, q_init,
                           after=bar3 + [twq], psA=psA)

    unit_state = {"idx": 0, "hist": {}}

    def attn_unit(qsrc, ksrc, vTsrc, nkeys, qt, head, use_mask, dep, dst):
        s = unit_state["idx"] % 2
        unit_state["idx"] += 1
        prev_slot = unit_state["hist"].get(s)
        hc, hb = head // 2, (head % 2) * 64
        w0 = list(dep) + (prev_slot or [])

        def mm_s(eng):
            ret = []
            for half in range((nkeys + 511) // 512):
                w = min(512, nkeys - 512 * half)
                ret.append(eng.matmul(
                    psS[:, 512 * half:512 * half + w],
                    qsrc[hc][hb:hb + 64, 128 * qt:128 * (qt + 1)],
                    ksrc[hc][hb:hb + 64, 512 * half:512 * half + w],
                    start=True, stop=True))
            return ret[-1]
        t_s = pl.emit("tensor", mm_s, after=w0 + cx.bank_wait(psS), tag="attn")
        if use_mask:
            t_ms = pl.emit(
                "vector",
                lambda eng, s=s: eng.scalar_tensor_tensor(
                    out=sm_s[s][:, 0:nkeys], in0=psS[:, 0:nkeys], scalar=0.125,
                    in1=mask_sb[:, 0:nkeys],
                    op0=ALU.mult, op1=ALU.add),
                after=[t_s] + (prev_slot or []), tag="attn")
        else:
            t_ms = pl.emit(
                "vector",
                lambda eng, s=s: eng.tensor_scalar_mul(
                    out=sm_s[s][:, 0:nkeys], in0=psS[:, 0:nkeys], scalar1=0.125),
                after=[t_s] + (prev_slot or []), tag="attn")
        cx.bank_set(psS, t_ms)
        t_m = pl.emit(
            "vector",
            lambda eng, s=s: eng.reduce_max(out=sm_m[s][:, 0:1],
                                            in_=sm_s[s][:, 0:nkeys],
                                            axis=AX.X, negate=True),
            after=[t_ms], strict_after=[t_ms], tag="attn")
        t_p = pl.emit(
            "scalar",
            lambda eng, s=s: eng.activation(out=sm_p[s][:, 0:nkeys],
                                            in_=sm_s[s][:, 0:nkeys],
                                            func=AF.Exp, bias=sm_m[s][:, 0:1],
                                            accum_out=sm_r[s][:, 0:1]),
            after=[t_m], tag="attn")
        t_rc = pl.emit(
            "vector",
            lambda eng, s=s: eng.reciprocal(out=sm_r[s][:, 0:1], in_=sm_r[s][:, 0:1]),
            after=[t_p], tag="attn")
        t_pn = pl.emit(
            "vector",
            lambda eng, s=s: eng.tensor_scalar_mul(
                out=sm_p[s][:, 0:nkeys], in0=sm_p[s][:, 0:nkeys],
                scalar1=sm_r[s][:, 0:1]),
            after=[t_rc], strict_after=[t_rc], tag="attn")

        def mm_t(eng, s=s):
            ret = []
            for kc in range(nkeys // 128):
                ret.append(eng.transpose(
                    psT[:, 128 * kc:128 * (kc + 1)],
                    sm_p[s][:, 128 * kc:128 * (kc + 1)], ident_bf[:, :]))
            return ret[-1]
        t_tr = pl.emit("tensor", mm_t, after=[t_pn] + cx.bank_wait(psT), tag="attn")
        t_te = pl.emit(
            "vector",
            lambda eng, s=s: eng.tensor_copy(out=sm_pt[s][:, 0:nkeys],
                                             in_=psT[:, 0:nkeys]),
            after=[t_tr], tag="attn")
        cx.bank_set(psT, t_te)

        def mm_av(eng, s=s):
            ret = []
            for kc in range(nkeys // 128):
                ret.append(eng.matmul(
                    psV[0:64, 0:128],
                    vTsrc[kc][:, head * 64:head * 64 + 64],
                    sm_pt[s][:, 128 * kc:128 * (kc + 1)],
                    start=(kc == 0), stop=(kc == nkeys // 128 - 1)))
            return ret[-1]
        t_av = pl.emit("tensor", mm_av, after=[t_te] + cx.bank_wait(psV), tag="attn")
        t_ev = pl.emit(
            "scalar",
            lambda eng: eng.copy(
                out=dst[hc][hb:hb + 64, 128 * qt:128 * (qt + 1)],
                in_=psV[0:64, 0:128]),
            after=[t_av], tag="attn")
        cx.bank_set(psV, t_ev)
        unit_state["hist"][s] = [t_pn, t_te, t_ev]
        return t_ev

    att_toks = []
    base = last(t_qini + t_kini + t_vti + tok_const)
    qt_last = []
    for qt in range(8):
        t_mk = pl.emit("sync",
                       lambda eng, qt=qt: eng.dma_start(
                           out=mask_sb[:, :],
                           in_=mask_d[128 * qt:128 * (qt + 1), :]),
                       after=base + qt_last, dma=True)
        qt_toks = []
        for head in range(16):
            t = attn_unit(q_init, k_init, vT_init, 1024, qt, head,
                          True, last(base + [t_mk]), initial_preO)
            qt_toks.append(t)
            att_toks.append(t)
        qt_last = last(qt_toks)
    ph3.close()
    mid.close()

    ph4 = ExitStack()
    x_t = [cx.sb("xt", [128, Tn], F32, ph4, side="right") for _ in range(NCH)]
    ln1_bf = [cx.sb("l1b", [128, Tn], BF16, ph4, side="right") for _ in range(NCH)]
    q_bf = [cx.sb("qb", [128, Tn], BF16, ph4, side="right") for _ in range(NCH)]
    att_f = [cx.sb("acat", [128, Tn], F32, ph4, side="right") for _ in range(NCH)]
    mrg_bf = [cx.sb("mrg", [128, Tn], BF16, ph4, side="right") for _ in range(NCH)]
    o_f = [cx.sb("of", [128, Tn], F32, ph4, side="right") for _ in range(NCH)]
    x1_t = [cx.sb("x1t", [128, Tn], F32, ph4, side="right") for _ in range(NCH)]
    lnp_f = att_f
    ln2_f = o_f

    bar4 = pl.barrier()
    t_gz = pl.emit("vector", lambda eng: eng.memset(gacc[:, :], 0.0))

    prev_tile_toks = last(bar4 + [two, t_gz])
    store_toks = []
    for ti in range(QT // Tn):
        t0 = ti * Tn
        t_x = [pl.emit("sync",
                       lambda eng, c=c, t0=t0: eng.dma_start(
                           out=x_t[c][:, :], in_=xq[128 * c:128 * (c + 1), t0:t0 + Tn]),
                       after=prev_tile_toks, dma=True, tag="dma4")
               for c in range(NCH)]
        t_ln1 = emit_ln(cx, x_t, Tn, dst_bf=ln1_bf, after=t_x + tok_const,
                        psum2=ps_ln, ones_col=ones_col, ones_row=ones_row,
                        scratch=lns)
        t_q = simple_linear(cx, wq_sb, NCH, D, ln1_bf, Tn, q_bf,
                            after=t_ln1, psA=psA)
        aw = last(t_q)
        cr_toks = []
        for head in range(16):
            for sq in range(Tn // 128):
                t = attn_unit(q_bf, k_ctx, vT_ctx, 256, sq, head, False,
                              aw, att_f)
                cr_toks.append(t)
        t_cross = last(cr_toks)
        t_mrg = []
        for c in range(NCH):
            if t0 < 1024:
                t_a = pl.emit(
                    "vector",
                    lambda eng, c=c, t0=t0: eng.tensor_scalar_mul(
                        out=lns["sq"][0][:, 0:Tn],
                        in0=initial_preO[c][:, t0:t0 + Tn],
                        scalar1=flag_sb[:, 0:1]),
                    after=t_cross, tag="mr")
                t_b = pl.emit(
                    "vector",
                    lambda eng, c=c: eng.scalar_tensor_tensor(
                        out=lns["sq"][1][:, 0:Tn], in0=att_f[c][:, 0:Tn],
                        scalar=flag_sb[:, 1:2], in1=lns["sq"][0][:, 0:Tn],
                        op0=ALU.mult, op1=ALU.add),
                    after=[t_a], tag="mr")
                t_c = pl.emit(
                    "vector",
                    lambda eng, c=c: eng.tensor_copy(
                        out=mrg_bf[c][:, :], in_=lns["sq"][1][:, 0:Tn]),
                    after=[t_b], tag="mr")
            else:
                t_c = pl.emit(
                    "vector",
                    lambda eng, c=c: eng.tensor_copy(
                        out=mrg_bf[c][:, :], in_=att_f[c][:, 0:Tn]),
                    after=t_cross, tag="mr")
            t_mrg.append(t_c)
        t_o = simple_linear(cx, wo_sb, NCH, D, mrg_bf, Tn, o_f,
                            after=t_mrg + [two], psA=psA)
        t_lnp = emit_ln(cx, o_f, Tn, dst_f32=lnp_f, after=t_o,
                        psum2=ps_ln, ones_col=ones_col, ones_row=ones_row,
                        scratch=lns)
        t_res = [pl.emit(
            "vector",
            lambda eng, c=c: eng.tensor_add(
                out=x1_t[c][:, :], in0=x_t[c][:, :], in1=lnp_f[c][:, :]),
            after=t_lnp + t_x, tag="mr") for c in range(NCH)]
        t_ln2 = emit_ln(cx, x1_t, Tn, dst_f32=ln2_f, after=t_res,
                        psum2=ps_ln, ones_col=ones_col, ones_row=ones_row,
                        scratch=lns)
        t_gs = []
        for c in range(NCH):
            tg = pl.emit(
                "vector",
                lambda eng, c=c: eng.tensor_reduce(
                    out=lns["sq"][0][:, 0:1],
                    in_=ln2_f[c][:, :], axis=AX.X, op=ALU.add),
                after=t_ln2, tag="mr")
            tg = pl.emit(
                "vector",
                lambda eng, c=c: eng.tensor_add(
                    out=gacc[:, c:c + 1], in0=gacc[:, c:c + 1],
                    in1=lns["sq"][0][:, 0:1]),
                after=[tg], strict_after=[tg], tag="mr")
            t_gs.append(tg)
        t_st = [pl.emit("gpsimd",
                        lambda eng, c=c, t0=t0: eng.dma_start(
                            out=x1_out[128 * c:128 * (c + 1), t0:t0 + Tn],
                            in_=x1_t[c][:, :]),
                        after=t_res[c:c + 1], dma=True, tag="dma4")
                for c in range(NCH)]
        store_toks += t_st
        prev_tile_toks = last(t_res + t_gs + t_ln2 + t_st)
    t_gp = pl.emit("gpsimd",
                   lambda eng: eng.dma_start(out=gpart_out, in_=gacc[:, :]),
                   after=prev_tile_toks, dma=True)
    pl.emit("gpsimd", lambda eng: eng.engine_nop(),
            after=last(store_toks + [t_gp]))
    ph4.close()

    _stg.close()
    cx.default_stack = _prev_stack
    cx.io_override = {}
    if standalone:
        return cx.finish()
    return {"x1": x1_out, "gpart": gpart_out}


def load_weight_ph(cx, dram_ap, rows, cols, name, phase_stack, wsb_stack="phase",
                   dep=(), cast_engine="gpsimd"):
    nk = (rows + 127) // 128
    stage = [cx.sb(name + "_st", [128, cols], F32, phase_stack, side="right")
             for _ in range(2)]
    if wsb_stack == "phase":
        wsb = cx.sb(name, [128, nk * cols], BF16, phase_stack, side="right")
    else:
        wsb = cx.sb(name, [128, nk * cols], BF16)
    cast_toks = [None, None]
    tok = None
    _saved_tag = cx.pl.default_tag
    cx.pl.default_tag = "wload"
    for k in range(nk):
        sg = stage[k % 2]
        rsz = min(128, rows - 128 * k)
        waits = list(dep) + ([cast_toks[k % 2]] if cast_toks[k % 2] else [])
        t_d = cx.pl.emit(
            "sync",
            lambda eng, k=k, sg=sg, rsz=rsz: eng.dma_start(
                out=sg[0:rsz, :], in_=dram_ap[128 * k:128 * k + rsz, :]),
            after=waits, dma=True)
        tok = cx.pl.emit(
            cast_engine,
            lambda eng, k=k, sg=sg, rsz=rsz: eng.tensor_copy(
                out=wsb[0:rsz, k * cols:(k + 1) * cols], in_=sg[0:rsz, :]),
            after=[t_d])
        cast_toks[k % 2] = tok
    cx.pl.default_tag = _saved_tag
    return wsb, tok


def build_launch_B(cx=None, io=None, **kw):
    standalone = cx is None
    if standalone:
        cx = Ctx("B")
    _stg = ExitStack()
    _prev_stack = cx.default_stack
    cx.default_stack = _stg
    cx.io_override = io or {}
    pl = cx.pl
    Tn = 512

    x1q = cx.inp("x1q", [D, QT])
    gp_all = cx.inp("gp_all", [128, 32])
    cnw1_d = cx.inp("cn_w1", [D, D]); cnw2_d = cx.inp("cn_w2", [D, 512])
    cnw3_d = cx.inp("cn_w3", [512, 256]); cnw4_d = cx.inp("cn_w4", [256, 1])
    cb1_d = cx.inp("cn_b1c", [128, 8]); cb2_d = cx.inp("cn_b2c", [128, 4])
    cb3_d = cx.inp("cn_b3c", [128, 2]); cb4_d = cx.inp("cn_b4c", [1, 1])
    gw1_d = cx.inp("gc_w1", [D, 512]); gw2_d = cx.inp("gc_w2", [512, 256])
    gw3_d = cx.inp("gc_w3", [256, 1])
    gb1_d = cx.inp("gc_b1c", [128, 4]); gb2_d = cx.inp("gc_b2c", [128, 2])
    gb3_d = cx.inp("gc_b3c", [1, 1])
    ar_d = cx.inp("ar", [1, 1])
    onesc_d = cx.inp("ones_col", [128, 1]); onesr_d = cx.inp("ones_row", [1, 128])
    negr_d = cx.inp("neg_row", [1, 128])

    x2_out = cx.outp("x2", [D, QT])
    ln3_out = cx.outp("ln3", [D, QT], BF16)

    ones_col = cx.sb("onesc", [128, 1]); ones_row = cx.sb("onesr", [1, 128])
    neg_row = cx.sb("negr", [1, 128])
    lns = ln_scratch(cx, Tn=512)
    rowbuf = cx.sb("rowb", [1, 5 * Tn], F32)
    sc_sb = cx.sb("scs", [1, 8], F32)
    b_cols = cx.sb("bcols", [128, 8 + 4 + 2 + 4 + 2], F32)
    cb4_sb = cx.sb("cb4", [1, 2], F32)

    tok_c = [
        pl.emit("sync", lambda eng: eng.dma_start(out=ones_col[:], in_=onesc_d), dma=True),
        pl.emit("sync", lambda eng: eng.dma_start(out=ones_row[:], in_=onesr_d), dma=True),
        pl.emit("sync", lambda eng: eng.dma_start(out=neg_row[:], in_=negr_d), dma=True),
        pl.emit("sync", lambda eng: eng.dma_start(out=b_cols[:, 0:8], in_=cb1_d), dma=True),
        pl.emit("sync", lambda eng: eng.dma_start(out=b_cols[:, 8:12], in_=cb2_d), dma=True),
        pl.emit("sync", lambda eng: eng.dma_start(out=b_cols[:, 12:14], in_=cb3_d), dma=True),
        pl.emit("sync", lambda eng: eng.dma_start(out=b_cols[:, 14:18], in_=gb1_d), dma=True),
        pl.emit("sync", lambda eng: eng.dma_start(out=b_cols[:, 18:20], in_=gb2_d), dma=True),
        pl.emit("sync", lambda eng: eng.dma_start(out=cb4_sb[:, 0:1], in_=cb4_d), dma=True),
        pl.emit("sync", lambda eng: eng.dma_start(out=cb4_sb[:, 1:2], in_=gb3_d), dma=True),
        pl.emit("sync", lambda eng: eng.dma_start(out=sc_sb[0:1, 0:1], in_=ar_d), dma=True),
    ]
    cb1 = b_cols[:, 0:8]; cb2 = b_cols[:, 8:12]; cb3 = b_cols[:, 12:14]
    gb1 = b_cols[:, 14:18]; gb2 = b_cols[:, 18:20]

    psA = [cx.ps("psA", [128, 512], F32) for _ in range(2)]
    ps_ln = [cx.ps("psStat", [64, 512], F32),
             cx.ps("psMu", [128, 512], F32), cx.ps("psRs", [128, 512], F32)]
    psB = [cx.ps("psB", [128, 512], F32) for _ in range(2)]

    wstack = ExitStack()
    cnw1_sb, t1 = load_weight_ph(cx, cnw1_d, D, D, "cnw1", wstack, wsb_stack=None)
    cnw2_sb, t2 = load_weight_ph(cx, cnw2_d, D, 512, "cnw2", wstack, wsb_stack=None)
    cnw3_sb, t3 = load_weight_ph(cx, cnw3_d, 512, 256, "cnw3", wstack, wsb_stack=None)
    cnw4_sb, t4 = load_weight_ph(cx, cnw4_d, 256, 1, "cnw4", wstack, wsb_stack=None)
    gw1_sb, t5 = load_weight_ph(cx, gw1_d, D, 512, "gw1", wstack, wsb_stack=None)
    gw2_sb, t6 = load_weight_ph(cx, gw2_d, 512, 256, "gw2", wstack, wsb_stack=None)
    gw3_sb, t7 = load_weight_ph(cx, gw3_d, 256, 1, "gw3", wstack, wsb_stack=None)
    wtoks = [t1, t2, t3, t4, t5, t6, t7]
    wstack.close()

    gp_sb = cx.sb("gp", [128, 32])
    gc_f = cx.sb("gc", [128, 8], F32)
    gc_bf = cx.sb("gcb", [128, 8], BF16)
    g1_bf = [cx.sb("g1", [128, 1], BF16) for _ in range(4)]
    g2_bf = [cx.sb("g2", [128, 1], BF16) for _ in range(2)]
    t_gp = pl.emit("sync", lambda eng: eng.dma_start(out=gp_sb[:], in_=gp_all), dma=True)
    t_gs = pl.emit(
        "vector",
        lambda eng: eng.tensor_reduce(
            out=gc_f[:, :],
            in_=gp_sb[:, :].rearrange("p (g e) -> p e g", g=4),
            axis=AX.X, op=ALU.add),
        after=[t_gp])
    t_gs = pl.emit(
        "vector",
        lambda eng: eng.tensor_scalar_mul(out=gc_f[:, :], in0=gc_f[:, :],
                                          scalar1=1.0 / 16384.0),
        after=[t_gs], strict_after=[t_gs])
    t_gc = pl.emit("vector", lambda eng: eng.tensor_copy(out=gc_bf[:, :], in_=gc_f[:, :]),
                   after=[t_gs], strict_after=[t_gs])
    t_g1 = simple_linear(cx, gw1_sb, NCH, 512, [gc_bf[:, c:c + 1] for c in range(NCH)],
                         1, g1_bf, after=[t_gc, t5] + tok_c, psA=psA,
                         bias_col=gb1, act=AF.Gelu)
    t_g2 = simple_linear(cx, gw2_sb, 4, 256, g1_bf, 1, g2_bf,
                         after=t_g1 + [t6], psA=psA, bias_col=gb2, act=AF.Gelu)
    def gf_mm(eng):
        return [eng.matmul(psA[0][0:1, 0:1], gw3_sb[:, k * 1:k * 1 + 1],
                           g2_bf[k][:, 0:1], start=(k == 0), stop=(k == 1))
                for k in range(2)][-1]
    t_g3 = pl.emit("tensor", gf_mm, after=last(t_g2) + cx.bank_wait(psA[0]))
    t_gf = pl.emit(
        "vector",
        lambda eng: eng.tensor_scalar_add(out=sc_sb[0:1, 1:2], in0=psA[0][0:1, 0:1],
                                          scalar1=cb4_sb[0:1, 1:2]),
        after=[t_g3])
    cx.bank_set(psA[0], t_gf)
    t_gf = pl.emit(
        "scalar",
        lambda eng: eng.activation(out=sc_sb[0:1, 1:2], in_=sc_sb[0:1, 1:2],
                                   func=AF.Sigmoid),
        after=[t_gf], strict_after=[t_gf])
    t_a = pl.emit("vector", lambda eng: eng.tensor_scalar_mul(
        out=sc_sb[0:1, 2:3], in0=sc_sb[0:1, 0:1], scalar1=0.6),
        after=[t_gf] + tok_c)
    t_b = pl.emit("vector", lambda eng: eng.tensor_mul(
        out=sc_sb[0:1, 4:5], in0=sc_sb[0:1, 0:1], in1=sc_sb[0:1, 1:2]),
        after=[t_gf] + tok_c)
    t_b = pl.emit("vector", lambda eng: eng.scalar_tensor_tensor(
        out=sc_sb[0:1, 3:4], in0=sc_sb[0:1, 4:5], scalar=1.4,
        in1=sc_sb[0:1, 0:1], op0=ALU.mult, op1=ALU.subtract),
        after=[t_b], strict_after=[t_b])
    t_b = pl.emit("vector", lambda eng: eng.tensor_scalar_add(
        out=sc_sb[0:1, 3:4], in0=sc_sb[0:1, 3:4], scalar1=0.1),
        after=[t_b], strict_after=[t_b])

    ph = ExitStack()
    bar = pl.barrier()
    x1_t = [cx.sb("x1t", [128, Tn], F32, ph, side="right") for _ in range(NCH)]
    outA = [cx.sb("oa", [128, Tn], F32, ph, side="right") for _ in range(NCH)]
    outB = [cx.sb("ob", [128, Tn], F32, ph, side="right") for _ in range(NCH)]
    out_bf = [cx.sb("obf", [128, Tn], BF16, ph, side="right") for _ in range(NCH)]
    h1_bf = [cx.sb("h1", [128, Tn], BF16, ph, side="right") for _ in range(NCH)]
    h2_bf = [cx.sb("h2", [128, Tn], BF16, ph, side="right") for _ in range(4)]
    h3_bf = [cx.sb("h3", [128, Tn], BF16, ph, side="right") for _ in range(2)]
    x2_t = [cx.sb("x2t", [128, Tn], F32, ph, side="right") for _ in range(NCH)]
    ln3_t = [cx.sb("l3t", [128, Tn], BF16, ph, side="right") for _ in range(NCH)]
    tw_s = cx.sb("tws", [128, Tn], F32, ph, side="right")

    TW_SRC = [4, 5, 6, 7, 0, 1, 2, 3]
    TW_SGN = [+1, +1, -1, -1, -1, -1, +1, +1]

    prev_tile = last(bar + wtoks + [t_a, t_b])
    store_toks = []
    for ti in range(QT // Tn):
        t0 = ti * Tn
        t_x = [pl.emit("sync",
                       lambda eng, c=c, t0=t0: eng.dma_start(
                           out=x1_t[c][:, :], in_=x1q[128 * c:128 * (c + 1), t0:t0 + Tn]),
                       after=prev_tile, dma=True, tag="dma4")
               for c in range(NCH)]
        t_ln = emit_ln(cx, x1_t, Tn, dst_f32=outA, dst_bf=out_bf,
                       after=t_x + tok_c, psum2=ps_ln, ones_col=ones_col,
                       ones_row=ones_row, scratch=lns)
        cur, nxt = outA, outB
        t_prev = t_ln
        for cyc in range(3):
            t_h1 = simple_linear(cx, cnw1_sb, NCH, D, out_bf, Tn, h1_bf,
                                 after=t_prev + [t1], psA=psA, bias_col=cb1,
                                 act=AF.Gelu)
            t_h2 = simple_linear(cx, cnw2_sb, NCH, 512, h1_bf, Tn, h2_bf,
                                 after=t_h1 + [t2], psA=psA, bias_col=cb2,
                                 act=AF.Gelu)
            t_h3 = simple_linear(cx, cnw3_sb, 4, 256, h2_bf, Tn, h3_bf,
                                 after=t_h2 + [t3], psA=psA, bias_col=cb3,
                                 act=AF.Gelu)
            def tf_mm(eng):
                return [eng.matmul(psA[0][0:1, 0:Tn], cnw4_sb[:, k:k + 1],
                                   h3_bf[k][:, 0:Tn], start=(k == 0), stop=(k == 1))
                        for k in range(2)][-1]
            t_tf = pl.emit("tensor", tf_mm, after=last(t_h3) + cx.bank_wait(psA[0]))
            t_tf2 = pl.emit("vector", lambda eng: eng.tensor_scalar_add(
                out=rowbuf[0:1, 0:Tn], in0=psA[0][0:1, 0:Tn],
                scalar1=cb4_sb[0:1, 0:1]), after=[t_tf])
            cx.bank_set(psA[0], t_tf2)
            t_tf3 = pl.emit("scalar", lambda eng: eng.activation(
                out=rowbuf[0:1, 0:Tn], in_=rowbuf[0:1, 0:Tn], func=AF.Sigmoid),
                after=[t_tf2], strict_after=[t_tf2])
            t_cp = pl.emit("vector", lambda eng: eng.tensor_scalar(
                out=rowbuf[0:1, Tn:2 * Tn], in0=rowbuf[0:1, 0:Tn],
                scalar1=sc_sb[0:1, 2:3], scalar2=sc_sb[0:1, 3:4],
                op0=ALU.mult, op1=ALU.add),
                after=[t_tf3, t_a, t_b], strict_after=[t_tf3])
            t_bp = pl.emit("tensor", lambda eng: eng.matmul(
                psB[0][:, 0:Tn], ones_row[0:1, :], rowbuf[0:1, Tn:2 * Tn],
                start=True, stop=True),
                after=[t_cp] + cx.bank_wait(psB[0]))
            t_bn = pl.emit("tensor", lambda eng: eng.matmul(
                psB[1][:, 0:Tn], neg_row[0:1, :], rowbuf[0:1, Tn:2 * Tn],
                start=True, stop=True),
                after=[t_cp] + cx.bank_wait(psB[1]))
            t_tw = []
            for c in range(NCH):
                psx = psB[0] if TW_SGN[c] > 0 else psB[1]
                srcb = cur[TW_SRC[c]]; curb = cur[c]; nxtb = nxt[c]
                ta = pl.emit("vector", lambda eng, srcb=srcb, psx=psx: eng.tensor_mul(
                    out=tw_s[:, 0:Tn], in0=srcb[:, 0:Tn],
                    in1=psx[:, 0:Tn]),
                    after=[t_bp, t_bn] + t_prev)
                tb = pl.emit("vector", lambda eng, curb=curb, nxtb=nxtb: eng.tensor_add(
                    out=nxtb[:, 0:Tn], in0=curb[:, 0:Tn], in1=tw_s[:, 0:Tn]),
                    after=[ta], strict_after=[ta])
                if cyc < 2:
                    tb = pl.emit("vector", lambda eng, c=c, nxtb=nxtb: eng.tensor_copy(
                        out=out_bf[c][:, :], in_=nxtb[:, 0:Tn]),
                        after=[tb], strict_after=[tb])
                t_tw.append(tb)
            cx.bank_set(psB[0], t_tw[-1]); cx.bank_set(psB[1], t_tw[-1])
            cur, nxt = nxt, cur
            t_prev = t_tw
        t_lnp = emit_ln(cx, cur, Tn, dst_f32=nxt, after=t_prev,
                        psum2=ps_ln, ones_col=ones_col, ones_row=ones_row,
                        scratch=lns)
        t_res = [pl.emit("vector", lambda eng, c=c: eng.tensor_add(
            out=x2_t[c][:, :], in0=x1_t[c][:, :], in1=nxt[c][:, 0:Tn]),
            after=t_lnp + t_x) for c in range(NCH)]
        t_ln3 = emit_ln(cx, x2_t, Tn, dst_bf=ln3_t, after=t_res,
                        psum2=ps_ln, ones_col=ones_col, ones_row=ones_row,
                        scratch=lns)
        t_st = [pl.emit("gpsimd", lambda eng, c=c, t0=t0: eng.dma_start(
            out=x2_out[128 * c:128 * (c + 1), t0:t0 + Tn], in_=x2_t[c][:, :]),
            after=t_res[c:c + 1], dma=True, tag="dma4") for c in range(NCH)]
        t_st += [pl.emit("gpsimd", lambda eng, c=c, t0=t0: eng.dma_start(
            out=ln3_out[128 * c:128 * (c + 1), t0:t0 + Tn], in_=ln3_t[c][:, :]),
            after=t_ln3, dma=True, tag="dma4") for c in range(NCH)]
        store_toks += t_st
        prev_tile = last(t_res + t_ln3 + t_st)
    pl.emit("gpsimd", lambda eng: eng.engine_nop(), after=last(store_toks))
    ph.close()
    _stg.close()
    cx.default_stack = _prev_stack
    cx.io_override = {}
    if standalone:
        return cx.finish()
    return {"x2": x2_out, "ln3": ln3_out}


def build_launch_C(cx=None, io=None, **kw):
    standalone = cx is None
    if standalone:
        cx = Ctx("C")
    _stg = ExitStack()
    _prev_stack = cx.default_stack
    cx.default_stack = _stg
    cx.io_override = io or {}
    pl = cx.pl
    Tn = 512
    NT = S // Tn

    ln3_blocks = kw.get("ln3_blocks", False)
    if ln3_blocks:
        ln3f = cx.inp("ln3f", [4 * D, QT], BF16)
    else:
        ln3f = cx.inp("ln3f", [D, S], BF16)
    winx_d = cx.inp("winxT", [D, 512])
    winz_d = cx.inp("winzT", [D, 512])
    convw_d = cx.inp("convw_c", [128, 16])
    convb_d = cx.inp("convb_c", [128, 4])
    wx_d = cx.inp("wxT", [512, 96])

    xsil_out = cx.outp("xsil", [512, S])
    zsil_out = cx.outp("zsil", [512, S])
    xp_out = cx.outp("xp", [96, S])

    convw = cx.sb("convw", [128, 16]); convb = cx.sb("convb", [128, 4])
    tok_c = [
        pl.emit("sync", lambda eng: eng.dma_start(out=convw[:], in_=convw_d), dma=True),
        pl.emit("sync", lambda eng: eng.dma_start(out=convb[:], in_=convb_d), dma=True),
    ]
    wst = ExitStack()
    winx_sb, tx = load_weight_ph(cx, winx_d, D, 512, "winx", wst, wsb_stack=None)
    winz_sb, tz = load_weight_ph(cx, winz_d, D, 512, "winz", wst, wsb_stack=None)
    wx_sb, twx = load_weight_ph(cx, wx_d, 512, 96, "wx", wst, wsb_stack=None)
    wst.close()

    psA = [cx.ps("psA", [128, 512], F32) for _ in range(2)]
    psZ = [cx.ps("psZ", [128, 512], F32) for _ in range(2)]
    psP = cx.ps("psP", [96, 512], F32)

    ph = ExitStack()
    bar = pl.barrier()
    u_t = [cx.sb("ut", [128, Tn], BF16, ph, side="right") for _ in range(NCH)]
    xin_ext = [cx.sb("xe", [128, Tn + 3], F32, ph, side="right") for _ in range(4)]
    tail = cx.sb("tail", [128, 4 * 3], F32)
    xcv = [cx.sb("xc", [128, Tn], F32, ph, side="right") for _ in range(4)]
    xsb = [cx.sb("xsb", [128, Tn], BF16, ph, side="right") for _ in range(4)]
    zs = [cx.sb("zs", [128, Tn], F32, ph, side="right") for _ in range(4)]
    xpt = cx.sb("xpt", [96, Tn], F32, ph, side="right")

    t_tz = pl.emit("vector", lambda eng: eng.memset(tail[:, :], 0.0))
    prev_tile = last(bar + [tx, tz, twx, t_tz] + tok_c)
    store_toks = []
    for ti in range(NT):
        t0 = ti * Tn
        if ln3_blocks:
            qq, st = ti // 8, (ti % 8) * Tn
            t_u = [pl.emit("sync", lambda eng, c=c, qq=qq, st=st: eng.dma_start(
                out=u_t[c][:, :],
                in_=ln3f[qq * D + 128 * c:qq * D + 128 * (c + 1), st:st + Tn]),
                after=prev_tile, dma=True, tag="dma4") for c in range(NCH)]
        else:
            t_u = [pl.emit("sync", lambda eng, c=c, t0=t0: eng.dma_start(
                out=u_t[c][:, :], in_=ln3f[128 * c:128 * (c + 1), t0:t0 + Tn]),
                after=prev_tile, dma=True, tag="dma4") for c in range(NCH)]
        def xwrite(of, o0, osz, nt, tsz, ps, pe_tok):
            t1 = pl.emit("vector", lambda eng, of=of: eng.tensor_copy(
                out=xin_ext[of][:, 3:3 + Tn], in_=ps[:, 0:Tn]), after=[pe_tok])
            return t1
        t_xin = emit_linear(cx, winx_sb, NCH, 512, u_t, Tn, xwrite,
                            after=t_u + [tx], psA=psA)
        t_tl = [pl.emit("vector", lambda eng, c4=c4: eng.tensor_copy(
            out=xin_ext[c4][:, 0:3], in_=tail[:, 3 * c4:3 * c4 + 3]),
            after=prev_tile + t_u) for c4 in range(4)]
        t_z = simple_linear(cx, winz_sb, NCH, 512, u_t, Tn, zs,
                            after=t_u + [tz], psA=psZ, act=AF.Silu)
        t_stz = [pl.emit("gpsimd", lambda eng, c4=c4, t0=t0: eng.dma_start(
            out=zsil_out[128 * c4:128 * (c4 + 1), t0:t0 + Tn], in_=zs[c4][:, :]),
            after=t_z, dma=True, tag="dma4") for c4 in range(4)]
        t_cv = []
        for c4 in range(4):
            tt = pl.emit("vector", lambda eng, c4=c4: eng.tensor_scalar_mul(
                out=xcv[c4][:, :], in0=xin_ext[c4][:, 0:Tn],
                scalar1=convw[:, 4 * c4:4 * c4 + 1]),
                after=t_xin + t_tl)
            for k in range(1, 4):
                tt = pl.emit("vector", lambda eng, c4=c4, k=k: eng.scalar_tensor_tensor(
                    out=xcv[c4][:, :], in0=xin_ext[c4][:, k:Tn + k],
                    scalar=convw[:, 4 * c4 + k:4 * c4 + k + 1],
                    in1=xcv[c4][:, :], op0=ALU.mult, op1=ALU.add),
                    after=[tt], strict_after=[tt])
            tt = pl.emit("scalar", lambda eng, c4=c4: eng.activation(
                out=xcv[c4][:, :], in_=xcv[c4][:, :], func=AF.Silu,
                bias=convb[:, c4:c4 + 1]),
                after=[tt], strict_after=[tt])
            tt2 = pl.emit("vector", lambda eng, c4=c4: eng.tensor_copy(
                out=xsb[c4][:, :], in_=xcv[c4][:, :]),
                after=[tt], strict_after=[tt])
            t_cv.append(tt2)
        t_tl2 = [pl.emit("vector", lambda eng, c4=c4: eng.tensor_copy(
            out=tail[:, 3 * c4:3 * c4 + 3], in_=xin_ext[c4][:, Tn:Tn + 3]),
            after=t_cv + t_xin) for c4 in range(4)]
        t_sx = [pl.emit("gpsimd", lambda eng, c4=c4, t0=t0: eng.dma_start(
            out=xsil_out[128 * c4:128 * (c4 + 1), t0:t0 + Tn], in_=xcv[c4][:, :]),
            after=t_cv, dma=True, tag="dma4") for c4 in range(4)]
        def xp_mm(eng):
            return [eng.matmul(psP[:, 0:Tn], wx_sb[:, k * 96:(k + 1) * 96],
                               xsb[k][:, :], start=(k == 0), stop=(k == 3))
                    for k in range(4)][-1]
        t_xp = pl.emit("tensor", xp_mm, after=last(t_cv + [twx]) + cx.bank_wait(psP))
        t_xpe = pl.emit("vector", lambda eng: eng.tensor_copy(
            out=xpt[:, :], in_=psP[:, 0:Tn]), after=[t_xp])
        cx.bank_set(psP, t_xpe)
        t_sp = pl.emit("gpsimd", lambda eng, t0=t0: eng.dma_start(
            out=xp_out[:, t0:t0 + Tn], in_=xpt[:, :]),
            after=[t_xpe], dma=True, tag="dma4")
        store_toks += t_sx + t_stz + [t_sp]
        prev_tile = last(t_cv + t_tl2 + [t_xpe, t_sp] + t_sx + t_stz)
    pl.emit("gpsimd", lambda eng: eng.engine_nop(), after=last(store_toks))
    ph.close()
    _stg.close()
    cx.default_stack = _prev_stack
    cx.io_override = {}
    if standalone:
        return cx.finish()
    return {"xsil": xsil_out, "zsil": zsil_out, "xp": xp_out}


def build_launch_D(cx=None, io=None, **kw):
    standalone = cx is None
    if standalone:
        cx = Ctx("D")
    _stg = ExitStack()
    _prev_stack = cx.default_stack
    cx.default_stack = _stg
    cx.io_override = io or {}
    pl = cx.pl
    TC = 1024
    NCK = S // TC
    NSUB = TC // 512

    presummed = kw.get("presummed", False)
    xsil_d = cx.inp("xsil", [512, S])
    zsil_d = cx.inp("zsil", [512, S])
    if presummed:
        xp_d = [cx.inp("xp0", [96, S])]
    else:
        xp_d = [cx.inp(f"xp{i}", [96, S]) for i in range(4)]
    wdt_d = cx.inp("wdtT", [64, 512])
    bdt_d = cx.inp("bdt_c", [128, 4])
    alog_d = cx.inp("alog_c", [128, 64])
    dp_d = cx.inp("dp_c", [128, 4])
    onesc_d = cx.inp("ones_col", [128, 1])
    onesr_d = cx.inp("ones_row", [1, 128])
    sel_d = cx.inp("sel16", [16, 2048])

    yg_out = cx.outp("ygated", [2048, QT], BF16)

    ones_col = cx.sb("onesc", [128, 1]); ones_row = cx.sb("onesr", [1, 128])
    bdt = cx.sb("bdt", [128, 4]); acol = cx.sb("acol", [128, 64])
    dpc = cx.sb("dpc", [128, 4])
    state = cx.sb("state", [128, 64], F32)
    sel16 = cx.sb("sel16", [16, 2048], F32)
    tok_c = [
        pl.emit("sync", lambda eng: eng.dma_start(out=sel16[:], in_=sel_d), dma=True),
        pl.emit("sync", lambda eng: eng.dma_start(out=ones_col[:], in_=onesc_d), dma=True),
        pl.emit("sync", lambda eng: eng.dma_start(out=ones_row[:], in_=onesr_d), dma=True),
        pl.emit("sync", lambda eng: eng.dma_start(out=bdt[:], in_=bdt_d), dma=True),
        pl.emit("sync", lambda eng: eng.dma_start(out=acol[:], in_=alog_d), dma=True),
        pl.emit("sync", lambda eng: eng.dma_start(out=dpc[:], in_=dp_d), dma=True),
    ]
    t_ac = pl.emit("scalar", lambda eng: eng.activation(
        out=acol[:, :], in_=acol[:, :], func=AF.Exp), after=tok_c)
    t_ac = pl.emit("vector", lambda eng: eng.tensor_scalar_mul(
        out=acol[:, :], in0=acol[:, :], scalar1=-1.0),
        after=[t_ac], strict_after=[t_ac])
    t_sz = pl.emit("vector", lambda eng: eng.memset(state[:, :], 0.0))

    wst = ExitStack()
    wdt_sb, twdt = load_weight_ph(cx, wdt_d, 64, 512, "wdt", wst, wsb_stack=None)
    wst.close()

    psD = [cx.ps("psD", [128, 512], F32) for _ in range(2)]
    psB = cx.ps("psBb", [128, 1024], F32)
    psC = cx.ps("psCb", [128, 1024], F32)

    ph = ExitStack()
    bar = pl.barrier()
    xs_t = [cx.sb("xs", [128, TC], F32, ph, side="right") for _ in range(4)]
    dt_t = [cx.sb("dt", [128, TC], F32, ph, side="right") for _ in range(4)]
    w_t = [cx.sb("w", [128, TC], F32, ph, side="right") for _ in range(4)]
    y_t = [cx.sb("y", [128, TC], F32, ph, side="right") for _ in range(4)]
    dA_t = [cx.sb("dA", [128, TC], F32, ph, side="right") for _ in range(4)]
    b_t = [cx.sb("b", [128, TC], F32, ph, side="right") for _ in range(4)]
    h_t = [cx.sb("h", [128, TC], F32, ph, side="right") for _ in range(4)]
    y2_t = [cx.sb("y2", [128, TC], F32, ph, side="right") for _ in range(4)]
    Bsb = [cx.sb("Bsb", [128, TC], F32, ph, side="right") for _ in range(2)]
    Csb = [cx.sb("Csb", [128, TC], F32, ph, side="right") for _ in range(2)]
    sp_s = cx.sb("sps", [128, TC], F32, ph, side="right")
    din = cx.sb("din", [64, TC], F32, ph, side="right")
    din_bf = cx.sb("dinb", [64, TC], BF16, ph, side="right")
    brow = cx.sb("brow", [16, TC], F32, ph, side="right")
    crow = cx.sb("crow", [16, TC], F32, ph, side="right")
    dpart = cx.sb("dpart", [64, TC], F32, ph, side="right")
    bpart = cx.sb("bpart", [16, TC], F32, ph, side="right")
    cpart = cx.sb("cpart", [16, TC], F32, ph, side="right")
    zrow = cx.sb("zrow", [128, 512], F32, ph, side="right")
    ygb = cx.sb("ygb", [128, TC], BF16, ph, side="right")

    prev = last(bar + [t_ac, t_sz, twdt])
    store_toks = []
    for ck in range(NCK):
        t0 = ck * TC
        t_x = [pl.emit("sync", lambda eng, c4=c4, t0=t0: eng.dma_start(
            out=xs_t[c4][:, :], in_=xsil_d[128 * c4:128 * (c4 + 1), t0:t0 + TC]),
            after=prev, dma=True, tag="dma4") for c4 in range(4)]
        if presummed:
            t_acc = [
                pl.emit("sync", lambda eng, t0=t0: eng.dma_start(
                    out=din[0:64, :], in_=xp_d[0][0:64, t0:t0 + TC]),
                    after=prev, dma=True, tag="dma4"),
                pl.emit("sync", lambda eng, t0=t0: eng.dma_start(
                    out=brow[0:16, :], in_=xp_d[0][64:80, t0:t0 + TC]),
                    after=prev, dma=True, tag="dma4"),
                pl.emit("sync", lambda eng, t0=t0: eng.dma_start(
                    out=crow[0:16, :], in_=xp_d[0][80:96, t0:t0 + TC]),
                    after=prev, dma=True, tag="dma4"),
            ]
        else:
          t_acc = []
          for i in range(4):
            td = pl.emit("sync", lambda eng, i=i, t0=t0: eng.dma_start(
                out=dpart[0:64, :], in_=xp_d[i][0:64, t0:t0 + TC]),
                after=prev + t_acc, dma=True, tag="dma4")
            tb = pl.emit("sync", lambda eng, i=i, t0=t0: eng.dma_start(
                out=bpart[0:16, :], in_=xp_d[i][64:80, t0:t0 + TC]),
                after=prev + t_acc, dma=True, tag="dma4")
            tc2 = pl.emit("sync", lambda eng, i=i, t0=t0: eng.dma_start(
                out=cpart[0:16, :], in_=xp_d[i][80:96, t0:t0 + TC]),
                after=prev + t_acc, dma=True, tag="dma4")
            if i == 0:
                t_acc = [
                    pl.emit("vector", lambda eng: eng.tensor_copy(out=din[:, :], in_=dpart[:, :]), after=[td]),
                    pl.emit("vector", lambda eng: eng.tensor_copy(out=brow[:, :], in_=bpart[:, :]), after=[tb]),
                    pl.emit("vector", lambda eng: eng.tensor_copy(out=crow[:, :], in_=cpart[:, :]), after=[tc2]),
                ]
            else:
                t_acc = [
                    pl.emit("vector", lambda eng: eng.tensor_add(out=din[:, :], in0=din[:, :], in1=dpart[:, :]), after=[td], strict_after=t_acc),
                    pl.emit("vector", lambda eng: eng.tensor_add(out=brow[:, :], in0=brow[:, :], in1=bpart[:, :]), after=[tb], strict_after=t_acc),
                    pl.emit("vector", lambda eng: eng.tensor_add(out=crow[:, :], in0=crow[:, :], in1=cpart[:, :]), after=[tc2], strict_after=t_acc),
                ]
        t_dinb = pl.emit("gpsimd", lambda eng: eng.tensor_copy(
            out=din_bf[:, :], in_=din[:, :]), after=t_acc)
        t_dt = []
        for of in range(4):
            ps = psD[of % 2]
            toks_of = []
            for sb in range(NSUB):
                def mm2(eng, of=of, ps=ps, sb=sb):
                    return eng.matmul(ps[:, 0:512],
                                      wdt_sb[0:64, 128 * of:128 * of + 128],
                                      din_bf[0:64, 512 * sb:512 * (sb + 1)],
                                      start=True, stop=True)
                pe = pl.emit("tensor", mm2,
                             after=last(t_acc + [twdt, t_dinb]) + cx.bank_wait(ps))
                tv = pl.emit("vector", lambda eng, of=of, sb=sb, ps=ps: eng.tensor_scalar_add(
                    out=dt_t[of][:, 512 * sb:512 * (sb + 1)], in0=ps[:, 0:512],
                    scalar1=bdt[:, of:of + 1]), after=[pe])
                cx.bank_set(ps, tv)
                toks_of.append(tv)
            tm = pl.emit("vector", lambda eng, of=of: eng.tensor_scalar_max(
                out=w_t[of][:, :], in0=dt_t[of][:, :], scalar1=0.0),
                after=toks_of, strict_after=toks_of)
            tab = pl.emit("scalar", lambda eng, of=of: eng.activation(
                out=sp_s[:, :], in_=dt_t[of][:, :], func=AF.Abs),
                after=toks_of)
            te = pl.emit("scalar", lambda eng: eng.activation(
                out=sp_s[:, :], in_=sp_s[:, :], func=AF.Exp, scale=-1.0),
                after=[tab], strict_after=[tab])
            tl = pl.emit("scalar", lambda eng: eng.activation(
                out=sp_s[:, :], in_=sp_s[:, :], func=AF.Ln,
                bias=ones_col[:, 0:1]),
                after=[te] + tok_c, strict_after=[te])
            tdt = pl.emit("vector", lambda eng, of=of: eng.tensor_add(
                out=dt_t[of][:, :], in0=w_t[of][:, :], in1=sp_s[:, :]),
                after=[tl, tm], strict_after=[tl, tm])
            t_dt.append(tdt)
        t_w = [pl.emit("vector", lambda eng, c4=c4: eng.tensor_mul(
            out=w_t[c4][:, :], in0=dt_t[c4][:, :], in1=xs_t[c4][:, :]),
            after=[t_dt[c4]] + t_x, strict_after=[t_dt[c4]]) for c4 in range(4)]
        t_y0 = [pl.emit("vector", lambda eng, c4=c4: eng.memset(y_t[c4][:, :], 0.0),
                        after=prev) for c4 in range(4)]

        y_toks = list(t_y0)
        slot_free = {}
        state_tok = {}
        ylast = {}
        bc_free = {}
        bc_toks = {}
        seq = [(n, c4) for n in range(16) for c4 in range(4)]
        pend = {}

        def emit_bcast(n):
            par = n % 2
            t_bb = pl.emit("tensor", lambda eng, n=n: [
                eng.matmul(psB[:, 512 * hh:512 * (hh + 1)],
                           sel16[0:16, 128 * n:128 * (n + 1)],
                           brow[0:16, 512 * hh:512 * (hh + 1)],
                           start=True, stop=True) for hh in range(NSUB)][-1],
                after=last(t_acc) + cx.bank_wait(psB))
            t_be = pl.emit("scalar", lambda eng, par=par: eng.copy(
                out=Bsb[par][:, :], in_=psB[:, 0:TC]),
                after=[t_bb] + bc_free.get(par, []))
            cx.bank_set(psB, t_be)
            t_cb = pl.emit("tensor", lambda eng, n=n: [
                eng.matmul(psC[:, 512 * hh:512 * (hh + 1)],
                           sel16[0:16, 128 * n:128 * (n + 1)],
                           crow[0:16, 512 * hh:512 * (hh + 1)],
                           start=True, stop=True) for hh in range(NSUB)][-1],
                after=last(t_acc) + cx.bank_wait(psC))
            t_ce = pl.emit("scalar", lambda eng, par=par: eng.copy(
                out=Csb[par][:, :], in_=psC[:, 0:TC]),
                after=[t_cb] + bc_free.get(par, []))
            cx.bank_set(psC, t_ce)
            bc_toks[n] = (t_be, t_ce)

        def emit_head(i):
            n, c4 = seq[i]
            if c4 == 0:
                emit_bcast(n)
            s = i % 4
            par = n % 2
            sidx = 16 * c4 + n
            freed = slot_free.get(s, [])
            t_be, t_ce = bc_toks[n]
            t_da = pl.emit("scalar", lambda eng, c4=c4, sidx=sidx, s=s: eng.activation(
                out=dA_t[s][:, :], in_=dt_t[c4][:, :], func=AF.Exp,
                scale=acol[:, sidx:sidx + 1]),
                after=[t_dt[c4], t_ac] + freed)
            t_b = pl.emit("gpsimd", lambda eng, c4=c4, s=s, par=par: eng.tensor_mul(
                out=b_t[s][:, :], in0=w_t[c4][:, :], in1=Bsb[par][:, :]),
                after=[t_w[c4], t_be] + freed)
            pend[i] = (t_da, t_b, s, n, c4)

        def emit_tail(i):
            t_da, t_b, s, n, c4 = pend.pop(i)
            par = n % 2
            sidx = 16 * c4 + n
            t_ce = bc_toks[n][1]
            t_h = pl.emit("vector", lambda eng, sidx=sidx, s=s: eng.tensor_tensor_scan(
                out=h_t[s][:, :], data0=dA_t[s][:, :], data1=b_t[s][:, :],
                initial=state[:, sidx:sidx + 1],
                op0=ALU.mult, op1=ALU.add),
                after=[t_da, t_b, t_sz] + ([state_tok[sidx]] if sidx in state_tok else []))
            t_su = pl.emit("scalar", lambda eng, sidx=sidx, s=s: eng.copy(
                out=state[:, sidx:sidx + 1], in_=h_t[s][:, TC - 1:TC]),
                after=[t_h])
            state_tok[sidx] = t_su
            t_hc = pl.emit("vector", lambda eng, s=s, par=par: eng.tensor_mul(
                out=b_t[s][:, :], in0=h_t[s][:, :], in1=Csb[par][:, :]),
                after=[t_h, t_ce], strict_after=[t_h])
            t_ya = pl.emit("vector", lambda eng, c4=c4, s=s: eng.tensor_add(
                out=y_t[c4][:, :], in0=y_t[c4][:, :], in1=b_t[s][:, :]),
                after=[t_hc] + t_y0 + ([ylast[c4]] if c4 in ylast else []),
                strict_after=[t_hc])
            ylast[c4] = t_ya
            y_toks.append(t_ya)
            slot_free[s] = [t_ya, t_su]
            if c4 == 3:
                bc_free[par] = [t_b, t_hc]

        for i in range(len(seq) + 2):
            if i < len(seq):
                emit_head(i)
            if i >= 2:
                emit_tail(i - 2)
        for c4 in range(4):
            wy = [ylast[c4]]
            t_yd = pl.emit("vector", lambda eng, c4=c4: eng.scalar_tensor_tensor(
                out=y_t[c4][:, :], in0=xs_t[c4][:, :], scalar=dpc[:, c4:c4 + 1],
                in1=y_t[c4][:, :], op0=ALU.mult, op1=ALU.add),
                after=wy, strict_after=wy)
            tg_all = []
            for sb in range(NSUB):
                t_zl = pl.emit("sync", lambda eng, c4=c4, sb=sb, t0=t0: eng.dma_start(
                    out=zrow[:, :], in_=zsil_d[128 * c4:128 * (c4 + 1),
                                              t0 + 512 * sb:t0 + 512 * (sb + 1)]),
                    after=[t_yd] + tg_all, dma=True, tag="dma4")
                t_g = pl.emit("vector", lambda eng, c4=c4, sb=sb: eng.tensor_mul(
                    out=ygb[:, 512 * sb:512 * (sb + 1)],
                    in0=y_t[c4][:, 512 * sb:512 * (sb + 1)], in1=zrow[:, :]),
                    after=[t_zl, t_yd])
                tg_all.append(t_g)
            r0 = 512 * (t0 // QT) + 128 * c4
            cc0 = t0 % QT
            t_sy = pl.emit("gpsimd", lambda eng, r0=r0, cc0=cc0: eng.dma_start(
                out=yg_out[r0:r0 + 128, cc0:cc0 + TC], in_=ygb[:, :]),
                after=tg_all, dma=True, tag="dma4")
            store_toks.append(t_sy)
            y_toks.append(t_sy)
        prev = last(y_toks + t_w + t_dt)
    pl.emit("gpsimd", lambda eng: eng.engine_nop(), after=last(store_toks))
    ph.close()
    _stg.close()
    cx.default_stack = _prev_stack
    cx.io_override = {}
    if standalone:
        return cx.finish()
    return {"ygated": yg_out}


def build_launch_E(cx=None, io=None, **kw):
    standalone = cx is None
    if standalone:
        cx = Ctx("E")
    _stg = ExitStack()
    _prev_stack = cx.default_stack
    cx.default_stack = _stg
    cx.io_override = io or {}
    pl = cx.pl
    Tn = 512

    yg_d = cx.inp("ygq", [2048, QT], BF16)
    x2q = cx.inp("x2q", [D, QT])
    woutT_d = cx.inp("woutT", [2048, D])
    onesc_d = cx.inp("ones_col", [128, 1]); onesr_d = cx.inp("ones_row", [1, 128])

    x3_out = cx.outp("x3", [D, QT])
    xl_out = cx.outp("xl", [D, QT])

    ones_col = cx.sb("onesc", [128, 1]); ones_row = cx.sb("onesr", [1, 128])
    lns = ln_scratch(cx, Tn=512)
    tok_c = [
        pl.emit("sync", lambda eng: eng.dma_start(out=ones_col[:], in_=onesc_d), dma=True),
        pl.emit("sync", lambda eng: eng.dma_start(out=ones_row[:], in_=onesr_d), dma=True),
    ]
    wst = ExitStack()
    wout_sb, tw = load_weight_ph(cx, woutT_d, 2048, D, "wout", wst, wsb_stack=None)
    wst.close()

    psA = [cx.ps("psA", [128, 512], F32) for _ in range(2)]
    ps_ln = [cx.ps("psStat", [64, 512], F32),
             cx.ps("psMu", [128, 512], F32), cx.ps("psRs", [128, 512], F32)]

    ph = ExitStack()
    bar = pl.barrier()
    yg_t = [cx.sb("yg", [128, Tn], BF16, ph, side="right") for _ in range(16)]
    o_f = [cx.sb("of", [128, Tn], F32, ph, side="right") for _ in range(NCH)]
    lnp = [cx.sb("lnp", [128, Tn], F32, ph, side="right") for _ in range(NCH)]
    x2_t = [cx.sb("x2t", [128, Tn], F32, ph, side="right") for _ in range(NCH)]
    x3_t = [cx.sb("x3t", [128, Tn], F32, ph, side="right") for _ in range(NCH)]
    xl_t = [cx.sb("xlt", [128, Tn], F32, ph, side="right") for _ in range(NCH)]

    prev = last(bar + [tw] + tok_c)
    stores = []
    for ti in range(QT // Tn):
        t0 = ti * Tn
        t_y = [pl.emit("sync", lambda eng, k=k, t0=t0: eng.dma_start(
            out=yg_t[k][:, :], in_=yg_d[128 * k:128 * (k + 1), t0:t0 + Tn]),
            after=prev, dma=True, tag="dma4") for k in range(16)]
        t_x2 = [pl.emit("sync", lambda eng, c=c, t0=t0: eng.dma_start(
            out=x2_t[c][:, :], in_=x2q[128 * c:128 * (c + 1), t0:t0 + Tn]),
            after=prev, dma=True, tag="dma4") for c in range(NCH)]
        t_o = simple_linear(cx, wout_sb, 16, D, yg_t, Tn, o_f,
                            after=t_y + [tw], psA=psA)
        t_lnp = emit_ln(cx, o_f, Tn, dst_f32=lnp, after=t_o, psum2=ps_ln,
                        ones_col=ones_col, ones_row=ones_row, scratch=lns)
        t_res = [pl.emit("vector", lambda eng, c=c: eng.tensor_add(
            out=x3_t[c][:, :], in0=x2_t[c][:, :], in1=lnp[c][:, :]),
            after=t_lnp + t_x2) for c in range(NCH)]
        t_xl = emit_ln(cx, x3_t, Tn, dst_f32=xl_t, after=t_res, psum2=ps_ln,
                       ones_col=ones_col, ones_row=ones_row, scratch=lns)
        t_s = [pl.emit("gpsimd", lambda eng, c=c, t0=t0: eng.dma_start(
            out=x3_out[128 * c:128 * (c + 1), t0:t0 + Tn], in_=x3_t[c][:, :]),
            after=t_res[c:c + 1], dma=True, tag="dma4") for c in range(NCH)]
        t_s += [pl.emit("gpsimd", lambda eng, c=c, t0=t0: eng.dma_start(
            out=xl_out[128 * c:128 * (c + 1), t0:t0 + Tn], in_=xl_t[c][:, :]),
            after=t_xl, dma=True, tag="dma4") for c in range(NCH)]
        stores += t_s
        prev = last(t_res + t_xl + t_s)
    pl.emit("gpsimd", lambda eng: eng.engine_nop(), after=last(stores))
    ph.close()
    _stg.close()
    cx.default_stack = _prev_stack
    cx.io_override = {}
    if standalone:
        return cx.finish()
    return {"x3": x3_out, "xl": xl_out}


def build_launch_F(cx=None, io=None, **kw):
    standalone = cx is None
    if standalone:
        cx = Ctx("F")
    _stg = ExitStack()
    _prev_stack = cx.default_stack
    cx.default_stack = _stg
    cx.io_override = io or {}
    pl = cx.pl
    Tn = 512

    xq_d = cx.inp("xcur", [D, QT])
    pcol_d = cx.inp("pcol", [128, 8])
    wqkv_d = cx.inp("wqkv", [D, 1536])
    qs_d = cx.inp("qs_c", [128, 1])
    ks_d = cx.inp("ks_c", [128, 1])
    e8_d = cx.inp("e8", [128, 4 * 8])
    g8_d = cx.inp("g8", [8, 4 * 128])
    rot_d = cx.inp("rotm", [128, 128])
    csq_d = cx.inp("csq", [128, QT]); ssq_d = cx.inp("ssq", [128, QT])
    csk_d = cx.inp("csk", [128, QT]); ssk_d = cx.inp("ssk", [128, QT])

    q_out = cx.outp("q", [512, QT], BF16)
    k_out = cx.outp("k", [512, QT], BF16)
    vt_out = cx.outp("vt", [QT, 512], BF16)

    qs_c = cx.sb("qs", [128, 1]); ks_c = cx.sb("ks", [128, 1])
    e8 = cx.sb("e8", [128, 32]); g8 = cx.sb("g8", [8, 512])
    rotm = cx.sb("rotm", [128, 128], F32)
    pcol = cx.sb("pcol", [128, 8], F32)
    tok_c = [
        pl.emit("sync", lambda eng: eng.dma_start(out=qs_c[:], in_=qs_d), dma=True),
        pl.emit("sync", lambda eng: eng.dma_start(out=ks_c[:], in_=ks_d), dma=True),
        pl.emit("sync", lambda eng: eng.dma_start(out=e8[:], in_=e8_d), dma=True),
        pl.emit("sync", lambda eng: eng.dma_start(out=g8[:], in_=g8_d), dma=True),
        pl.emit("sync", lambda eng: eng.dma_start(out=rotm[:], in_=rot_d), dma=True),
        pl.emit("sync", lambda eng: eng.dma_start(
            out=pcol[:, :], in_=pcol_d), dma=True),
    ]
    t_qs8 = pl.emit("vector", lambda eng: eng.tensor_scalar_mul(
        out=qs_c[:, :], in0=qs_c[:, :], scalar1=8.0), after=tok_c)

    wst = ExitStack()
    wqkv_sb, twq = load_weight_ph(cx, wqkv_d, D, 1536, "wqkv", wst, wsb_stack=None)
    wst.close()

    psA = [cx.ps("psA", [128, 512], F32) for _ in range(2)]
    psS = cx.ps("psS", [64, 512], F32)
    psR = cx.ps("psR", [128, 512], F32)
    psRot = cx.ps("psRot", [128, 512], F32)

    ph = ExitStack()
    bar = pl.barrier()
    xw = [cx.sb("xw", [128, Tn + 1], F32, ph, side="right") for _ in range(NCH)]
    hs_bf = [cx.sb("hs", [128, Tn], BF16, ph, side="right") for _ in range(NCH)]
    q_f = [cx.sb("qf", [128, Tn], F32, ph, side="right") for _ in range(4)]
    k_f = [cx.sb("kf", [128, Tn], F32, ph, side="right") for _ in range(4)]
    sq_s = cx.sb("sqs", [128, Tn], F32, ph, side="right")
    rows = cx.sb("rows", [8, Tn], F32, ph, side="right")
    tabs = [cx.sb("tab", [128, Tn], F32, ph, side="right") for _ in range(2)]
    scr = cx.sb("scr", [128, Tn], F32, ph, side="right")
    ob = [cx.sb("obf", [128, Tn], BF16, ph, side="right") for _ in range(4)]
    vtb = cx.sb("vtb", [128, 512], BF16, ph, side="right")

    tabs_readers = [None]

    def l2n_rot(src_f, sc_col, cs_d, ss_d, dst_bf, t_in, t0, out_d):
        toks = []
        t_sq = []
        for c4 in range(4):
            ts = pl.emit("scalar", lambda eng, c4=c4: eng.activation(
                out=sq_s[:, :], in_=src_f[c4][:, :], func=AF.Square),
                after=t_in + ([t_sq[-1][1]] if t_sq else []))
            tm = pl.emit("tensor", lambda eng, c4=c4: eng.matmul(
                psS[0:8, 0:Tn], e8[:, 8 * c4:8 * (c4 + 1)], sq_s[:, :],
                start=(c4 == 0), stop=(c4 == 3)),
                after=[ts] + (cx.bank_wait(psS) if c4 == 0 else []))
            t_sq.append((ts, tm))
        t_nm = pl.emit("vector", lambda eng: eng.tensor_scalar_max(
            out=rows[0:8, :], in0=psS[0:8, 0:Tn], scalar1=1e-24),
            after=[t_sq[-1][1]])
        cx.bank_set(psS, t_nm)
        t_nm2 = pl.emit("scalar", lambda eng: eng.activation(
            out=rows[0:8, :], in_=rows[0:8, :], func=AF.Ln),
            after=[t_nm], strict_after=[t_nm])
        t_nm3 = pl.emit("scalar", lambda eng: eng.activation(
            out=rows[0:8, :], in_=rows[0:8, :], func=AF.Exp, scale=-0.5),
            after=[t_nm2], strict_after=[t_nm2])
        tdep = list(t_in) + ([tabs_readers[0]] if tabs_readers[0] else [])
        t_tab = [pl.emit("sync", lambda eng, d=d, i=i: eng.dma_start(
            out=tabs[i][:, :], in_=d[:, t0:t0 + Tn]), after=tdep, dma=True,
            tag="dma4") for i, d in ((0, cs_d), (1, ss_d))]
        for c4 in range(4):
            t_rb = pl.emit("tensor", lambda eng, c4=c4: eng.matmul(
                psR[:, 0:Tn], g8[0:8, 128 * c4:128 * (c4 + 1)], rows[0:8, :],
                start=True, stop=True),
                after=[t_nm3] + cx.bank_wait(psR))
            t_qn = pl.emit("vector", lambda eng, c4=c4: eng.scalar_tensor_tensor(
                out=src_f[c4][:, :], in0=src_f[c4][:, :], scalar=sc_col[:, 0:1],
                in1=psR[:, 0:Tn], op0=ALU.mult, op1=ALU.mult),
                after=[t_rb, t_qs8] + t_in)
            cx.bank_set(psR, t_qn)
            t_rt = pl.emit("tensor", lambda eng, c4=c4: eng.matmul(
                psRot[:, 0:Tn], rotm[:, :], src_f[c4][:, :],
                start=True, stop=True),
                after=[t_qn] + cx.bank_wait(psRot))
            t_m1 = pl.emit("vector", lambda eng, c4=c4: eng.tensor_mul(
                out=scr[:, :], in0=psRot[:, 0:Tn], in1=tabs[1][:, :]),
                after=[t_rt] + t_tab)
            cx.bank_set(psRot, t_m1)
            t_m2 = pl.emit("vector", lambda eng, c4=c4: eng.tensor_mul(
                out=src_f[c4][:, :], in0=src_f[c4][:, :], in1=tabs[0][:, :]),
                after=[t_m1] + t_tab, strict_after=[t_m1])
            t_m3 = pl.emit("vector", lambda eng, c4=c4: eng.tensor_add(
                out=dst_bf[c4][:, :], in0=src_f[c4][:, :], in1=scr[:, :]),
                after=[t_m2], strict_after=[t_m2])
            t_s = pl.emit("gpsimd", lambda eng, c4=c4: eng.dma_start(
                out=out_d[128 * c4:128 * (c4 + 1), t0:t0 + Tn],
                in_=dst_bf[c4][:, :]), after=[t_m3], dma=True, tag="dma4")
            toks.append(t_s)
            toks.append(t_m3)
            tabs_readers[0] = t_m2
        return toks

    prev = last(bar + [twq, t_qs8])
    stores = []
    for ti in range(QT // Tn):
        t0 = ti * Tn
        t_x = []
        for c in range(NCH):
            if ti == 0:
                t1 = pl.emit("sync", lambda eng, c=c: eng.dma_start(
                    out=xw[c][:, 1:Tn + 1], in_=xq_d[128 * c:128 * (c + 1), 0:Tn]),
                    after=prev, dma=True, tag="dma4")
                t2 = pl.emit("vector", lambda eng, c=c: eng.tensor_copy(
                    out=xw[c][:, 0:1], in_=pcol[:, c:c + 1]),
                    after=prev + tok_c)
                t_x += [t1, t2]
            else:
                t_x.append(pl.emit("sync", lambda eng, c=c, t0=t0: eng.dma_start(
                    out=xw[c][:, 0:Tn + 1],
                    in_=xq_d[128 * c:128 * (c + 1), t0 - 1:t0 + Tn]),
                    after=prev, dma=True, tag="dma4"))
        t_hs = []
        for c in range(NCH):
            off = 1 if c < 4 else 0
            t_hs.append(pl.emit("gpsimd", lambda eng, c=c, off=off: eng.tensor_copy(
                out=hs_bf[c][:, :], in_=xw[c][:, off:off + Tn]),
                after=t_x))
        def qk_write(dst):
            def w(of, o0, osz, nt, tsz, ps, pe_tok):
                return pl.emit("vector", lambda eng, of=of: eng.tensor_copy(
                    out=dst[of][:, :], in_=ps[:, 0:Tn]), after=[pe_tok])
            return w
        t_q = []
        for of in range(4):
            ps = psA[of % 2]
            def mmq(eng, of=of, ps=ps):
                return [eng.matmul(ps[:, 0:Tn],
                                   wqkv_sb[:, k * 1536 + 128 * of:k * 1536 + 128 * of + 128],
                                   hs_bf[k][:, :], start=(k == 0), stop=(k == NCH - 1))
                        for k in range(NCH)][-1]
            pe = pl.emit("tensor", mmq, after=last(t_hs + [twq]) + cx.bank_wait(ps))
            ev = pl.emit("vector", lambda eng, of=of, ps=ps: eng.tensor_copy(
                out=q_f[of][:, :], in_=ps[:, 0:Tn]), after=[pe])
            cx.bank_set(ps, ev)
            t_q.append(ev)
        t_k = []
        for of in range(4):
            ps = psA[of % 2]
            def mmk(eng, of=of, ps=ps):
                return [eng.matmul(ps[:, 0:Tn],
                                   wqkv_sb[:, k * 1536 + 512 + 128 * of:k * 1536 + 512 + 128 * of + 128],
                                   hs_bf[k][:, :], start=(k == 0), stop=(k == NCH - 1))
                        for k in range(NCH)][-1]
            pe = pl.emit("tensor", mmk, after=last(t_hs + [twq]) + cx.bank_wait(ps))
            ev = pl.emit("vector", lambda eng, of=of, ps=ps: eng.tensor_copy(
                out=k_f[of][:, :], in_=ps[:, 0:Tn]), after=[pe])
            cx.bank_set(ps, ev)
            t_k.append(ev)
        t_v = []
        for tt in range(Tn // 128):
            ps = psA[tt % 2]
            def mmv(eng, tt=tt, ps=ps):
                return [eng.matmul(ps[:, 0:512],
                                   hs_bf[k][:, 128 * tt:128 * (tt + 1)],
                                   wqkv_sb[:, k * 1536 + 1024:k * 1536 + 1536],
                                   start=(k == 0), stop=(k == NCH - 1))
                        for k in range(NCH)][-1]
            pe = pl.emit("tensor", mmv, after=last(t_hs + [twq]) + cx.bank_wait(ps))
            ev = pl.emit("vector", lambda eng, ps=ps: eng.tensor_copy(
                out=vtb[:, :], in_=ps[:, 0:512]),
                after=[pe] + (t_v[-1:] if t_v else []))
            cx.bank_set(ps, ev)
            t_sv = pl.emit("gpsimd", lambda eng, tt=tt, t0=t0: eng.dma_start(
                out=vt_out[t0 + 128 * tt:t0 + 128 * (tt + 1), :], in_=vtb[:, :]),
                after=[ev], dma=True, tag="dma4")
            cx.bank_set(ps, t_sv)
            t_v.append(t_sv)
        t_lq = l2n_rot(q_f, qs_c, csq_d, ssq_d, ob, t_q, t0, q_out)
        t_lk = l2n_rot(k_f, ks_c, csk_d, ssk_d, ob, t_k, t0, k_out)
        stores += t_v + t_lq + t_lk
        prev = last(t_v + t_lq + t_lk + t_hs)
    pl.emit("gpsimd", lambda eng: eng.engine_nop(), after=last(stores))
    ph.close()
    _stg.close()
    cx.default_stack = _prev_stack
    cx.io_override = {}
    if standalone:
        return cx.finish()
    return {"q": q_out, "k": k_out, "vt": vt_out}


def build_launch_G(cx=None, io=None, **kw):
    standalone = cx is None
    if standalone:
        cx = Ctx("G")
    _stg = ExitStack()
    _prev_stack = cx.default_stack
    cx.default_stack = _stg
    cx.io_override = io or {}
    pl = cx.pl
    Tn = 512
    NW = QT // 512

    q_d = cx.inp("q", [512, QT], BF16)
    k_d = cx.inp("k", [512, QT], BF16)
    vt_d = cx.inp("vt", [QT, 512], BF16)
    kprev_d = cx.inp("kprev", [512, 512], BF16)
    vtprev_d = cx.inp("vtprev", [512, 512], BF16)
    mfirst_d = cx.inp("mask_first", [512, 1024])
    mrest_d = cx.inp("mask_rest", [512, 1024])
    xc_d = cx.inp("xcur", [D, QT])
    wo_d = cx.inp("wo", [512, D])
    ident_d = cx.inp("ident_bf", [128, 128], BF16)

    x4_out = cx.outp("x4", [D, QT])

    ident_bf = cx.sb("id", [128, 128], BF16)
    tok_c = [pl.emit("sync", lambda eng: eng.dma_start(out=ident_bf[:], in_=ident_d), dma=True)]
    wst = ExitStack()
    wo_sb, two = load_weight_ph(cx, wo_d, 512, D, "wo", wst, wsb_stack=None)
    wst.close()

    kq = [cx.sb("kq", [128, 512 + QT], BF16) for _ in range(4)]
    vt = [cx.sb("vt", [128, 512], BF16) for _ in range(4 + QT // 128)]
    q_sb = [cx.sb("q", [128, Tn], BF16) for _ in range(4)]
    masks = [cx.sb("mk", [128, 1024], F32) for _ in range(8)]
    sm_s = [cx.sb("sms", [128, 1024], F32) for _ in range(2)]
    sm_m = [cx.sb("smm", [128, 1], F32) for _ in range(2)]
    sm_r = [cx.sb("smr", [128, 1], F32) for _ in range(2)]
    sm_p = [cx.sb("smp", [128, 1024], BF16) for _ in range(2)]
    sm_pt = [cx.sb("smpt", [128, 1024], BF16) for _ in range(2)]
    o_bf = [cx.sb("ob", [128, Tn], BF16) for _ in range(4)]
    o_f = [cx.sb("of", [128, Tn], F32) for _ in range(NCH)]
    xc_t = [cx.sb("xct", [128, Tn], F32) for _ in range(NCH)]
    x4_t = [cx.sb("x4t", [128, Tn], F32) for _ in range(NCH)]

    psA = [cx.ps("psA", [128, 512], F32) for _ in range(2)]
    psS = cx.ps("psS", [128, 1024], F32)
    psT = cx.ps("psT", [128, 1024], BF16)
    psV = psA[0]

    t_kl = [pl.emit("sync", lambda eng, c4=c4: eng.dma_start(
        out=kq[c4][:, 0:512], in_=kprev_d[128 * c4:128 * (c4 + 1), :]),
        dma=True) for c4 in range(4)]
    t_kl += [pl.emit("sync", lambda eng, c4=c4: eng.dma_start(
        out=kq[c4][:, 512:512 + QT], in_=k_d[128 * c4:128 * (c4 + 1), :]),
        dma=True) for c4 in range(4)]
    t_kl += [pl.emit("sync", lambda eng, r=r: eng.dma_start(
        out=vt[r][:, :], in_=vtprev_d[128 * r:128 * (r + 1), :]),
        dma=True) for r in range(4)]
    t_kl += [pl.emit("sync", lambda eng, r=r: eng.dma_start(
        out=vt[4 + r][:, :], in_=vt_d[128 * r:128 * (r + 1), :]),
        dma=True) for r in range(QT // 128)]
    t_kl += [pl.emit("sync", lambda eng, qs=qs: eng.dma_start(
        out=masks[qs][:, :], in_=mfirst_d[128 * qs:128 * (qs + 1), :]),
        dma=True) for qs in range(4)]
    t_kl += [pl.emit("sync", lambda eng, qs=qs: eng.dma_start(
        out=masks[4 + qs][:, :], in_=mrest_d[128 * qs:128 * (qs + 1), :]),
        dma=True) for qs in range(4)]

    unit_state = {"idx": 0, "hist": {}}

    def attn_unit(w, head, qs, dep):
        s = unit_state["idx"] % 2
        unit_state["idx"] += 1
        prev_slot = unit_state["hist"].get(s)
        hc, hb = head // 2, (head % 2) * 64
        w0 = list(dep) + (prev_slot or [])

        def mm_s(eng):
            return [eng.matmul(
                psS[:, 512 * half:512 * (half + 1)],
                q_sb[hc][hb:hb + 64, 128 * qs:128 * (qs + 1)],
                kq[hc][hb:hb + 64, w * 512 + 512 * half:w * 512 + 512 * (half + 1)],
                start=True, stop=True) for half in range(2)][-1]
        t_s = pl.emit("tensor", mm_s, after=w0 + cx.bank_wait(psS), tag="attn")
        mk = masks[qs] if w == 0 else masks[4 + qs]
        t_ms = pl.emit("vector", lambda eng, s=s: eng.tensor_add(
            out=sm_s[s][:, :], in0=psS[:, 0:1024], in1=mk[:, :]),
            after=[t_s] + (prev_slot or []), tag="attn")
        cx.bank_set(psS, t_ms)
        t_m = pl.emit("vector", lambda eng, s=s: eng.reduce_max(
            out=sm_m[s][:, 0:1], in_=sm_s[s][:, :], axis=AX.X, negate=True),
            after=[t_ms], strict_after=[t_ms], tag="attn")
        t_p = pl.emit("scalar", lambda eng, s=s: eng.activation(
            out=sm_p[s][:, :], in_=sm_s[s][:, :], func=AF.Exp, bias=sm_m[s][:, 0:1],
            accum_out=sm_r[s][:, 0:1]), after=[t_m], tag="attn")
        t_rc = pl.emit("vector", lambda eng, s=s: eng.reciprocal(
            out=sm_r[s][:, 0:1], in_=sm_r[s][:, 0:1]), after=[t_p], tag="attn")
        t_pn = pl.emit("vector", lambda eng, s=s: eng.tensor_scalar_mul(
            out=sm_p[s][:, :], in0=sm_p[s][:, :], scalar1=sm_r[s][:, 0:1]),
            after=[t_rc], strict_after=[t_rc], tag="attn")
        def mm_t(eng, s=s):
            return [eng.transpose(psT[:, 128 * kc:128 * (kc + 1)],
                                  sm_p[s][:, 128 * kc:128 * (kc + 1)], ident_bf[:, :])
                    for kc in range(8)][-1]
        t_tr = pl.emit("tensor", mm_t, after=[t_pn] + cx.bank_wait(psT), tag="attn")
        t_te = pl.emit("vector", lambda eng, s=s: eng.tensor_copy(
            out=sm_pt[s][:, :], in_=psT[:, 0:1024]), after=[t_tr], tag="attn")
        cx.bank_set(psT, t_te)
        def mm_av(eng, s=s):
            return [eng.matmul(psV[0:64, 0:128],
                               vt[w * 4 + kc][:, head * 64:head * 64 + 64],
                               sm_pt[s][:, 128 * kc:128 * (kc + 1)],
                               start=(kc == 0), stop=(kc == 7))
                    for kc in range(8)][-1]
        t_av = pl.emit("tensor", mm_av, after=[t_te] + cx.bank_wait(psV), tag="attn")
        t_ev = pl.emit("scalar", lambda eng: eng.copy(
            out=o_bf[hc][hb:hb + 64, 128 * qs:128 * (qs + 1)],
            in_=psV[0:64, 0:128]), after=[t_av], tag="attn")
        cx.bank_set(psV, t_ev)
        unit_state["hist"][s] = [t_pn, t_te, t_ev]
        return t_ev

    prev = last(t_kl + [two] + tok_c)
    stores = []
    for w in range(NW):
        t0 = w * Tn
        t_q = [pl.emit("sync", lambda eng, c4=c4, t0=t0: eng.dma_start(
            out=q_sb[c4][:, :], in_=q_d[128 * c4:128 * (c4 + 1), t0:t0 + Tn]),
            after=prev, dma=True, tag="dma4") for c4 in range(4)]
        t_xc = [pl.emit("sync", lambda eng, c=c, t0=t0: eng.dma_start(
            out=xc_t[c][:, :], in_=xc_d[128 * c:128 * (c + 1), t0:t0 + Tn]),
            after=prev, dma=True, tag="dma4") for c in range(NCH)]
        aw = last(t_q + t_kl)
        un_toks = []
        for head in range(8):
            for qs in range(4):
                t = attn_unit(w, head, qs, aw)
                un_toks.append(t)
        t_o = simple_linear(cx, wo_sb, 4, D, o_bf, Tn, o_f,
                            after=last(un_toks) + [two], psA=psA)
        t_res = [pl.emit("vector", lambda eng, c=c: eng.tensor_add(
            out=x4_t[c][:, :], in0=xc_t[c][:, :], in1=o_f[c][:, :]),
            after=t_o + t_xc) for c in range(NCH)]
        t_s = [pl.emit("gpsimd", lambda eng, c=c, t0=t0: eng.dma_start(
            out=x4_out[128 * c:128 * (c + 1), t0:t0 + Tn], in_=x4_t[c][:, :]),
            after=t_res[c:c + 1], dma=True, tag="dma4") for c in range(NCH)]
        stores += t_s
        prev = last(t_res + t_s)
    pl.emit("gpsimd", lambda eng: eng.engine_nop(), after=last(stores))
    _stg.close()
    cx.default_stack = _prev_stack
    cx.io_override = {}
    if standalone:
        return cx.finish()
    return {"x4": x4_out}


def build_launch_H(cx=None, io=None, final_z=False, **kw):
    standalone = cx is None
    if standalone:
        cx = Ctx("HZ" if final_z else "H")
    _stg = ExitStack()
    _prev_stack = cx.default_stack
    cx.default_stack = _stg
    cx.io_override = io or {}
    pl = cx.pl
    Tn = 512
    NOF = 44
    NAG = 22

    xc_d = cx.inp("xcur", [D, QT])
    pcol_d = cx.inp("pcol", [128, 8])
    w1_d = cx.inp("w1re", [D, 5632])
    b1_d = cx.inp("b1_c", [128, NOF])
    w2_d = cx.inp("w2re", [2816, D])
    b2_d = cx.inp("b2_c", [128, 8])
    lg_d = cx.inp("lg_c", [128, 8])
    lb_d = cx.inp("lb_c", [128, 8])
    onesc_d = cx.inp("ones_col", [128, 1]); onesr_d = cx.inp("ones_row", [1, 128])

    if final_z:
        x3_d = cx.inp("x3q", [D, QT])
        out_d = cx.outp("out", [D, QT])
    else:
        x5_out = cx.outp("x5", [D, QT])

    ones_col = cx.sb("onesc", [128, 1]); ones_row = cx.sb("onesr", [1, 128])
    b1c = cx.sb("b1c", [128, NOF]); b2c = cx.sb("b2c", [128, 8])
    lgc = cx.sb("lgc", [128, 8]); lbc = cx.sb("lbc", [128, 8])
    pcol = cx.sb("pcol", [128, 8], F32)
    lns = ln_scratch(cx, Tn=512)
    tok_c = [
        pl.emit("sync", lambda eng: eng.dma_start(out=ones_col[:], in_=onesc_d), dma=True),
        pl.emit("sync", lambda eng: eng.dma_start(out=ones_row[:], in_=onesr_d), dma=True),
        pl.emit("sync", lambda eng: eng.dma_start(out=b1c[:], in_=b1_d), dma=True),
        pl.emit("sync", lambda eng: eng.dma_start(out=b2c[:], in_=b2_d), dma=True),
        pl.emit("sync", lambda eng: eng.dma_start(out=lgc[:], in_=lg_d), dma=True),
        pl.emit("sync", lambda eng: eng.dma_start(out=lbc[:], in_=lb_d), dma=True),
        pl.emit("sync", lambda eng: eng.dma_start(
            out=pcol[:, :], in_=pcol_d), dma=True),
    ]
    w1bf = cx.dram("w1bf", [D, 5632], BF16)
    wst = ExitStack()
    stg = [cx.sb("w1st", [128, 5632], F32, wst, side="right") for _ in range(2)]
    stgb = [cx.sb("w1sb", [128, 5632], BF16, wst, side="right") for _ in range(2)]
    t_wc = []
    prevk = [None, None]
    for k in range(NCH):
        sg = stg[k % 2]; sb = stgb[k % 2]
        dep = [prevk[k % 2]] if prevk[k % 2] else []
        t1 = pl.emit("sync", lambda eng, k=k, sg=sg: eng.dma_start(
            out=sg[:, :], in_=w1_d[128 * k:128 * (k + 1), :]), after=dep, dma=True)
        t2 = pl.emit("gpsimd", lambda eng, sg=sg, sb=sb: eng.tensor_copy(
            out=sb[:, :], in_=sg[:, :]), after=[t1])
        t3 = pl.emit("gpsimd", lambda eng, k=k, sb=sb: eng.dma_start(
            out=w1bf[128 * k:128 * (k + 1), :], in_=sb[:, :]), after=[t2], dma=True)
        prevk[k % 2] = t3
        t_wc.append(t3)
    wst.close()
    wst2 = ExitStack()
    w2_sb, tw2 = load_weight_ph(cx, w2_d, 2816, D, "w2re", wst2, wsb_stack=None,
                                dep=last(t_wc))
    wst2.close()

    psA = [cx.ps("psA", [128, 512], F32) for _ in range(2)]
    ps_ln = [cx.ps("psStat", [64, 512], F32),
             cx.ps("psMu", [128, 512], F32), cx.ps("psRs", [128, 512], F32)]

    ph = ExitStack()
    bar = pl.barrier()
    xw = [cx.sb("xw", [128, Tn + 1], F32, ph, side="right") for _ in range(NCH)]
    hs_f = [cx.sb("hsf", [128, Tn], F32, ph, side="right") for _ in range(NCH)]
    ln_bf = [cx.sb("lnb", [128, Tn], BF16, ph, side="right") for _ in range(NCH)]
    w1c = [cx.sb("w1c", [128, NCH * 128], BF16, ph, side="right") for _ in range(2)]
    a_bf = [cx.sb("abf", [128, Tn], BF16, ph, side="right") for _ in range(NAG)]
    g_s = cx.sb("gs", [128, Tn], F32, ph, side="right")
    agg = [cx.sb("agg", [128, Tn], BF16, ph, side="right") for _ in range(NAG)]
    x5_t = [cx.sb("x5t", [128, Tn], F32, ph, side="right") for _ in range(NCH)]
    if final_z:
        x3_t = [cx.sb("x3t", [128, Tn], F32, ph, side="right") for _ in range(NCH)]
        lnp = hs_f

    prev = last(bar + [tw2] + tok_c + last(t_wc))
    stores = []
    for ti in range(QT // Tn):
        t0 = ti * Tn
        t_x = []
        for c in range(NCH):
            if ti == 0:
                t1 = pl.emit("sync", lambda eng, c=c: eng.dma_start(
                    out=xw[c][:, 1:Tn + 1], in_=xc_d[128 * c:128 * (c + 1), 0:Tn]),
                    after=prev, dma=True, tag="dma4")
                t2 = pl.emit("vector", lambda eng, c=c: eng.tensor_copy(
                    out=xw[c][:, 0:1], in_=pcol[:, c:c + 1]),
                    after=prev + tok_c)
                t_x += [t1, t2]
            else:
                t_x.append(pl.emit("sync", lambda eng, c=c, t0=t0: eng.dma_start(
                    out=xw[c][:, 0:Tn + 1],
                    in_=xc_d[128 * c:128 * (c + 1), t0 - 1:t0 + Tn]),
                    after=prev, dma=True, tag="dma4"))
        t_hs = []
        for c in range(NCH):
            off = 1 if c < 4 else 0
            t_hs.append(pl.emit("vector", lambda eng, c=c, off=off: eng.tensor_copy(
                out=hs_f[c][:, :], in_=xw[c][:, off:off + Tn]),
                after=t_x))
        t_ln = emit_ln(cx, hs_f, Tn, dst_bf=ln_bf, after=t_hs + tok_c,
                       psum2=ps_ln, ones_col=ones_col, ones_row=ones_row,
                       scratch=lns, gain_col=lgc, bias_col=lbc)
        wtoks = [None, None]
        t_ag = []
        for of in range(NOF):
            wc = w1c[of % 2]
            dep = last(t_ln) + ([wtoks[of % 2]] if wtoks[of % 2] else [])
            t_wl = [pl.emit("sync", lambda eng, k=k, of=of, wc=wc: eng.dma_start(
                out=wc[:, 128 * k:128 * (k + 1)],
                in_=w1bf[128 * k:128 * (k + 1), 128 * of:128 * (of + 1)]),
                after=dep + last(t_wc), dma=True, tag="dma4") for k in range(NCH)]
            ps = psA[of % 2]
            def mm1(eng, wc=wc, ps=ps):
                return [eng.matmul(ps[:, 0:Tn], wc[:, 128 * k:128 * (k + 1)],
                                   ln_bf[k][:, :], start=(k == 0), stop=(k == NCH - 1))
                        for k in range(NCH)][-1]
            pe = pl.emit("tensor", mm1,
                         after=last(t_wl + last(t_ln)) + cx.bank_wait(ps))
            wtoks[of % 2] = pe
            if of < NAG:
                ev = pl.emit("vector", lambda eng, of=of, ps=ps: eng.tensor_scalar_add(
                    out=a_bf[of][:, :], in0=ps[:, 0:Tn],
                    scalar1=b1c[:, of:of + 1]), after=[pe])
                cx.bank_set(ps, ev)
            else:
                og = of - NAG
                ev = pl.emit("scalar", lambda eng, of=of, ps=ps: eng.activation(
                    out=g_s[:, :], in_=ps[:, 0:Tn], func=AF.Gelu,
                    bias=b1c[:, of:of + 1]), after=[pe])
                cx.bank_set(ps, ev)
                ev = pl.emit("vector", lambda eng, og=og: eng.tensor_mul(
                    out=agg[og][:, :], in0=a_bf[og][:, :], in1=g_s[:, :]),
                    after=[ev], strict_after=[ev])
                t_ag.append(ev)
        def h2_write(of, o0, osz, nt, tsz, ps, pe_tok):
            return pl.emit("vector", lambda eng, of=of, ps=ps: eng.scalar_tensor_tensor(
                out=x5_t[of][:, :], in0=ps[:, 0:Tn], scalar=b2c[:, of:of + 1],
                in1=xw[of][:, 1:Tn + 1], op0=ALU.add, op1=ALU.add),
                after=[pe_tok] + t_x)
        t_h2 = emit_linear(cx, w2_sb, NAG, D, agg, Tn, h2_write,
                           after=t_ag + [tw2], psA=psA)
        if not final_z:
            t_s = [pl.emit("gpsimd", lambda eng, c=c, t0=t0: eng.dma_start(
                out=x5_out[128 * c:128 * (c + 1), t0:t0 + Tn], in_=x5_t[c][:, :]),
                after=t_h2, dma=True, tag="dma4") for c in range(NCH)]
            stores += t_s
            prev = last(t_h2 + t_s + t_ag)
        else:
            t_3 = [pl.emit("sync", lambda eng, c=c, t0=t0: eng.dma_start(
                out=x3_t[c][:, :], in_=x3_d[128 * c:128 * (c + 1), t0:t0 + Tn]),
                after=prev, dma=True, tag="dma4") for c in range(NCH)]
            t_ln2 = emit_ln(cx, x5_t, Tn, dst_f32=lnp, after=t_h2 + tok_c,
                            psum2=ps_ln, ones_col=ones_col, ones_row=ones_row,
                            scratch=lns)
            t_r = [pl.emit("vector", lambda eng, c=c: eng.tensor_add(
                out=xw[c][:, 0:Tn], in0=x3_t[c][:, :], in1=lnp[c][:, :]),
                after=t_ln2 + t_3) for c in range(NCH)]
            t_s = [pl.emit("gpsimd", lambda eng, c=c, t0=t0: eng.dma_start(
                out=out_d[128 * c:128 * (c + 1), t0:t0 + Tn], in_=xw[c][:, 0:Tn]),
                after=t_r[c:c + 1], dma=True, tag="dma4") for c in range(NCH)]
            stores += t_s
            prev = last(t_h2 + t_ag + t_r + t_s)
    pl.emit("gpsimd", lambda eng: eng.engine_nop(), after=last(stores))
    ph.close()
    _stg.close()
    cx.default_stack = _prev_stack
    cx.io_override = {}
    if standalone:
        return cx.finish()
    return {"out": out_d} if final_z else {"x5": x5_out}


def build_launch_HZ(cx=None, io=None, **kw):
    return build_launch_H(cx, io, final_z=True, **kw)


def build_launch_Z(cx=None, io=None, **kw):
    standalone = cx is None
    if standalone:
        cx = Ctx("Z")
    _stg = ExitStack()
    _prev_stack = cx.default_stack
    cx.default_stack = _stg
    cx.io_override = io or {}
    pl = cx.pl
    Tn = 512
    x3_d = cx.inp("x3q", [D, QT])
    cur_d = cx.inp("curq", [D, QT])
    onesc_d = cx.inp("ones_col", [128, 1]); onesr_d = cx.inp("ones_row", [1, 128])
    out_d = cx.outp("out", [D, QT])

    ones_col = cx.sb("onesc", [128, 1]); ones_row = cx.sb("onesr", [1, 128])
    lns = ln_scratch(cx, Tn=512)
    tok_c = [
        pl.emit("sync", lambda eng: eng.dma_start(out=ones_col[:], in_=onesc_d), dma=True),
        pl.emit("sync", lambda eng: eng.dma_start(out=ones_row[:], in_=onesr_d), dma=True),
    ]
    ps_ln = [cx.ps("psStat", [64, 512], F32),
             cx.ps("psMu", [128, 512], F32), cx.ps("psRs", [128, 512], F32)]
    ph = ExitStack()
    cur_t = [cx.sb("cur", [128, Tn], F32, ph, side="right") for _ in range(NCH)]
    x3_t = [cx.sb("x3", [128, Tn], F32, ph, side="right") for _ in range(NCH)]
    lnp = [cx.sb("lnp", [128, Tn], F32, ph, side="right") for _ in range(NCH)]
    o_t = [cx.sb("o", [128, Tn], F32, ph, side="right") for _ in range(NCH)]
    prev = list(tok_c)
    stores = []
    for ti in range(QT // Tn):
        t0 = ti * Tn
        t_c = [pl.emit("sync", lambda eng, c=c, t0=t0: eng.dma_start(
            out=cur_t[c][:, :], in_=cur_d[128 * c:128 * (c + 1), t0:t0 + Tn]),
            after=prev, dma=True) for c in range(NCH)]
        t_3 = [pl.emit("sync", lambda eng, c=c, t0=t0: eng.dma_start(
            out=x3_t[c][:, :], in_=x3_d[128 * c:128 * (c + 1), t0:t0 + Tn]),
            after=prev, dma=True) for c in range(NCH)]
        t_ln = emit_ln(cx, cur_t, Tn, dst_f32=lnp, after=t_c + tok_c,
                       psum2=ps_ln, ones_col=ones_col, ones_row=ones_row,
                       scratch=lns)
        t_r = [pl.emit("vector", lambda eng, c=c: eng.tensor_add(
            out=o_t[c][:, :], in0=x3_t[c][:, :], in1=lnp[c][:, :]),
            after=t_ln + t_3) for c in range(NCH)]
        t_s = [pl.emit("gpsimd", lambda eng, c=c, t0=t0: eng.dma_start(
            out=out_d[128 * c:128 * (c + 1), t0:t0 + Tn], in_=o_t[c][:, :]),
            after=t_r[c:c + 1], dma=True) for c in range(NCH)]
        stores += t_s
        prev = last(t_r + t_s)
    pl.emit("gpsimd", lambda eng: eng.engine_nop(), after=last(stores))
    ph.close()
    _stg.close()
    cx.default_stack = _prev_stack
    cx.io_override = {}
    if standalone:
        return cx.finish()
    return {"out": out_d}


RG = [[0, 1, 2, 3], [4, 5, 6, 7]]


def _emit_collective(cx, kind, op, ins, outs, after=()):
    pl = cx.pl
    return pl.emit("gpsimd", lambda eng: eng.collective_compute(
        kind, op, replica_groups=RG, ins=[a.opt() for a in ins],
        outs=[a.opt() for a in outs]), after=after, dma=True)


def emit_pcol(cx, cur_ap, psel, tag):
    pl = cx.pl
    contrib = cx.dram(f"pc_c_{tag}", [D, 1])
    agout = cx.dram(f"pc_ag_{tag}", [4 * D, 1])
    pcol_int = cx.dram(f"pc_o_{tag}", [128, 8])
    bar = pl.barrier()
    t0 = pl.emit("gpsimd", lambda eng: eng.dma_start(
        out=contrib, in_=cur_ap[:, QT - 1:QT],
        allow_slow_non_contiguous=True), after=bar, dma=True)
    t1 = _emit_collective(cx, "AllGather", ALU.bypass, [contrib], [agout],
                          after=[t0])
    with ExitStack() as hst:
        pc_sb = cx.sb("pcsb", [128, 32], F32, hst, side="right")
        ps_sb = cx.sb("pssb", [128, 4], F32, hst, side="right")
        out_sb = cx.sb("pout", [128, 8], F32, hst, side="right")
        t_s = pl.emit("sync", lambda eng: eng.dma_start(out=ps_sb[:, :], in_=psel),
                      after=bar, dma=True)
        t_l = []
        for g in range(4):
            for c in range(8):
                t_l.append(pl.emit(
                    "sync", lambda eng, g=g, c=c: eng.dma_start(
                        out=pc_sb[:, 8 * g + c:8 * g + c + 1],
                        in_=agout[1024 * g + 128 * c:1024 * g + 128 * (c + 1), 0:1]),
                    after=[t1], dma=True))
        t_a = pl.emit("vector", lambda eng: eng.tensor_scalar_mul(
            out=out_sb[:, :], in0=pc_sb[:, 0:8], scalar1=ps_sb[:, 0:1]),
            after=t_l + [t_s])
        for g in range(1, 4):
            t_a = pl.emit("vector", lambda eng, g=g: eng.scalar_tensor_tensor(
                out=out_sb[:, :], in0=pc_sb[:, 8 * g:8 * (g + 1)],
                scalar=ps_sb[:, g:g + 1], in1=out_sb[:, :],
                op0=ALU.mult, op1=ALU.add),
                after=[t_a], strict_after=[t_a])
        t_o = pl.emit("gpsimd", lambda eng: eng.dma_start(
            out=pcol_int, in_=out_sb[:, :]), after=[t_a], dma=True)
    return pcol_int, [t_o]


def emit_kvprev(cx, k_ap, vt_ap, psel, tag):
    pl = cx.pl
    ck = cx.dram(f"kv_ck_{tag}", [512, 512], BF16)
    cv = cx.dram(f"kv_cv_{tag}", [512, 512], BF16)
    agk = cx.dram(f"kv_agk_{tag}", [2048, 512], BF16)
    agv = cx.dram(f"kv_agv_{tag}", [2048, 512], BF16)
    kprev = cx.dram(f"kv_ko_{tag}", [512, 512], BF16)
    vtprev = cx.dram(f"kv_vo_{tag}", [512, 512], BF16)
    bar = pl.barrier()
    t0 = pl.emit("gpsimd", lambda eng: eng.dma_start(
        out=ck, in_=k_ap[:, QT - 512:QT]), after=bar, dma=True)
    t0b = pl.emit("gpsimd", lambda eng: eng.dma_start(
        out=cv, in_=vt_ap[QT - 512:QT, :]), after=bar, dma=True)
    t1 = _emit_collective(cx, "AllGather", ALU.bypass, [ck], [agk], after=[t0])
    t2 = _emit_collective(cx, "AllGather", ALU.bypass, [cv], [agv], after=[t0b, t1])
    toks = []
    with ExitStack() as hst:
        blk = cx.sb("kvb", [128, 512], BF16, hst, side="right")
        acc = cx.sb("kva", [128, 512], F32, hst, side="right")
        obf = cx.sb("kvo", [128, 512], BF16, hst, side="right")
        ps_sb = cx.sb("kvps", [128, 4], F32, hst, side="right")
        t_s = pl.emit("sync", lambda eng: eng.dma_start(out=ps_sb[:, :], in_=psel),
                      after=bar, dma=True)
        for src_ap, dst_ap, tdep in ((agk, kprev, t1), (agv, vtprev, t2)):
            for c4 in range(4):
                t_a = None
                for g in range(4):
                    t_ld = pl.emit("sync", lambda eng, g=g, c4=c4, src_ap=src_ap: eng.dma_start(
                        out=blk[:, :],
                        in_=src_ap[512 * g + 128 * c4:512 * g + 128 * (c4 + 1), :]),
                        after=[tdep] + ([t_a] if t_a else []), dma=True)
                    if g == 0:
                        t_a = pl.emit("vector", lambda eng: eng.tensor_scalar_mul(
                            out=acc[:, :], in0=blk[:, :], scalar1=ps_sb[:, 0:1]),
                            after=[t_ld, t_s])
                    else:
                        t_a = pl.emit("vector", lambda eng, g=g: eng.scalar_tensor_tensor(
                            out=acc[:, :], in0=blk[:, :], scalar=ps_sb[:, g:g + 1],
                            in1=acc[:, :], op0=ALU.mult, op1=ALU.add),
                            after=[t_ld], strict_after=[t_a])
                t_c = pl.emit("gpsimd", lambda eng: eng.tensor_copy(
                    out=obf[:, :], in_=acc[:, :]), after=[t_a])
                t_o = pl.emit("gpsimd", lambda eng, c4=c4, dst_ap=dst_ap: eng.dma_start(
                    out=dst_ap[128 * c4:128 * (c4 + 1), :], in_=obf[:, :]),
                    after=[t_c], dma=True)
                toks.append(t_o)
    return kprev, vtprev, toks


def emit_ygsel(cx, yg_ap, qsel, tag):
    pl = cx.pl
    ag = cx.dram(f"yg_ag_{tag}", [4 * 2048, QT], BF16)
    out = cx.dram(f"yg_sel_{tag}", [2048, QT], BF16)
    bar = pl.barrier()
    t1 = _emit_collective(cx, "AllGather", ALU.bypass, [yg_ap], [ag], after=bar)
    toks = []
    with ExitStack() as hst:
        blk = cx.sb("ygb2", [128, QT], BF16, hst, side="right")
        acc = cx.sb("yga2", [128, QT], F32, hst, side="right")
        obf = cx.sb("ygo2", [128, QT], BF16, hst, side="right")
        qs_sb = cx.sb("ygqs", [128, 4], F32, hst, side="right")
        t_s = pl.emit("sync", lambda eng: eng.dma_start(out=qs_sb[:, :], in_=qsel),
                      after=bar, dma=True)
        for r in range(16):
            i, c4 = r // 4, r % 4
            t_a = None
            for g in range(4):
                row = 2048 * i + 512 * g + 128 * c4
                t_ld = pl.emit("sync", lambda eng, row=row: eng.dma_start(
                    out=blk[:, :], in_=ag[row:row + 128, :]),
                    after=[t1] + ([t_a] if t_a else []), dma=True)
                if g == 0:
                    t_a = pl.emit("vector", lambda eng: eng.tensor_scalar_mul(
                        out=acc[:, :], in0=blk[:, :], scalar1=qs_sb[:, 0:1]),
                        after=[t_ld, t_s])
                else:
                    t_a = pl.emit("vector", lambda eng, g=g: eng.scalar_tensor_tensor(
                        out=acc[:, :], in0=blk[:, :], scalar=qs_sb[:, g:g + 1],
                        in1=acc[:, :], op0=ALU.mult, op1=ALU.add),
                        after=[t_ld], strict_after=[t_a])
            t_c = pl.emit("gpsimd", lambda eng: eng.tensor_copy(
                out=obf[:, :], in_=acc[:, :]), after=[t_a])
            t_o = pl.emit("gpsimd", lambda eng, r=r: eng.dma_start(
                out=out[128 * r:128 * (r + 1), :], in_=obf[:, :]),
                after=[t_c], dma=True)
            toks.append(t_o)
    return out, toks


def build_fused():
    cx = Ctx("FU")
    cx.fused = True
    pl = cx.pl
    psel = cx.inp("psel", [128, 4])
    qsel = cx.inp("qsel", [128, 4])

    cx.prefix = "A_"
    oA = build_launch_A(cx, io={})
    gp_ag = cx.dram("gp_ag", [512, 8])
    tA = _emit_collective(cx, "AllGather", ALU.bypass, [oA["gpart"]], [gp_ag],
                          after=pl.barrier())
    cx.prefix = "B_"
    oB = build_launch_B(cx, io={
        "x1q": oA["x1"],
        "gp_all": gp_ag.rearrange("(g p) e -> p g e", p=128)})
    ln3_ag = cx.dram("ln3_ag", [4 * D, QT], BF16)
    tB = _emit_collective(cx, "AllGather", ALU.bypass, [oB["ln3"]], [ln3_ag],
                          after=pl.barrier())
    cx.prefix = "C_"
    oC = build_launch_C(cx, io={"ln3f": ln3_ag}, ln3_blocks=True)
    xp_sum = cx.dram("xp_sum", [96, S])
    tC = _emit_collective(cx, "AllReduce", ALU.add, [oC["xp"]], [xp_sum],
                          after=pl.barrier())
    cx.prefix = "D_"
    oD = build_launch_D(cx, io={"xp0": xp_sum, "xsil": oC["xsil"],
                                "zsil": oC["zsil"]}, presummed=True)
    yg_x, _t = emit_ygsel(cx, oD["ygated"], qsel, "d")
    cx.prefix = "E_"
    oE = build_launch_E(cx, io={"ygq": yg_x, "x2q": oB["x2"]})
    cur = oE["xl"]
    x3 = oE["x3"]
    for l in range(2):
        cx.prefix = f"F{l}_"
        pcol1, _t = emit_pcol(cx, cur, psel, f"f{l}")
        oF = build_launch_F(cx, io={"xcur": cur, "pcol": pcol1})
        cx.prefix = f"G{l}_"
        kprev, vtprev, _t = emit_kvprev(cx, oF["k"], oF["vt"], psel, f"g{l}")
        oG = build_launch_G(cx, io={"q": oF["q"], "k": oF["k"], "vt": oF["vt"],
                                    "kprev": kprev, "vtprev": vtprev,
                                    "xcur": cur})
        cur = oG["x4"]
        cx.prefix = f"H{l}_"
        pcol2, _t = emit_pcol(cx, cur, psel, f"h{l}")
        oH = build_launch_H(cx, io={"xcur": cur, "pcol": pcol2})
        cur = oH["x5"]
    cx.prefix = "Z_"
    build_launch_Z(cx, io={"x3q": x3, "curq": cur})
    return cx.finish()


import ml_dtypes

_nc_cache = {}


SER_LAUNCHES = set()
DESER_TAGS = {"attn", "ln", "lin", "mr", "dma4", "wload"}


def _get_nc(name):
    if name not in _nc_cache:
        old = SERIALIZE[0]
        try:
            SERIALIZE[0] = name in SER_LAUNCHES
            _nc_cache[name] = globals()[f"build_launch_{name}"]()
        finally:
            SERIALIZE[0] = old
    return _nc_cache[name]


def _run(name, in_maps):
    nc = _get_nc(name)
    return run_bass_kernel_spmd(nc, in_maps, core_ids=list(range(8))).results


def _mask_causal():
    i = np.arange(1024)[:, None]
    j = np.arange(1024)[None, :]
    return np.where(j <= i, 0.0, NEG).astype(np.float32)


def _col(v, n128):
    v = np.asarray(v, np.float32)
    return np.ascontiguousarray(v.reshape(n128, 128).T)


def _ones_consts():
    return {"ones_col": np.ones((128, 1), np.float32),
            "ones_row": np.ones((1, 128), np.float32)}


def run_A(inputs, x):
    mask = _mask_causal()
    ident_bf = np.eye(128, dtype=np.float32).astype(ml_dtypes.bfloat16)
    in_maps = []
    for c in range(8):
        b, j = c // 4, c % 4
        flag = np.zeros((128, 2), np.float32)
        flag[:, 0] = 1.0 if j == 0 else 0.0
        flag[:, 1] = 1.0 - flag[:, 0]
        in_maps.append({
            "xq": np.ascontiguousarray(x[b, 4096 * j:4096 * (j + 1), :].T),
            "xinit": np.ascontiguousarray(x[b, :1024, :].T),
            "Wq": inputs["L0_attn_Wq"], "Wk": inputs["L0_attn_Wk"],
            "Wv": inputs["L0_attn_Wv"], "Wo": inputs["L0_attn_Wo"],
            "W1": inputs["L0_attn_W1"], "W2": inputs["L0_attn_W2"],
            "b1c": _col(inputs["L0_attn_b1"], 4),
            "b2c": _col(inputs["L0_attn_b2"], 8),
            "causal_mask": mask, "flag": flag, "ident_bf": ident_bf,
            **_ones_consts(),
        })
    res = _run("A", in_maps)
    return ([res[c]["x1"] for c in range(8)],
            [res[c]["gpart"] for c in range(8)])


def run_B(inputs, x1_fm, gparts):
    in_maps = []
    for c in range(8):
        b = c // 4
        gp_all = np.concatenate([gparts[b * 4 + i] for i in range(4)], axis=1)
        in_maps.append({
            "x1q": x1_fm[c], "gp_all": np.ascontiguousarray(gp_all),
            "cn_w1": inputs["L0_ffn_cn_w1"], "cn_w2": inputs["L0_ffn_cn_w2"],
            "cn_w3": inputs["L0_ffn_cn_w3"], "cn_w4": inputs["L0_ffn_cn_w4"],
            "cn_b1c": _col(inputs["L0_ffn_cn_b1"], 8),
            "cn_b2c": _col(inputs["L0_ffn_cn_b2"], 4),
            "cn_b3c": _col(inputs["L0_ffn_cn_b3"], 2),
            "cn_b4c": np.asarray(inputs["L0_ffn_cn_b4"], np.float32).reshape(1, 1),
            "gc_w1": inputs["L0_ffn_gc_w1"], "gc_w2": inputs["L0_ffn_gc_w2"],
            "gc_w3": inputs["L0_ffn_gc_w3"],
            "gc_b1c": _col(inputs["L0_ffn_gc_b1"], 4),
            "gc_b2c": _col(inputs["L0_ffn_gc_b2"], 2),
            "gc_b3c": np.asarray(inputs["L0_ffn_gc_b3"], np.float32).reshape(1, 1),
            "ar": np.asarray(inputs["L0_ffn_adaptive_range"], np.float32).reshape(1, 1),
            "neg_row": -np.ones((1, 128), np.float32),
            **_ones_consts(),
        })
    res = _run("B", in_maps)
    return ([res[c]["x2"] for c in range(8)], [res[c]["ln3"] for c in range(8)])


def run_C(inputs, ln3_fm):
    W_in = np.asarray(inputs["L1_attn_W_in"], np.float32)
    W_x = np.asarray(inputs["L1_attn_W_x"], np.float32)
    conv_w = np.asarray(inputs["L1_attn_conv_w"], np.float32)
    conv_b = np.asarray(inputs["L1_attn_conv_b"], np.float32)
    in_maps = []
    for c in range(8):
        b, j = c // 4, c % 4
        ln3f = np.concatenate([ln3_fm[b * 4 + i] for i in range(4)], axis=1)
        cw = conv_w[512 * j:512 * (j + 1)]
        in_maps.append({
            "ln3f": np.ascontiguousarray(ln3f),
            "winxT": np.ascontiguousarray(W_in[512 * j:512 * (j + 1), :].T),
            "winzT": np.ascontiguousarray(W_in[2048 + 512 * j:2048 + 512 * (j + 1), :].T),
            "convw_c": np.ascontiguousarray(
                cw.reshape(4, 128, 4).transpose(1, 0, 2).reshape(128, 16)),
            "convb_c": _col(conv_b[512 * j:512 * (j + 1)], 4),
            "wxT": np.ascontiguousarray(W_x[:, 512 * j:512 * (j + 1)].T),
        })
    res = _run("C", in_maps)
    return ([res[c]["xsil"] for c in range(8)],
            [res[c]["zsil"] for c in range(8)],
            [res[c]["xp"] for c in range(8)])


def run_D(inputs, xsil, zsil, xp):
    W_dt = np.asarray(inputs["L1_attn_W_dt"], np.float32)
    b_dt = np.asarray(inputs["L1_attn_b_dt"], np.float32)
    A_log = np.asarray(inputs["L1_attn_A_log"], np.float32)
    Dp = np.asarray(inputs["L1_attn_Dp"], np.float32)
    sel16 = np.zeros((16, 2048), np.float32)
    for n in range(16):
        sel16[n, 128 * n:128 * (n + 1)] = 1.0
    in_maps = []
    for c in range(8):
        b, j = c // 4, c % 4
        sl = slice(512 * j, 512 * (j + 1))
        im = {
            "xsil": xsil[c], "zsil": zsil[c],
            "wdtT": np.ascontiguousarray(W_dt[sl, :].T),
            "bdt_c": _col(b_dt[sl], 4),
            "alog_c": np.ascontiguousarray(
                A_log[sl].reshape(4, 128, 16).transpose(1, 0, 2).reshape(128, 64)),
            "dp_c": _col(Dp[sl], 4),
            "sel16": sel16,
            **_ones_consts(),
        }
        for i in range(4):
            im[f"xp{i}"] = xp[b * 4 + i]
        in_maps.append(im)
    res = _run("D", in_maps)
    return [res[c]["ygated"] for c in range(8)]


def run_E(inputs, yg, x2_fm):
    W_out = np.asarray(inputs["L1_attn_W_out"], np.float32)
    woutT = np.ascontiguousarray(W_out.T)
    in_maps = []
    for c in range(8):
        b, j = c // 4, c % 4
        ygq = np.concatenate(
            [yg[b * 4 + i][512 * j:512 * (j + 1), :] for i in range(4)], axis=0)
        in_maps.append({
            "ygq": np.ascontiguousarray(ygq),
            "x2q": x2_fm[c], "woutT": woutT,
            **_ones_consts(),
        })
    res = _run("E", in_maps)
    return ([res[c]["x3"] for c in range(8)], [res[c]["xl"] for c in range(8)])


def _rot_tables(j):
    DIM_HEAD, WINDOW, n = 64, 512, S
    half_idx = np.arange(0, DIM_HEAD, 2, dtype=np.float64)
    inv_freq = 1.0 / 10000.0 ** (half_idx / DIM_HEAD)
    t = np.arange(4096 * j, 4096 * (j + 1), dtype=np.float64)
    freqs = t[:, None] * inv_freq[None, :]
    freqs = np.concatenate([freqs, freqs], axis=-1)
    base = (half_idx + 0.4 * DIM_HEAD) / (1.4 * DIM_HEAD)
    power = (t - n // 2) / float(WINDOW // 2)
    scale = base[None, :] ** power[:, None]
    scale = np.concatenate([scale, scale], axis=-1)
    cos, sin = np.cos(freqs), np.sin(freqs)
    csq = (cos * scale).T; ssq = (sin * scale).T
    csk = (cos / scale).T; ssk = (sin / scale).T
    out = []
    for tab in (csq, ssq, csk, ssk):
        out.append(np.ascontiguousarray(
            np.concatenate([tab, tab], axis=0).astype(np.float32)))
    return out


_e8 = None
_g8 = None
_rotm = None


def _f_consts():
    global _e8, _g8, _rotm
    if _e8 is None:
        e8 = np.zeros((128, 32), np.float32)
        g8 = np.zeros((8, 512), np.float32)
        for c4 in range(4):
            for p in range(128):
                h = 2 * c4 + (p // 64)
                e8[p, 8 * c4 + h] = 1.0
                g8[h, 128 * c4 + p] = 1.0
        rotm = np.zeros((128, 128), np.float32)
        for beta in (0, 64):
            for i in range(32):
                rotm[beta + 32 + i, beta + i] = -1.0
                rotm[beta + i, beta + 32 + i] = 1.0
        _e8, _g8, _rotm = e8, g8, rotm
    return _e8, _g8, _rotm


def run_F(inputs, l, cur_fm):
    e8, g8, rotm = _f_consts()
    qs = np.tile(np.asarray(inputs["L1_ffn_q_scale"], np.float32)[l], 2).reshape(128, 1)
    ks = np.tile(np.asarray(inputs["L1_ffn_k_scale"], np.float32)[l], 2).reshape(128, 1)
    wqkv = np.ascontiguousarray(np.asarray(inputs["L1_ffn_wqkv"], np.float32)[l])
    in_maps = []
    for c in range(8):
        b, j = c // 4, c % 4
        if j == 0:
            pcol = np.zeros((128, 8), np.float32)
        else:
            pcol = np.ascontiguousarray(cur_fm[c - 1][:, -1].reshape(8, 128).T)
        csq, ssq, csk, ssk = _rot_tables(j)
        in_maps.append({
            "xcur": cur_fm[c], "pcol": pcol, "wqkv": wqkv,
            "qs_c": qs, "ks_c": ks, "e8": e8, "g8": g8, "rotm": rotm,
            "csq": csq, "ssq": ssq, "csk": csk, "ssk": ssk,
        })
    res = _run("F", in_maps)
    return ([res[c]["q"] for c in range(8)], [res[c]["k"] for c in range(8)],
            [res[c]["vt"] for c in range(8)])


_g_masks = None


def _g_mask_consts():
    global _g_masks
    if _g_masks is None:
        qi = np.arange(512)[:, None]
        kj = np.arange(512)[None, :]
        left_rest = np.where(qi > kj, NEG, 0.0).astype(np.float32)
        right = np.where(qi < kj, NEG, 0.0).astype(np.float32)
        mrest = np.concatenate([left_rest, right], axis=1)
        mfirst = np.concatenate([np.full((512, 512), NEG, np.float32), right], axis=1)
        _g_masks = (mfirst, mrest)
    return _g_masks


def run_G(inputs, l, cur_fm, q, k, vt):
    mfirst, mrest = _g_mask_consts()
    ident_bf = np.eye(128, dtype=np.float32).astype(ml_dtypes.bfloat16)
    wo = np.ascontiguousarray(np.asarray(inputs["L1_ffn_wo"], np.float32)[l])
    zk = np.zeros((512, 512), ml_dtypes.bfloat16)
    zv = np.zeros((512, 512), ml_dtypes.bfloat16)
    in_maps = []
    for c in range(8):
        b, j = c // 4, c % 4
        kprev = zk if j == 0 else np.ascontiguousarray(k[c - 1][:, 3584:4096])
        vtprev = zv if j == 0 else np.ascontiguousarray(vt[c - 1][3584:4096, :])
        in_maps.append({
            "q": q[c], "k": k[c], "vt": vt[c],
            "kprev": kprev, "vtprev": vtprev,
            "mask_first": mfirst if j == 0 else mrest, "mask_rest": mrest,
            "xcur": cur_fm[c], "wo": wo, "ident_bf": ident_bf,
        })
    res = _run("G", in_maps)
    return [res[c]["x4"] for c in range(8)]


def run_H(inputs, l, cur_fm, x3_fm=None):
    w1 = np.asarray(inputs["L1_ffn_w1"], np.float32)[l]
    b1 = np.asarray(inputs["L1_ffn_b1"], np.float32)[l]
    w2 = np.asarray(inputs["L1_ffn_w2"], np.float32)[l]
    b2 = np.asarray(inputs["L1_ffn_b2"], np.float32)[l]
    lg = np.asarray(inputs["L1_ffn_ln_g"], np.float32)[l]
    lb = np.asarray(inputs["L1_ffn_ln_b"], np.float32)[l]
    w1re = np.zeros((1024, 5632), np.float32)
    w1re[:, 0:2730] = w1[:, 0:2730]
    w1re[:, 2816:2816 + 2730] = w1[:, 2730:5460]
    b1re = np.zeros(5632, np.float32)
    b1re[0:2730] = b1[0:2730]
    b1re[2816:2816 + 2730] = b1[2730:5460]
    w2re = np.zeros((2816, 1024), np.float32)
    w2re[0:2730] = w2
    in_maps = []
    for c in range(8):
        b, j = c // 4, c % 4
        if j == 0:
            pcol = np.zeros((128, 8), np.float32)
        else:
            pcol = np.ascontiguousarray(cur_fm[c - 1][:, -1].reshape(8, 128).T)
        in_maps.append({
            "xcur": cur_fm[c], "pcol": pcol,
            "w1re": w1re, "b1_c": _col(b1re, 44),
            "w2re": w2re, "b2_c": _col(b2, 8),
            "lg_c": _col(lg, 8), "lb_c": _col(lb, 8),
            **({"x3q": x3_fm[c]} if x3_fm is not None else {}),
            **_ones_consts(),
        })
    if x3_fm is not None:
        res = _run("HZ", in_maps)
        return [res[c]["out"] for c in range(8)]
    res = _run("H", in_maps)
    return [res[c]["x5"] for c in range(8)]


def run_Z(x3_fm, cur_fm):
    in_maps = [{"x3q": x3_fm[c], "curq": cur_fm[c], **_ones_consts()}
               for c in range(8)]
    res = _run("Z", in_maps)
    return [res[c]["out"] for c in range(8)]


_fused_cache = {}


def kernel_fused(inputs):
    x = np.asarray(inputs["L0_attn_hidden_states"], np.float32)
    if "nc" not in _fused_cache:
        SERIALIZE[0] = True
        _fused_cache["nc"] = build_fused()
    nc = _fused_cache["nc"]
    mask = _mask_causal()
    ident_bf = np.eye(128, dtype=np.float32).astype(ml_dtypes.bfloat16)
    e8, g8, rotm = _f_consts()
    mfirst, mrest = _g_mask_consts()
    oc = _ones_consts()

    W_in = np.asarray(inputs["L1_attn_W_in"], np.float32)
    W_x = np.asarray(inputs["L1_attn_W_x"], np.float32)
    conv_w = np.asarray(inputs["L1_attn_conv_w"], np.float32)
    conv_b = np.asarray(inputs["L1_attn_conv_b"], np.float32)
    W_dt = np.asarray(inputs["L1_attn_W_dt"], np.float32)
    b_dt = np.asarray(inputs["L1_attn_b_dt"], np.float32)
    A_log = np.asarray(inputs["L1_attn_A_log"], np.float32)
    Dp = np.asarray(inputs["L1_attn_Dp"], np.float32)
    sel16 = np.zeros((16, 2048), np.float32)
    for n in range(16):
        sel16[n, 128 * n:128 * (n + 1)] = 1.0
    woutT = np.ascontiguousarray(np.asarray(inputs["L1_attn_W_out"], np.float32).T)

    fgh = {}
    for l in range(2):
        w1 = np.asarray(inputs["L1_ffn_w1"], np.float32)[l]
        b1 = np.asarray(inputs["L1_ffn_b1"], np.float32)[l]
        w2 = np.asarray(inputs["L1_ffn_w2"], np.float32)[l]
        w1re = np.zeros((1024, 5632), np.float32)
        w1re[:, 0:2730] = w1[:, 0:2730]
        w1re[:, 2816:2816 + 2730] = w1[:, 2730:5460]
        b1re = np.zeros(5632, np.float32)
        b1re[0:2730] = b1[0:2730]
        b1re[2816:2816 + 2730] = b1[2730:5460]
        w2re = np.zeros((2816, 1024), np.float32)
        w2re[0:2730] = w2
        fgh[l] = dict(
            wqkv=np.ascontiguousarray(np.asarray(inputs["L1_ffn_wqkv"], np.float32)[l]),
            qs=np.tile(np.asarray(inputs["L1_ffn_q_scale"], np.float32)[l], 2).reshape(128, 1),
            ks=np.tile(np.asarray(inputs["L1_ffn_k_scale"], np.float32)[l], 2).reshape(128, 1),
            wo=np.ascontiguousarray(np.asarray(inputs["L1_ffn_wo"], np.float32)[l]),
            w1re=w1re, b1re=b1re, w2re=w2re,
            b2=np.asarray(inputs["L1_ffn_b2"], np.float32)[l],
            lg=np.asarray(inputs["L1_ffn_ln_g"], np.float32)[l],
            lb=np.asarray(inputs["L1_ffn_ln_b"], np.float32)[l],
        )

    in_maps = []
    for c in range(8):
        b, j = c // 4, c % 4
        flag = np.zeros((128, 2), np.float32)
        flag[:, 0] = 1.0 if j == 0 else 0.0
        flag[:, 1] = 1.0 - flag[:, 0]
        psel = np.zeros((128, 4), np.float32)
        if j > 0:
            psel[:, j - 1] = 1.0
        qsel = np.zeros((128, 4), np.float32)
        qsel[:, j] = 1.0
        sl = slice(512 * j, 512 * (j + 1))
        cw = conv_w[sl]
        im = {"psel": psel, "qsel": qsel}
        im.update({"A_" + k: v for k, v in {
            "xq": np.ascontiguousarray(x[b, 4096 * j:4096 * (j + 1), :].T),
            "xinit": np.ascontiguousarray(x[b, :1024, :].T),
            "Wq": inputs["L0_attn_Wq"], "Wk": inputs["L0_attn_Wk"],
            "Wv": inputs["L0_attn_Wv"], "Wo": inputs["L0_attn_Wo"],
            "W1": inputs["L0_attn_W1"], "W2": inputs["L0_attn_W2"],
            "b1c": _col(inputs["L0_attn_b1"], 4),
            "b2c": _col(inputs["L0_attn_b2"], 8),
            "causal_mask": mask, "flag": flag, "ident_bf": ident_bf,
            **oc}.items()})
        im.update({"B_" + k: v for k, v in {
            "cn_w1": inputs["L0_ffn_cn_w1"], "cn_w2": inputs["L0_ffn_cn_w2"],
            "cn_w3": inputs["L0_ffn_cn_w3"], "cn_w4": inputs["L0_ffn_cn_w4"],
            "cn_b1c": _col(inputs["L0_ffn_cn_b1"], 8),
            "cn_b2c": _col(inputs["L0_ffn_cn_b2"], 4),
            "cn_b3c": _col(inputs["L0_ffn_cn_b3"], 2),
            "cn_b4c": np.asarray(inputs["L0_ffn_cn_b4"], np.float32).reshape(1, 1),
            "gc_w1": inputs["L0_ffn_gc_w1"], "gc_w2": inputs["L0_ffn_gc_w2"],
            "gc_w3": inputs["L0_ffn_gc_w3"],
            "gc_b1c": _col(inputs["L0_ffn_gc_b1"], 4),
            "gc_b2c": _col(inputs["L0_ffn_gc_b2"], 2),
            "gc_b3c": np.asarray(inputs["L0_ffn_gc_b3"], np.float32).reshape(1, 1),
            "ar": np.asarray(inputs["L0_ffn_adaptive_range"], np.float32).reshape(1, 1),
            "neg_row": -np.ones((1, 128), np.float32),
            **oc}.items()})
        im.update({"C_" + k: v for k, v in {
            "winxT": np.ascontiguousarray(W_in[sl, :].T),
            "winzT": np.ascontiguousarray(W_in[2048 + 512 * j:2048 + 512 * (j + 1), :].T),
            "convw_c": np.ascontiguousarray(
                cw.reshape(4, 128, 4).transpose(1, 0, 2).reshape(128, 16)),
            "convb_c": _col(conv_b[sl], 4),
            "wxT": np.ascontiguousarray(W_x[:, sl].T)}.items()})
        im.update({"D_" + k: v for k, v in {
            "wdtT": np.ascontiguousarray(W_dt[sl, :].T),
            "bdt_c": _col(b_dt[sl], 4),
            "alog_c": np.ascontiguousarray(
                A_log[sl].reshape(4, 128, 16).transpose(1, 0, 2).reshape(128, 64)),
            "dp_c": _col(Dp[sl], 4),
            "sel16": sel16, **oc}.items()})
        im.update({"E_" + k: v for k, v in {"woutT": woutT, **oc}.items()})
        for l in range(2):
            csq, ssq, csk, ssk = _rot_tables(j)
            im.update({f"F{l}_" + k: v for k, v in {
                "wqkv": fgh[l]["wqkv"], "qs_c": fgh[l]["qs"], "ks_c": fgh[l]["ks"],
                "e8": e8, "g8": g8, "rotm": rotm,
                "csq": csq, "ssq": ssq, "csk": csk, "ssk": ssk}.items()})
            im.update({f"G{l}_" + k: v for k, v in {
                "mask_first": mfirst if j == 0 else mrest, "mask_rest": mrest,
                "wo": fgh[l]["wo"], "ident_bf": ident_bf}.items()})
            im.update({f"H{l}_" + k: v for k, v in {
                "w1re": fgh[l]["w1re"], "b1_c": _col(fgh[l]["b1re"], 44),
                "w2re": fgh[l]["w2re"], "b2_c": _col(fgh[l]["b2"], 8),
                "lg_c": _col(fgh[l]["lg"], 8), "lb_c": _col(fgh[l]["lb"], 8),
                **oc}.items()})
        im.update({"Z_" + k: v for k, v in oc.items()})
        in_maps.append(im)

    res = run_bass_kernel_spmd(nc, in_maps, core_ids=list(range(8)))
    out = np.empty((2, S, D), np.float32)
    for c in range(8):
        b, j = c // 4, c % 4
        out[b, 4096 * j:4096 * (j + 1), :] = res.results[c]["Z_out"].T
    return out


def kernel(**inputs):
    inputs = {k: np.asarray(v) for k, v in inputs.items()}
    x = np.asarray(inputs["L0_attn_hidden_states"], np.float32)
    x1_fm, gparts = run_A(inputs, x)
    x2_fm, ln3_fm = run_B(inputs, x1_fm, gparts)
    xsil, zsil, xp = run_C(inputs, ln3_fm)
    yg = run_D(inputs, xsil, zsil, xp)
    x3_fm, xl_fm = run_E(inputs, yg, x2_fm)
    cur = xl_fm
    for l in range(2):
        q, k, vt = run_F(inputs, l, cur)
        cur = run_G(inputs, l, cur, q, k, vt)
        cur = run_H(inputs, l, cur, x3_fm=x3_fm if l == 1 else None)
    out_fm = cur
    out = np.empty((2, S, D), np.float32)
    for c in range(8):
        b, j = c // 4, c % 4
        out[b, 4096 * j:4096 * (j + 1), :] = out_fm[c].T
    return out
